# version 05
# baseline (speedup 1.0000x reference)
import numpy as np
from contextlib import ExitStack

import concourse.bass as bass
import concourse.mybir as mybir
from concourse.bass_utils import run_bass_kernel_spmd

F32 = mybir.dt.float32
BF16 = mybir.dt.bfloat16
AF = mybir.ActivationFunctionType
ALU = mybir.AluOpType
AX = mybir.AxisListType

SEQ = 4096
TOK = 1024
C = 4096
D = 1024
HID = 10928
H8 = 1366
FC = C // 128
DC = D // 128
KCH = SEQ // 128
NQT = TOK // 128
NG = SEQ // 512
OG = TOK // 512
EPS = 1e-5
QSCALE = 1.0 / 32.0
LFFN = 8 * H8 * 512
THIRDS = [(0, 512), (512, 1024), (1024, 1366)]
WD_W = [128] * 10 + [86]
ENGS = ("sync", "gpsimd", "tensor", "vector", "scalar")


def _oct_maps():
    TRIPLES = [(1, 2, 3), (1, 4, 5), (1, 7, 6), (2, 4, 6), (2, 5, 7), (3, 4, 7), (3, 6, 5)]
    S = np.zeros((8, 8, 8), np.float32)
    S[0, 0, 0] = 1.0
    for i in range(1, 8):
        S[0, i, i] = 1.0
        S[i, 0, i] = 1.0
        S[i, i, 0] = -1.0
    for a, b, c in TRIPLES:
        for x, y, z in ((a, b, c), (b, c, a), (c, a, b)):
            S[x, y, z] = 1.0
            S[y, x, z] = -1.0
    jmap = np.zeros((8, 8), np.int64)
    smap = np.zeros((8, 8), np.float64)
    for i in range(8):
        for k in range(8):
            js = np.nonzero(S[:, i, k])[0]
            assert len(js) == 1
            jmap[i, k] = js[0]
            smap[i, k] = S[js[0], i, k]
    return jmap, smap


JMAP, SMAP = _oct_maps()
DEBUG_SCR = False


class Ph:
    def __init__(self, S):
        self.S = S
        self.ops = {e: [] for e in ENGS}

    def add(self, eng, fn, waits=(), inc=None, n=None):
        if n is None:
            n = 16 if inc in ("sLd", "sSt") else 1
        ws = [(s, v) for (s, v) in waits if v > 0]
        self.ops[eng].append((fn, ws, inc, n))
        if inc is None:
            return None
        self.S.cnt[inc] += n
        return self.S.cnt[inc]


class Sched:
    def __init__(self, nc, stack):
        self.nc = nc
        self.cnt = {}
        self.sems = {}
        self.stack = stack
        for name in ("sLd", "sSt", "sT", "sV", "sS"):
            h = stack.enter_context(nc.semaphore(name))
            self.sems[name] = h
            self.cnt[name] = 0

    def emit(self, ph):
        sems = self.sems
        with self.nc.Block() as blk:
            for e in ENGS:
                ops = ph.ops[e]
                if not ops:
                    continue

                def runner(eng, ops=ops):
                    seen = {}
                    for fn, ws, inc, n in ops:
                        for s, v in ws:
                            if seen.get(s, 0) < v:
                                eng.wait_ge(sems[s], v)
                                seen[s] = v
                        ins = fn(eng)
                        if inc is not None:
                            ins.then_inc(sems[inc], n)
                getattr(blk, e)(runner)


def emit_ln(ph, x_ap, out_ap, st, trash_ap, wait_x, n_feat=4096, eps_ap=None):
    inv = 1.0 / n_feat
    v1 = ph.add("vector", lambda e, x=x_ap, st=st: e.reduce_sum(st[:, 0:1], x, AX.X),
                waits=wait_x, inc="sV")
    s1 = ph.add("scalar", lambda e, x=x_ap, st=st, tr=trash_ap:
                e.activation(tr, x, AF.Square, accum_out=st[:, 1:2]),
                waits=wait_x, inc="sS")
    v2 = ph.add("vector", lambda e, st=st, inv=inv:
                e.tensor_scalar_mul(st[:, 2:3], st[:, 0:1], inv),
                waits=[("sV", v1)], inc="sV")
    s2 = ph.add("scalar", lambda e, st=st: e.activation(st[:, 3:4], st[:, 2:3], AF.Square),
                waits=[("sV", v2)], inc="sS")
    v3 = ph.add("vector", lambda e, st=st, inv=inv:
                e.tensor_scalar(st[:, 4:5], st[:, 1:2], inv, st[:, 3:4], ALU.mult, ALU.subtract),
                waits=[("sS", s2)], inc="sV")
    s3 = ph.add("scalar", lambda e, st=st, eps_ap=eps_ap:
                e.activation(st[:, 5:6], st[:, 4:5], AF.Sqrt, bias=eps_ap, scale=1.0),
                waits=[("sV", v3)], inc="sS")
    v3r = ph.add("vector", lambda e, st=st: e.reciprocal(st[:, 6:7], st[:, 5:6]),
                 waits=[("sS", s3)], inc="sV")
    v4 = ph.add("vector", lambda e, x=x_ap, o=out_ap, st=st:
                e.tensor_scalar(o, x, st[:, 2:3], st[:, 6:7], ALU.subtract, ALU.mult),
                waits=[("sV", v3r)], inc="sV")
    return v4, s1


def build_nc():
    nc = bass.Bass("TRN2", target_bir_lowering=False)

    ext = {}
    ext["xseq"] = nc.dram_tensor("xseq", [SEQ, C], F32, kind="ExternalInput").ap()
    ext["xown"] = nc.dram_tensor("xown", [TOK, C], F32, kind="ExternalInput").ap()
    ext["wqkv"] = nc.dram_tensor("wqkv", [C, 3 * D], F32, kind="ExternalInput").ap()
    ext["bqkv"] = nc.dram_tensor("bqkv", [3 * D], F32, kind="ExternalInput").ap()
    ext["wo"] = nc.dram_tensor("wo", [D, C], F32, kind="ExternalInput").ap()
    ext["bo"] = nc.dram_tensor("bo", [C], F32, kind="ExternalInput").ap()
    ext["wg"] = nc.dram_tensor("wg", [8, H8, 512], F32, kind="ExternalInput").ap()
    ext["wu"] = nc.dram_tensor("wu", [8, H8, 512], F32, kind="ExternalInput").ap()
    ext["wd"] = nc.dram_tensor("wd", [8, 512, H8], F32, kind="ExternalInput").ap()
    ext["eye"] = nc.dram_tensor("eye", [128, 128], F32, kind="ExternalInput").ap()
    ext["out"] = nc.dram_tensor("out", [TOK, C], F32, kind="ExternalOutput").ap()

    kind = "ExternalOutput" if DEBUG_SCR else "Internal"
    scr = {}
    scr["ktS"] = nc.dram_tensor("ktS", [D, SEQ], BF16, kind=kind).ap()
    scr["vS"] = nc.dram_tensor("vS", [SEQ, D], BF16, kind=kind).ap()
    scr["lnxtS"] = nc.dram_tensor("lnxtS", [NG, 128, FC * 512], BF16, kind="Internal").ap()
    scr["x2S"] = nc.dram_tensor("x2S", [TOK, C], F32, kind=kind).ap()
    scr["lnx2tS"] = nc.dram_tensor("lnx2tS", [NQT, 128, FC * 128], BF16, kind=kind).ap()
    scr["hS"] = nc.dram_tensor("hS", [TOK, HID], BF16, kind=kind).ap()
    scr["wgtS"] = nc.dram_tensor("wgtS", [FC * 128, H8], BF16, kind=kind).ap()
    scr["wutS"] = nc.dram_tensor("wutS", [FC * 128, H8], BF16, kind=kind).ap()
    scr["wdtS"] = nc.dram_tensor("wdtS", [8 * 11 * 128, 512], BF16, kind=kind).ap()

    with ExitStack() as stack:
        S = Sched(nc, stack)
        eye_bf = stack.enter_context(nc.sbuf_tensor("eye_bf", [128, 128], BF16))
        eye_f32 = stack.enter_context(nc.sbuf_tensor("eye_f32", [128, 128], F32))
        ones_col = stack.enter_context(nc.sbuf_tensor("ones_col", [128, 1], F32))
        ones_row = stack.enter_context(nc.sbuf_tensor("ones_row", [1, 128], F32))
        bqkv_sb = stack.enter_context(nc.sbuf_tensor("bqkv_sb", [128, 24], F32))
        bq32 = stack.enter_context(nc.sbuf_tensor("bq32", [128, 8], F32))
        bv_bc = stack.enter_context(nc.sbuf_tensor("bv_bc", [128, 1024], F32))
        scal = stack.enter_context(nc.sbuf_tensor("scal", [128, 8], F32))
        trash = stack.enter_context(nc.sbuf_tensor("trash", [128, 4096], BF16))
        eps_col = stack.enter_context(nc.sbuf_tensor("eps_col", [128, 1], F32))

        phase_p0(nc, S, ext, eye_bf, eye_f32, ones_col, ones_row, bqkv_sb, bq32, bv_bc, eps_col)
        phase_pscale(nc, S, ext, scal, eye_f32, ones_row)
        phase_wprep_gu(nc, S, ext, scr, eye_bf, scal)
        phase_wprep_d(nc, S, ext, scr, eye_bf, scal)

        with (
            nc.sbuf_tensor("QT", [128, DC * 1024], BF16) as QT,
            nc.sbuf_tensor("oaT", [128, DC * 1024], BF16) as oaT,
        ):
            phase_a1k(nc, S, ext, scr, eye_bf, bqkv_sb, trash, eps_col)
            phase_a1v(nc, S, ext, scr, bv_bc)
            phase_a1q(nc, S, ext, scr, eye_bf, bqkv_sb, trash, QT, eps_col)
            phase_a2(nc, S, scr, eye_bf, trash, QT, oaT)
            phase_a3(nc, S, ext, scr, ones_row, trash, oaT, eps_col)

        for third in range(3):
            phase_f1(nc, S, ext, scr, eye_bf, scal, trash, third, eps_col)
        phase_f2(nc, S, ext, scr, eye_bf, trash, eps_col)

    return nc


def phase_p0(nc, S, ext, eye_bf, eye_f32, ones_col, ones_row, bqkv_sb, bq32, bv_bc, eps_col):
    with (
        nc.sbuf_tensor("p0stg", [128, 128], F32) as stg,
        nc.sbuf_tensor("bvrow", [1, 1024], F32) as bvrow,
        nc.psum_tensor("p0ps", [128, 512], F32) as ps,
    ):
        ph = Ph(S)
        ld1 = ph.add("sync", lambda e: e.dma_start(stg[:, :], ext["eye"]), inc="sLd")
        ph.add("vector", lambda e: e.tensor_copy(eye_bf[:, :], stg[:, :]),
               waits=[("sLd", ld1)], inc="sV")
        ph.add("vector", lambda e: e.tensor_copy(eye_f32[:, :], stg[:, :]),
               waits=[("sLd", ld1)], inc="sV")
        ld2 = ph.add("sync", lambda e: e.dma_start(
            bqkv_sb[:, :], ext["bqkv"].rearrange("(i p) -> p i", p=128),
            allow_slow_non_contiguous=True), inc="sLd")
        ph.add("scalar", lambda e: e.mul(bq32[:, :], bqkv_sb[:, 0:8], QSCALE),
               waits=[("sLd", ld2)], inc="sS")
        ld3 = ph.add("sync", lambda e: e.dma_start(
            bvrow[:, :], ext["bqkv"][2048:3072].rearrange("(o n) -> o n", o=1)), inc="sLd")
        ph.add("vector", lambda e: e.memset(ones_col[:, :], 1.0), inc="sV")
        ph.add("vector", lambda e: e.memset(eps_col[:, :], EPS), inc="sV")
        vo = ph.add("vector", lambda e: e.memset(ones_row[:, :], 1.0), inc="sV")
        for h in range(2):
            t = ph.add("tensor", lambda e, h=h: e.matmul(
                ps[:, h * 512:(h + 1) * 512] if False else ps[:, 0:512],
                ones_row[:, :], bvrow[:, h * 512:(h + 1) * 512], start=True, stop=True),
                waits=[("sLd", ld3), ("sV", vo), ("sS", S.cnt["sS"])], inc="sT")
            ph.add("scalar", lambda e, h=h: e.copy(bv_bc[:, h * 512:(h + 1) * 512], ps[:, 0:512]),
                   waits=[("sT", t)], inc="sS")
        S.emit(ph)


def phase_pscale(nc, S, ext, scal, eye_f32, ones_row):
    NPER = LFFN // 128
    chunks = [(i * 4096, min(NPER, (i + 1) * 4096)) for i in range((NPER + 4095) // 4096)]
    with (
        nc.sbuf_tensor("scstg", [128, 2 * 4096], F32) as stg,
        nc.sbuf_tensor("scpart", [128, 16], F32) as part,
        nc.sbuf_tensor("scacc", [128, 4], F32) as acc,
        nc.sbuf_tensor("sctot", [1, 4], F32) as tot,
        nc.psum_tensor("scps", [128, 128], F32) as ps,
        nc.psum_tensor("scps2", [128, 4], F32) as ps2,
    ):
        ph = Ph(S)
        lastred = {0: 0, 1: 0}
        for mi, (wname, col_sc2, col_sc) in enumerate(
                [("wg", 0, 3), ("wu", 1, 4), ("wd", 2, None)]):
            flat = ext[wname].rearrange("a b c -> (a b c)").rearrange("(p n) -> p n", p=128)
            rfin = 0
            for ci, (c0, c1) in enumerate(chunks):
                b = ci % 2
                ld = ph.add("sync", lambda e, b=b, c0=c0, c1=c1, flat=flat:
                            e.dma_start(stg[:, b * 4096:b * 4096 + (c1 - c0)], flat[:, c0:c1]),
                            waits=[("sV", lastred[b])], inc="sLd")
                lastred[b] = ph.add("vector", lambda e, b=b, w=c1 - c0, ci=ci:
                                    e.reduce_sum(part[:, ci:ci + 1],
                                                 stg[:, b * 4096:b * 4096 + w],
                                                 AX.X, apply_absolute_value=True),
                                    waits=[("sLd", ld)], inc="sV")
                rfin = lastred[b]
            vacc = ph.add("vector", lambda e, mi=mi, n=len(chunks):
                          e.reduce_sum(acc[:, mi:mi + 1], part[:, 0:n], AX.X),
                          waits=[("sV", rfin)], inc="sV")
            t1 = ph.add("tensor", lambda e, mi=mi:
                        e.transpose(ps[0:1, 0:128], acc[:, mi:mi + 1], eye_f32[:, :]),
                        waits=[("sV", vacc), ("sV", getattr(ph, "_psrd", 0))], inc="sT")
            vt = ph.add("vector", lambda e, mi=mi:
                        e.reduce_sum(tot[0:1, mi:mi + 1], ps[0:1, 0:128], AX.X),
                        waits=[("sT", t1)], inc="sV")
            setattr(ph, "_psrd", vt)
            t2 = ph.add("tensor", lambda e, mi=mi:
                        e.matmul(ps2[:, mi:mi + 1], ones_row[:, :], tot[0:1, mi:mi + 1],
                                 start=True, stop=True),
                        waits=[("sV", vt)], inc="sT")
            ph.add("scalar", lambda e, mi=mi, col=col_sc2:
                   e.mul(scal[:, col:col + 1], ps2[:, mi:mi + 1], 0.5 / LFFN),
                   waits=[("sT", t2)], inc="sS")
            if col_sc is not None:
                ph.add("scalar", lambda e, mi=mi, col=col_sc:
                       e.mul(scal[:, col:col + 1], ps2[:, mi:mi + 1], 1.0 / LFFN),
                       waits=[("sT", t2)], inc="sS")
        S.emit(ph)


def emit_ternary(ph, src_ap, rows, width, absb, sgnb, ternb, thr_ap, wait_ld,
                 wait_tern=()):
    sA = ph.add("scalar", lambda e, s=src_ap, r=rows, w=width:
                e.activation(absb[0:r, 0:w], s, AF.Abs), waits=wait_ld, inc="sS")
    sG = ph.add("scalar", lambda e, s=src_ap, r=rows, w=width:
                e.sign(sgnb[0:r, 0:w], s), waits=wait_ld, inc="sS")
    vM = ph.add("vector", lambda e, r=rows, w=width, thr=thr_ap:
                e.tensor_single_scalar(absb[0:r, 0:w], absb[0:r, 0:w], thr[0:r, :],
                                       ALU.is_gt),
                waits=[("sS", sA)], inc="sV")
    vT = ph.add("vector", lambda e, r=rows, w=width:
                e.tensor_mul(ternb[0:r, 0:w], absb[0:r, 0:w], sgnb[0:r, 0:w]),
                waits=[("sS", sG), ("sV", vM)] + list(wait_tern), inc="sV")
    return vT, sG


def phase_wprep_gu(nc, S, ext, scr, eye_bf, scal):
    otiles = [(0, 128), (128, 128), (256, 128), (384, 128), (512, 128), (640, 128),
              (768, 128), (896, 128), (1024, 128), (1152, 128), (1238, 128)]
    with (
        nc.sbuf_tensor("wp_in", [128, 2 * 512], F32) as stg,
        nc.sbuf_tensor("wp_abs", [128, 512], F32) as absb,
        nc.sbuf_tensor("wp_sgn", [128, 512], F32) as sgnb,
        nc.sbuf_tensor("wp_tern", [128, 2 * 512], BF16) as tern,
        nc.sbuf_tensor("wp_out", [128, 2 * 512], BF16) as outb,
        nc.psum_tensor("wp_ps0", [128, 512], BF16) as ps0,
        nc.psum_tensor("wp_ps1", [128, 512], BF16) as ps1,
    ):
        tps = [ps0, ps1]
        ph = Ph(S)
        prev_vT = 0
        stg_rd = {0: 0, 1: 0}
        tern_rd = {0: 0, 1: 0}
        wst = {0: 0, 1: 0}
        ev_prev = {0: 0, 1: 0}
        for mi, (wname, dst, thrcol) in enumerate(
                [("wg", scr["wgtS"], 0), ("wu", scr["wutS"], 1)]):
            for j in range(8):
                for ti, (orow0, orows) in enumerate(otiles):
                    it = (mi * 8 + j) * 11 + ti
                    b = it % 2
                    ld = ph.add("sync", lambda e, b=b, w=wname, j=j, o0=orow0, r=orows:
                                e.dma_start(stg[0:r, b * 512:b * 512 + 512],
                                            ext[w][j, o0:o0 + r, :]),
                                waits=[("sS", stg_rd[b])], inc="sLd")
                    vT, sG = emit_ternary(ph, stg[0:orows, b * 512:b * 512 + 512],
                                          orows, 512, absb, sgnb,
                                          tern[:, b * 512:b * 512 + 512],
                                          scal[:, thrcol:thrcol + 1],
                                          [("sLd", ld), ("sV", prev_vT)],
                                          [("sT", tern_rd[b])])
                    prev_vT = vT
                    stg_rd[b] = sG
                    tcnt = None
                    for dc in range(4):
                        tcnt = ph.add("tensor", lambda e, b=b, dc=dc, r=orows:
                                      e.transpose(tps[b][0:128, dc * 128:dc * 128 + r],
                                                  tern[0:r, b * 512 + dc * 128:b * 512 + (dc + 1) * 128],
                                                  eye_bf[0:r, 0:r]),
                                      waits=([("sV", vT), ("sS", ev_prev[b])]
                                             if dc == 0 else []),
                                      inc="sT" if dc == 3 else None)
                    tern_rd[b] = tcnt
                    ev = ph.add("scalar", lambda e, b=b, r=orows: e.copy(
                        outb[:, b * 512:(b + 1) * 512]
                        .rearrange("p (c o) -> p c o", c=4)[:, :, 0:r],
                        tps[b][:, :].rearrange("p (c o) -> p c o", c=4)[:, :, 0:r]),
                                waits=[("sT", tcnt), ("sSt", wst[b])], inc="sS")
                    ev_prev[b] = ev
                    wst[b] = ph.add("gpsimd", lambda e, b=b, j=j, o0=orow0, r=orows, dst=dst:
                                    e.dma_start(
                                        dst.rearrange("(c p) o -> p c o", p=128)[:, j::8, o0:o0 + r],
                                        outb[:, b * 512:b * 512 + 512]
                                        .rearrange("p (c o) -> p c o", c=4)[:, :, 0:r]),
                                    waits=[("sS", ev)], inc="sSt")
        ph.add("gpsimd", lambda e: e.engine_nop(), waits=[("sSt", S.cnt["sSt"])])
        S.emit(ph)


def phase_wprep_d(nc, S, ext, scr, eye_bf, scal):
    with (
        nc.sbuf_tensor("wd_in", [128, 2 * H8], F32) as stg,
        nc.sbuf_tensor("wd_abs", [128, H8], F32) as absb,
        nc.sbuf_tensor("wd_sgn", [128, H8], F32) as sgnb,
        nc.sbuf_tensor("wd_tern", [128, 2 * H8], BF16) as tern,
        nc.sbuf_tensor("wd_out", [128, 2 * 11 * 128], BF16) as outb,
        nc.psum_tensor("wd_ps0", [128, 512], BF16) as ps0,
        nc.psum_tensor("wd_ps1", [128, 512], BF16) as ps1,
        nc.psum_tensor("wd_ps2", [128, 512], BF16) as ps2,
    ):
        tps = [ps0, ps1, ps2]
        ph = Ph(S)
        dst = scr["wdtS"]
        mzo = ph.add("vector", lambda e: e.memset(outb[:, :], 0.0), inc="sV")
        prev_vT = mzo
        stg_rd = {0: 0, 1: 0}
        tern_rd = {0: 0, 1: 0}
        dev = {0: 0, 1: 0, 2: 0}
        dst_done = {0: 0, 1: 0}
        for j in range(8):
            for oc in range(4):
                it = j * 4 + oc
                b = it % 2
                ld = ph.add("sync", lambda e, b=b, j=j, oc=oc:
                            e.dma_start(stg[:, b * H8:(b + 1) * H8],
                                        ext["wd"][j, oc * 128:(oc + 1) * 128, :]),
                            waits=[("sS", stg_rd[b])], inc="sLd")
                vT, sG = emit_ternary(ph, stg[:, b * H8:(b + 1) * H8], 128, H8,
                                      absb, sgnb, tern[:, b * H8:(b + 1) * H8],
                                      scal[:, 2:3], [("sLd", ld), ("sV", prev_vT)],
                                      [("sT", tern_rd[b])])
                prev_vT = vT
                stg_rd[b] = sG
                evs = []
                tlast = None
                for pf in range(3):
                    ndc = 4 if pf < 2 else 3
                    tcnt = None
                    for q in range(ndc):
                        dc = pf * 4 + q
                        w = WD_W[dc]
                        tcnt = ph.add("tensor", lambda e, b=b, pf=pf, q=q, dc=dc, w=w:
                                      e.transpose(tps[pf][0:w, q * 128:q * 128 + 128],
                                                  tern[:, b * H8 + dc * 128:b * H8 + dc * 128 + w],
                                                  eye_bf[:, :]),
                                      waits=([("sV", vT), ("sS", dev[pf]),
                                              ("sSt", dst_done[b])]
                                             if q == 0 else []),
                                      inc="sT" if q == ndc - 1 else None)
                    ev = None
                    for q in range(ndc):
                        dc = pf * 4 + q
                        w = WD_W[dc]
                        ev = ph.add("scalar", lambda e, b=b, pf=pf, q=q, dc=dc, w=w:
                                    e.copy(outb[0:w, b * 1408 + dc * 128:b * 1408 + (dc + 1) * 128],
                                           tps[pf][0:w, q * 128:(q + 1) * 128]),
                                    waits=[("sT", tcnt), ("sV", mzo)] if q == 0 else [],
                                    inc="sS")
                    dev[pf] = ev
                    evs.append(ev)
                    tlast = tcnt
                tern_rd[b] = tlast
                dst_done[b] = ph.add("gpsimd", lambda e, b=b, j=j, oc=oc, dst=dst:
                                     e.dma_start(
                                         dst.rearrange("(c p) o -> p c o", p=128)
                                         [:, j * 11:(j + 1) * 11, oc * 128:(oc + 1) * 128],
                                         outb[:, b * 1408:(b + 1) * 1408]
                                         .rearrange("p (c o) -> p c o", c=11)),
                                     waits=[("sS", evs[-1])], inc="sSt")
        ph.add("gpsimd", lambda e: e.engine_nop(), waits=[("sSt", S.cnt["sSt"])])
        S.emit(ph)


def phase_a1k(nc, S, ext, scr, eye_bf, bqkv_sb, trash, eps_col):
    with (
        nc.sbuf_tensor("wk", [128, FC * 1024], BF16) as wk,
        nc.sbuf_tensor("a1_wstg", [128, 2 * 1024], F32) as wstg,
        nc.sbuf_tensor("a1_x", [128, 2 * 4096], F32) as xb,
        nc.sbuf_tensor("a1_lnx", [128, 2 * 4096], BF16) as lnx,
        nc.sbuf_tensor("a1_lnxt", [128, FC * 512], BF16) as lnxt,
        nc.sbuf_tensor("a1_kt", [128, 2 * 512], BF16) as ktb,
        nc.sbuf_tensor("a1_st", [128, 16], F32) as stats,
        nc.psum_tensor("a1_kp0", [128, 512], F32) as kp0,
        nc.psum_tensor("a1_kp1", [128, 512], F32) as kp1,
        nc.psum_tensor("a1_tp0", [128, 512], BF16) as tp0,
        nc.psum_tensor("a1_tp1", [128, 512], BF16) as tp1,
    ):
        kps = [kp0, kp1]
        tps = [tp0, tp1]
        ph = Ph(S)
        cast = {0: 0, 1: 0}
        for fc in range(FC):
            b = fc % 2
            ld = ph.add("sync", lambda e, b=b, fc=fc: e.dma_start(
                wstg[:, b * 1024:(b + 1) * 1024],
                ext["wqkv"][fc * 128:(fc + 1) * 128, 1024:2048]),
                waits=[("sV", cast[b])], inc="sLd")
            cast[b] = ph.add("vector", lambda e, b=b, fc=fc: e.tensor_copy(
                wk[:, fc * 1024:(fc + 1) * 1024], wstg[:, b * 1024:(b + 1) * 1024]),
                waits=[("sLd", ld)], inc="sV")
        xread_v = {0: 0, 1: 0}
        xread_s = {0: 0, 1: 0}
        lnx_rd_t = {0: 0, 1: 0}
        tp_ev = {0: 0, 1: 0}
        kev = {0: 0, 1: 0}
        kst = {0: 0, 1: 0}
        lnxt_store = 0
        kmm_done = 0
        for g in range(NG):
            ev_last = 0
            for t in range(4):
                tile = g * 4 + t
                b = tile % 2
                ld = ph.add("sync", lambda e, b=b, tile=tile: e.dma_start(
                    xb[:, b * 4096:(b + 1) * 4096],
                    ext["xseq"][tile * 128:(tile + 1) * 128, :]),
                    waits=[("sV", xread_v[b]), ("sS", xread_s[b])], inc="sLd")
                st = stats[:, (tile % 2) * 8:(tile % 2) * 8 + 8]
                vdone, s1 = emit_ln(ph, xb[:, b * 4096:(b + 1) * 4096],
                                    lnx[:, b * 4096:(b + 1) * 4096], st, trash[:, :],
                                    [("sLd", ld), ("sT", lnx_rd_t[b])], eps_ap=eps_col[:, :])
                xread_v[b], xread_s[b] = vdone, s1
                for pf in range(8):
                    par = pf % 2
                    tcnt = None
                    for q in range(4):
                        fc = pf * 4 + q
                        tcnt = ph.add("tensor",
                                      lambda e, par=par, q=q, fc=fc, b=b:
                                      e.transpose(tps[par][:, q * 128:(q + 1) * 128],
                                                  lnx[:, b * 4096 + fc * 128:b * 4096 + (fc + 1) * 128],
                                                  eye_bf[:, :]),
                                      waits=([("sV", vdone), ("sS", tp_ev[par])] if q == 0 else []),
                                      inc="sT" if q == 3 else None)
                    ev = ph.add("scalar",
                                lambda e, par=par, pf=pf, t=t:
                                e.copy(lnxt.rearrange("p (f q) -> p f q", q=512)
                                       [:, pf * 4:(pf + 1) * 4, t * 128:(t + 1) * 128],
                                       tps[par][:, :].rearrange("p (f q) -> p f q", q=128)),
                                waits=[("sT", tcnt), ("sT", kmm_done), ("sSt", lnxt_store)],
                                inc="sS")
                    tp_ev[par] = ev
                    ev_last = ev
                lnx_rd_t[b] = S.cnt["sT"]
            for dc in range(DC):
                par = dc % 2
                mm = None
                for fc in range(FC):
                    mm = ph.add("tensor", lambda e, par=par, dc=dc, fc=fc:
                                e.matmul(kps[par][:, :],
                                         wk[:, fc * 1024 + dc * 128:fc * 1024 + (dc + 1) * 128],
                                         lnxt[:, fc * 512:(fc + 1) * 512],
                                         start=(fc == 0), stop=(fc == FC - 1)),
                                waits=([("sS", ev_last), ("sV", kev[par]),
                                        ("sV", cast[0]), ("sV", cast[1])] if fc == 0 else []),
                                inc="sT" if fc == FC - 1 else None)
                kev[par] = ph.add("vector", lambda e, par=par, dc=dc:
                                  e.tensor_scalar(ktb[:, par * 512:(par + 1) * 512],
                                                  kps[par][:, :], bqkv_sb[:, 8 + dc:9 + dc], None,
                                                  ALU.add),
                                  waits=[("sT", mm), ("sSt", kst[par])], inc="sV")
                kst[par] = ph.add("gpsimd", lambda e, par=par, dc=dc, g=g:
                                  e.dma_start(scr["ktS"][dc * 128:(dc + 1) * 128,
                                                         g * 512:(g + 1) * 512],
                                              ktb[:, par * 512:(par + 1) * 512]),
                                  waits=[("sV", kev[par])], inc="sSt")
            kmm_done = S.cnt["sT"]
            lnxt_store = ph.add("gpsimd", lambda e, g=g:
                                e.dma_start(scr["lnxtS"][g, :, :], lnxt[:, :]),
                                waits=[("sS", ev_last)], inc="sSt")
        ph.add("gpsimd", lambda e: e.engine_nop(), waits=[("sSt", S.cnt["sSt"])])
        S.emit(ph)


def phase_a1v(nc, S, ext, scr, bv_bc):
    with (
        nc.sbuf_tensor("wv", [128, FC * 1024], BF16) as wv,
        nc.sbuf_tensor("v_wstg", [128, 2 * 1024], F32) as wstg,
        nc.sbuf_tensor("v_lnxt", [128, 2 * FC * 512], BF16) as lnxt,
        nc.sbuf_tensor("v_vt", [128, 2 * 512], BF16) as vtb,
        nc.psum_tensor("v_p0", [128, 512], F32) as p0,
        nc.psum_tensor("v_p1", [128, 512], F32) as p1,
    ):
        vps = [p0, p1]
        ph = Ph(S)
        cast = {0: 0, 1: 0}
        for fc in range(FC):
            b = fc % 2
            ld = ph.add("sync", lambda e, b=b, fc=fc: e.dma_start(
                wstg[:, b * 1024:(b + 1) * 1024],
                ext["wqkv"][fc * 128:(fc + 1) * 128, 2048:3072]),
                waits=[("sV", cast[b])], inc="sLd")
            cast[b] = ph.add("vector", lambda e, b=b, fc=fc: e.tensor_copy(
                wv[:, fc * 1024:(fc + 1) * 1024], wstg[:, b * 1024:(b + 1) * 1024]),
                waits=[("sLd", ld)], inc="sV")
        gdone = {0: 0, 1: 0}
        vev = {0: 0, 1: 0}
        vst = {0: 0, 1: 0}
        for g in range(NG):
            gb = g % 2
            ld = ph.add("sync", lambda e, gb=gb, g=g: e.dma_start(
                lnxt[:, gb * FC * 512:(gb + 1) * FC * 512], scr["lnxtS"][g, :, :]),
                waits=[("sT", gdone[gb])], inc="sLd")
            for t in range(4):
                for half in range(2):
                    par = (t * 2 + half) % 2
                    mm = None
                    for fc in range(FC):
                        mm = ph.add("tensor", lambda e, par=par, gb=gb, t=t, half=half, fc=fc:
                                    e.matmul(vps[par][:, :],
                                             lnxt[:, gb * FC * 512 + fc * 512 + t * 128:
                                                  gb * FC * 512 + fc * 512 + (t + 1) * 128],
                                             wv[:, fc * 1024 + half * 512:fc * 1024 + (half + 1) * 512],
                                             start=(fc == 0), stop=(fc == FC - 1)),
                                    waits=([("sLd", ld), ("sV", vev[par]),
                                            ("sV", cast[0]), ("sV", cast[1])] if fc == 0 else []),
                                    inc="sT" if fc == FC - 1 else None)
                    vev[par] = ph.add("vector", lambda e, par=par, half=half:
                                      e.tensor_add(vtb[:, par * 512:(par + 1) * 512],
                                                   vps[par][:, :],
                                                   bv_bc[:, half * 512:(half + 1) * 512]),
                                      waits=[("sT", mm), ("sSt", vst[par])], inc="sV")
                    vst[par] = ph.add("gpsimd", lambda e, par=par, g=g, t=t, half=half:
                                      e.dma_start(scr["vS"][(g * 4 + t) * 128:(g * 4 + t + 1) * 128,
                                                            half * 512:(half + 1) * 512],
                                                  vtb[:, par * 512:(par + 1) * 512]),
                                      waits=[("sV", vev[par])], inc="sSt")
            gdone[gb] = S.cnt["sT"]
        ph.add("gpsimd", lambda e: e.engine_nop(), waits=[("sSt", S.cnt["sSt"])])
        S.emit(ph)


def phase_a1q(nc, S, ext, scr, eye_bf, bqkv_sb, trash, QT, eps_col):
    with (
        nc.sbuf_tensor("wq", [128, FC * 1024], BF16) as wq,
        nc.sbuf_tensor("q_wstg", [128, 2 * 1024], F32) as wstg,
        nc.sbuf_tensor("q_x", [128, 2 * 4096], F32) as xb,
        nc.sbuf_tensor("q_lnx", [128, 2 * 4096], BF16) as lnx,
        nc.sbuf_tensor("q_lnxt", [128, FC * 512], BF16) as lnxt,
        nc.sbuf_tensor("q_st", [128, 16], F32) as stats,
        nc.psum_tensor("q_p0", [128, 512], F32) as p0,
        nc.psum_tensor("q_p1", [128, 512], F32) as p1,
        nc.psum_tensor("q_tp0", [128, 512], BF16) as tp0,
        nc.psum_tensor("q_tp1", [128, 512], BF16) as tp1,
    ):
        qps = [p0, p1]
        tps = [tp0, tp1]
        ph = Ph(S)
        cast = {0: 0, 1: 0}
        for fc in range(FC):
            b = fc % 2
            ld = ph.add("sync", lambda e, b=b, fc=fc: e.dma_start(
                wstg[:, b * 1024:(b + 1) * 1024],
                ext["wqkv"][fc * 128:(fc + 1) * 128, 0:1024]),
                waits=[("sV", cast[b])], inc="sLd")
            cast[b] = ph.add("vector", lambda e, b=b, fc=fc: e.tensor_copy(
                wq[:, fc * 1024:(fc + 1) * 1024], wstg[:, b * 1024:(b + 1) * 1024]),
                waits=[("sLd", ld)], inc="sV")
        xread_v = {0: 0, 1: 0}
        xread_s = {0: 0, 1: 0}
        lnx_rd_t = {0: 0, 1: 0}
        tp_ev = {0: 0, 1: 0}
        qev = {0: 0, 1: 0}
        qmm_done = 0
        for g in range(OG):
            ev_last = 0
            for t in range(4):
                tile = g * 4 + t
                b = tile % 2
                ld = ph.add("sync", lambda e, b=b, tile=tile: e.dma_start(
                    xb[:, b * 4096:(b + 1) * 4096],
                    ext["xown"][tile * 128:(tile + 1) * 128, :]),
                    waits=[("sV", xread_v[b]), ("sS", xread_s[b])], inc="sLd")
                st = stats[:, (tile % 2) * 8:(tile % 2) * 8 + 8]
                vdone, s1 = emit_ln(ph, xb[:, b * 4096:(b + 1) * 4096],
                                    lnx[:, b * 4096:(b + 1) * 4096], st, trash[:, :],
                                    [("sLd", ld), ("sT", lnx_rd_t[b])], eps_ap=eps_col[:, :])
                xread_v[b], xread_s[b] = vdone, s1
                for pf in range(8):
                    par = pf % 2
                    tcnt = None
                    for q in range(4):
                        fc = pf * 4 + q
                        tcnt = ph.add("tensor",
                                      lambda e, par=par, q=q, fc=fc, b=b:
                                      e.transpose(tps[par][:, q * 128:(q + 1) * 128],
                                                  lnx[:, b * 4096 + fc * 128:b * 4096 + (fc + 1) * 128],
                                                  eye_bf[:, :]),
                                      waits=([("sV", vdone), ("sS", tp_ev[par])] if q == 0 else []),
                                      inc="sT" if q == 3 else None)
                    ev = ph.add("scalar",
                                lambda e, par=par, pf=pf, t=t:
                                e.copy(lnxt.rearrange("p (f q) -> p f q", q=512)
                                       [:, pf * 4:(pf + 1) * 4, t * 128:(t + 1) * 128],
                                       tps[par][:, :].rearrange("p (f q) -> p f q", q=128)),
                                waits=[("sT", tcnt), ("sT", qmm_done)], inc="sS")
                    tp_ev[par] = ev
                    ev_last = ev
                lnx_rd_t[b] = S.cnt["sT"]
            for dc in range(DC):
                par = dc % 2
                mm = None
                for fc in range(FC):
                    mm = ph.add("tensor", lambda e, par=par, dc=dc, fc=fc:
                                e.matmul(qps[par][:, :],
                                         wq[:, fc * 1024 + dc * 128:fc * 1024 + (dc + 1) * 128],
                                         lnxt[:, fc * 512:(fc + 1) * 512],
                                         start=(fc == 0), stop=(fc == FC - 1)),
                                waits=([("sS", ev_last), ("sV", qev[par]),
                                        ("sV", cast[0]), ("sV", cast[1])] if fc == 0 else []),
                                inc="sT" if fc == FC - 1 else None)
                qev[par] = ph.add("vector", lambda e, par=par, dc=dc, g=g:
                                  e.tensor_scalar(QT[:, dc * 1024 + g * 512:dc * 1024 + (g + 1) * 512],
                                                  qps[par][:, :], bqkv_sb[:, dc:dc + 1], QSCALE,
                                                  ALU.add, ALU.mult),
                                  waits=[("sT", mm)], inc="sV")
            qmm_done = S.cnt["sT"]
        S.emit(ph)


def phase_a2(nc, S, scr, eye_bf, trash, QT, oaT):
    with (
        nc.sbuf_tensor("KT", [128, DC * SEQ], BF16) as KT,
        nc.sbuf_tensor("Vb", [128, KCH * 1024], BF16) as Vb,
        nc.sbuf_tensor("scoresF", [128, SEQ], F32) as scoresF,
        nc.sbuf_tensor("Pbf", [128, SEQ], BF16) as Pbf,
        nc.sbuf_tensor("PT", [128, SEQ], BF16) as PT,
        nc.sbuf_tensor("oa", [128, 1024], BF16) as oa,
        nc.sbuf_tensor("a2_st", [128, 8], F32) as stats,
        nc.psum_tensor("a2_s0", [128, 512], F32) as s0,
        nc.psum_tensor("a2_s1", [128, 512], F32) as s1,
        nc.psum_tensor("a2_tp0", [128, 512], BF16) as tp0,
        nc.psum_tensor("a2_tp1", [128, 512], BF16) as tp1,
        nc.psum_tensor("a2_pv0", [128, 512], F32) as pv0,
        nc.psum_tensor("a2_pv1", [128, 512], F32) as pv1,
    ):
        sps = [s0, s1]
        tps = [tp0, tp1]
        pvs = [pv0, pv1]
        ph = Ph(S)
        ldk = ph.add("sync", lambda e: e.dma_start(
            KT.rearrange("p (d k) -> p d k", k=SEQ),
            scr["ktS"].rearrange("(d p) k -> p d k", p=128)), inc="sLd")
        ldv = ph.add("sync", lambda e: e.dma_start(
            Vb.rearrange("p (k d) -> p k d", d=1024),
            scr["vS"].rearrange("(k p) d -> p k d", p=128)), inc="sLd")
        sev = {0: 0, 1: 0}
        exp_prev = 0
        pt_tr_prev = 0
        pv_prev = 0
        oa_tr_prev = 0
        ptev = {0: 0, 1: 0}
        for qt in range(NQT):
            stp = stats[:, (qt % 2) * 4:(qt % 2) * 4 + 4]
            ev8 = 0
            for kg in range(8):
                par = kg % 2
                mm = None
                for dc in range(DC):
                    mm = ph.add("tensor", lambda e, par=par, qt=qt, kg=kg, dc=dc:
                                e.matmul(sps[par][:, :],
                                         QT[:, dc * 1024 + qt * 128:dc * 1024 + qt * 128 + 128],
                                         KT[:, dc * SEQ + kg * 512:dc * SEQ + (kg + 1) * 512],
                                         start=(dc == 0), stop=(dc == DC - 1)),
                                waits=([("sLd", ldk), ("sV", sev[par]),
                                        ("sS", exp_prev)] if dc == 0 else []),
                                inc="sT" if dc == DC - 1 else None)
                sev[par] = ph.add("vector", lambda e, par=par, kg=kg:
                                  e.tensor_copy(scoresF[:, kg * 512:(kg + 1) * 512], sps[par][:, :]),
                                  waits=[("sT", mm)], inc="sV")
                ev8 = sev[par]
            vmax = ph.add("vector", lambda e, stp=stp:
                          e.reduce_max(stp[:, 0:1], scoresF[:, :], AX.X, negate=True),
                          waits=[("sV", ev8)], inc="sV")
            exp_prev = ph.add("scalar", lambda e, stp=stp:
                              e.activation(Pbf[:, :], scoresF[:, :], AF.Exp,
                                           bias=stp[:, 0:1], scale=1.0,
                                           accum_out=stp[:, 1:2]),
                              waits=[("sV", vmax), ("sT", pt_tr_prev)], inc="sS")
            vrec = ph.add("vector", lambda e, stp=stp:
                          e.reciprocal(stp[:, 2:3], stp[:, 1:2]),
                          waits=[("sS", exp_prev)], inc="sV")
            ptev_last = 0
            for pf in range(8):
                par = pf % 2
                tcnt = None
                for q in range(4):
                    kc = pf * 4 + q
                    tcnt = ph.add("tensor", lambda e, par=par, q=q, kc=kc:
                                  e.transpose(tps[par][:, q * 128:(q + 1) * 128],
                                              Pbf[:, kc * 128:(kc + 1) * 128], eye_bf[:, :]),
                                  waits=([("sS", exp_prev), ("sS", ptev[par])] if q == 0 else []),
                                  inc="sT" if q == 3 else None)
                ptev[par] = ph.add("scalar", lambda e, par=par, pf=pf:
                                   e.copy(PT[:, pf * 512:(pf + 1) * 512], tps[par][:, :]),
                                   waits=[("sT", tcnt), ("sT", pv_prev)], inc="sS")
                ptev_last = ptev[par]
            oaev = 0
            for half in range(2):
                mm = None
                for kc in range(KCH):
                    mm = ph.add("tensor", lambda e, half=half, kc=kc:
                                e.matmul(pvs[half][:, :],
                                         PT[:, kc * 128:(kc + 1) * 128],
                                         Vb[:, kc * 1024 + half * 512:kc * 1024 + (half + 1) * 512],
                                         start=(kc == 0), stop=(kc == KCH - 1)),
                                waits=([("sLd", ldv), ("sS", ptev_last),
                                        ("sV", oa_tr_prev if False else 0),
                                        ("sT", oa_tr_prev)] if kc == 0 else []),
                                inc="sT" if kc == KCH - 1 else None)
                oaev = ph.add("vector", lambda e, half=half, stp=stp:
                              e.tensor_scalar_mul(oa[:, half * 512:(half + 1) * 512],
                                                  pvs[half][:, :], stp[:, 2:3]),
                              waits=[("sT", mm), ("sV", vrec)], inc="sV")
            pv_prev = S.cnt["sT"]
            for pf in range(2):
                par = pf % 2
                tcnt = None
                for q in range(4):
                    dvc = pf * 4 + q
                    tcnt = ph.add("tensor", lambda e, par=par, q=q, dvc=dvc:
                                  e.transpose(tps[par][:, q * 128:(q + 1) * 128],
                                              oa[:, dvc * 128:(dvc + 1) * 128], eye_bf[:, :]),
                                  waits=([("sV", oaev), ("sS", ptev[par])] if q == 0 else []),
                                  inc="sT" if q == 3 else None)
                ptev[par] = ph.add("scalar", lambda e, par=par, pf=pf, qt=qt:
                                   e.copy(oaT.rearrange("p (d q) -> p d q", q=1024)
                                          [:, pf * 4:(pf + 1) * 4, qt * 128:(qt + 1) * 128],
                                          tps[par][:, :].rearrange("p (d q) -> p d q", q=128)),
                                   waits=[("sT", tcnt)], inc="sS")
            oa_tr_prev = S.cnt["sT"]
            pt_tr_prev = oa_tr_prev
        S.emit(ph)


def phase_a3(nc, S, ext, scr, ones_row, trash, oaT, eps_col):
    with (
        nc.sbuf_tensor("wo_sb", [128, DC * 4096], BF16) as wo_sb,
        nc.sbuf_tensor("a3_wstg", [128, 4096], F32) as wstg,
        nc.sbuf_tensor("bo_row", [1, 4096], F32) as bo_row,
        nc.sbuf_tensor("bo_bc", [128, 4096], F32) as bo_bc,
        nc.sbuf_tensor("a3_attn", [128, 4096], F32) as attn,
        nc.sbuf_tensor("a3_x", [128, 4096], F32) as xb,
        nc.sbuf_tensor("a3_x2", [128, 4096], F32) as x2b,
        nc.sbuf_tensor("a3_st", [128, 8], F32) as stats,
        nc.psum_tensor("a3_p0", [128, 512], F32) as p0,
        nc.psum_tensor("a3_p1", [128, 512], F32) as p1,
    ):
        pps = [p0, p1]
        ph = Ph(S)
        ldb = ph.add("sync", lambda e: e.dma_start(
            bo_row[:, :], ext["bo"].rearrange("(o n) -> o n", o=1)), inc="sLd")
        bo_ev = {0: 0, 1: 0}
        for cg in range(8):
            par = cg % 2
            t = ph.add("tensor", lambda e, par=par, cg=cg: e.matmul(
                pps[par][:, :], ones_row[:, :], bo_row[:, cg * 512:(cg + 1) * 512],
                start=True, stop=True),
                waits=[("sLd", ldb), ("sS", bo_ev[par])], inc="sT")
            bo_ev[par] = ph.add(
                "scalar", lambda e, par=par, cg=cg:
                e.copy(bo_bc[:, cg * 512:(cg + 1) * 512], pps[par][:, :]),
                waits=[("sT", t)], inc="sS")
        bo_done = S.cnt["sS"]
        cast = 0
        for dvc in range(DC):
            ld = ph.add("sync", lambda e, dvc=dvc: e.dma_start(
                wstg[:, :], ext["wo"][dvc * 128:(dvc + 1) * 128, :]),
                waits=[("sV", cast)], inc="sLd")
            cast = ph.add("vector", lambda e, dvc=dvc: e.tensor_copy(
                wo_sb[:, dvc * 4096:(dvc + 1) * 4096], wstg[:, :]),
                waits=[("sLd", ld)], inc="sV")
        aev = {0: 0, 1: 0}
        attn_rd_v = 0
        attn_rd_s = 0
        x2_st = 0
        xread = 0
        for qt in range(NQT):
            ldx = ph.add("sync", lambda e, qt=qt: e.dma_start(
                xb[:, :], ext["xown"][qt * 128:(qt + 1) * 128, :]),
                waits=[("sV", xread)], inc="sLd")
            for cg in range(8):
                par = cg % 2
                mm = None
                for dvc in range(DC):
                    mm = ph.add("tensor", lambda e, par=par, qt=qt, cg=cg, dvc=dvc:
                                e.matmul(pps[par][:, :],
                                         oaT[:, dvc * 1024 + qt * 128:dvc * 1024 + qt * 128 + 128],
                                         wo_sb[:, dvc * 4096 + cg * 512:dvc * 4096 + (cg + 1) * 512],
                                         start=(dvc == 0), stop=(dvc == DC - 1)),
                                waits=([("sV", aev[par]), ("sV", cast),
                                        ("sS", bo_done)] if dvc == 0 else []),
                                inc="sT" if dvc == DC - 1 else None)
                aev[par] = ph.add("vector", lambda e, par=par, cg=cg:
                                  e.tensor_add(attn[:, cg * 512:(cg + 1) * 512],
                                               pps[par][:, :], bo_bc[:, cg * 512:(cg + 1) * 512]),
                                  waits=[("sT", mm), ("sV", attn_rd_v),
                                         ("sS", attn_rd_s)], inc="sV")
            vdone, s1 = emit_ln(ph, attn[:, :], attn[:, :], stats[:, 0:8], trash[:, :],
                                [("sV", aev[0]), ("sV", aev[1])], eps_ap=eps_col[:, :])
            attn_rd_s = s1
            vadd = ph.add("vector", lambda e: e.tensor_add(x2b[:, :], attn[:, :], xb[:, :]),
                          waits=[("sLd", ldx), ("sSt", x2_st), ("sV", vdone)], inc="sV")
            attn_rd_v = vadd
            xread = vadd
            x2_st = ph.add("gpsimd", lambda e, qt=qt: e.dma_start(
                scr["x2S"][qt * 128:(qt + 1) * 128, :], x2b[:, :]),
                waits=[("sV", vadd)], inc="sSt")
        ph.add("gpsimd", lambda e: e.engine_nop(), waits=[("sSt", S.cnt["sSt"])])
        S.emit(ph)


def phase_f1(nc, S, ext, scr, eye_bf, scal, trash, third, eps_col):
    o0, o1 = THIRDS[third]
    W = o1 - o0
    sfx = "_%d" % third
    with (
        nc.sbuf_tensor("f1_wg" + sfx, [128, FC * W], BF16) as wgt,
        nc.sbuf_tensor("f1_wu" + sfx, [128, FC * W], BF16) as wut,
        nc.sbuf_tensor("f1_lnxt" + sfx, [128, (2 if third > 0 else 1) * FC * 128], BF16) as lnxt,
        nc.sbuf_tensor("f1_lnxtn" + sfx, [128, FC * 128], BF16) as lnxtn,
        nc.sbuf_tensor("f1_sg" + sfx, [128, 2 * 512], BF16) as sgb,
        nc.sbuf_tensor("f1_h" + sfx, [128, 2 * 512], BF16) as hb,
        nc.psum_tensor("f1_g0" + sfx, [128, 512], F32) as g0,
        nc.psum_tensor("f1_g1" + sfx, [128, 512], F32) as g1,
        nc.psum_tensor("f1_u0" + sfx, [128, 512], F32) as u0,
        nc.psum_tensor("f1_u1" + sfx, [128, 512], F32) as u1,
    ):
        gps = [g0, g1]
        ups = [u0, u1]
        extra = []
        if third == 0:
            extra = [
                nc.sbuf_tensor("f1_x2", [128, 2 * 4096], F32),
                nc.sbuf_tensor("f1_lnx", [128, 4096], BF16),
                nc.sbuf_tensor("f1_st", [128, 16], F32),
                nc.psum_tensor("f1_tp0", [128, 512], BF16),
                nc.psum_tensor("f1_tp1", [128, 512], BF16),
            ]
            x2b, lnxb, stats, tp0, tp1 = [e.__enter__() for e in extra]
            tps = [tp0, tp1]
        ph = Ph(S)
        ldg = ph.add("sync", lambda e: e.dma_start(
            wgt.rearrange("p (c o) -> p c o", c=FC),
            scr["wgtS"].rearrange("(c p) o -> p c o", p=128)[:, :, o0:o1]), inc="sLd")
        ldu = ph.add("sync", lambda e: e.dma_start(
            wut.rearrange("p (c o) -> p c o", c=FC),
            scr["wutS"].rearrange("(c p) o -> p c o", p=128)[:, :, o0:o1]), inc="sLd")
        gev = {0: 0, 1: 0}
        uev = {0: 0, 1: 0}
        hst = {0: 0, 1: 0}
        tp_ev = {0: 0, 1: 0}
        lnxt_rd = {0: 0, 1: 0}
        neg_done = 0
        lnxt_st = 0
        xread = {0: 0, 1: 0}
        xread_s = {0: 0, 1: 0}
        for qt in range(NQT):
            if third == 0:
                b = qt % 2
                ldx = ph.add("sync", lambda e, b=b, qt=qt: e.dma_start(
                    x2b[:, b * 4096:(b + 1) * 4096], scr["x2S"][qt * 128:(qt + 1) * 128, :]),
                    waits=[("sV", xread[b]), ("sS", xread_s[b])], inc="sLd")
                st = stats[:, b * 8:b * 8 + 8]
                vdone, s1 = emit_ln(ph, x2b[:, b * 4096:(b + 1) * 4096], lnxb[:, :],
                                    st, trash[:, :],
                                    [("sLd", ldx), ("sT", getattr(ph, "_f1lnrd", 0))],
                                    eps_ap=eps_col[:, :])
                xread[b], xread_s[b] = vdone, s1
                ev_last = 0
                for pf in range(8):
                    par = pf % 2
                    tcnt = None
                    for q in range(4):
                        fc = pf * 4 + q
                        tcnt = ph.add("tensor", lambda e, par=par, q=q, fc=fc:
                                      e.transpose(tps[par][:, q * 128:(q + 1) * 128],
                                                  lnxb[:, fc * 128:(fc + 1) * 128], eye_bf[:, :]),
                                      waits=([("sV", vdone), ("sS", tp_ev[par])] if q == 0 else []),
                                      inc="sT" if q == 3 else None)
                    tp_ev[par] = ph.add("scalar", lambda e, par=par, pf=pf:
                                        e.copy(lnxt[:, pf * 512:(pf + 1) * 512], tps[par][:, :]),
                                        waits=[("sT", tcnt), ("sT", getattr(ph, "_f1mm", 0)),
                                               ("sSt", lnxt_st)], inc="sS")
                    ev_last = tp_ev[par]
                setattr(ph, "_f1lnrd", S.cnt["sT"])
                neg_done = ph.add("scalar", lambda e: e.mul(lnxtn[:, :], lnxt[:, :], -1.0),
                                  waits=[("sS", ev_last)], inc="sS")
                lnxt_st = ph.add("gpsimd", lambda e, qt=qt: e.dma_start(
                    scr["lnx2tS"][qt, :, :], lnxt[:, :]),
                    waits=[("sS", ev_last)], inc="sSt")
                lnxt_view = lnxt
                lnxt_ready = [("sS", neg_done)]
            else:
                gb = qt % 2
                ldx = ph.add("sync", lambda e, gb=gb, qt=qt: e.dma_start(
                    lnxt[:, gb * FC * 128:(gb + 1) * FC * 128], scr["lnx2tS"][qt, :, :]),
                    waits=[("sT", lnxt_rd[gb])], inc="sLd")
                neg_done = ph.add("scalar", lambda e, gb=gb: e.mul(
                    lnxtn[:, :], lnxt[:, gb * FC * 128:(gb + 1) * FC * 128], -1.0),
                    waits=[("sLd", ldx), ("sT", getattr(ph, "_f1mm", 0))], inc="sS")
                lnxt_view = lnxt[:, (qt % 2) * FC * 128:((qt % 2) + 1) * FC * 128]
                lnxt_ready = [("sLd", ldx), ("sS", neg_done)]
            for k in range(8):
                par = k % 2
                gm = um = None
                for fc in range(FC):
                    i, dc = fc // 4, fc % 4
                    j = int(JMAP[i, k])
                    pos = SMAP[i, k] > 0
                    lsrc = lnxt_view if pos else lnxtn
                    gm = ph.add("tensor", lambda e, par=par, fc=fc, dc=dc, j=j,
                                lsrc=lsrc, W=W:
                                e.matmul(gps[par][:, 0:W],
                                         lsrc[:, fc * 128:(fc + 1) * 128],
                                         wgt[:, (dc * 8 + j) * W:(dc * 8 + j + 1) * W],
                                         start=(fc == 0), stop=(fc == FC - 1)),
                                waits=(lnxt_ready + [("sLd", ldg), ("sS", gev[par])]
                                       if fc == 0 else []),
                                inc="sT" if fc == FC - 1 else None)
                for fc in range(FC):
                    i, dc = fc // 4, fc % 4
                    j = int(JMAP[i, k])
                    pos = SMAP[i, k] > 0
                    lsrc = lnxt_view if pos else lnxtn
                    um = ph.add("tensor", lambda e, par=par, fc=fc, dc=dc, j=j,
                                lsrc=lsrc, W=W:
                                e.matmul(ups[par][:, 0:W],
                                         lsrc[:, fc * 128:(fc + 1) * 128],
                                         wut[:, (dc * 8 + j) * W:(dc * 8 + j + 1) * W],
                                         start=(fc == 0), stop=(fc == FC - 1)),
                                waits=([("sLd", ldu), ("sV", uev[par])] if fc == 0 else []),
                                inc="sT" if fc == FC - 1 else None)
                setattr(ph, "_f1mm", S.cnt["sT"])
                gev[par] = ph.add("scalar", lambda e, par=par, W=W:
                                  e.activation(sgb[:, par * 512:par * 512 + W], gps[par][:, 0:W],
                                               AF.Silu, bias=0.0, scale=scal[:, 3:4]),
                                  waits=[("sT", gm), ("sV", uev[par])], inc="sS")
                uev[par] = ph.add("vector", lambda e, par=par, W=W:
                                  e.scalar_tensor_tensor(hb[:, par * 512:par * 512 + W],
                                                         ups[par][:, 0:W], scal[:, 4:5],
                                                         sgb[:, par * 512:par * 512 + W],
                                                         ALU.mult, ALU.mult),
                                  waits=[("sT", um), ("sS", gev[par]), ("sSt", hst[par])],
                                  inc="sV")
                hst[par] = ph.add("gpsimd", lambda e, par=par, qt=qt, k=k, o0=o0, W=W:
                                  e.dma_start(scr["hS"][qt * 128:(qt + 1) * 128,
                                                        k * H8 + o0:k * H8 + o0 + W],
                                              hb[:, par * 512:par * 512 + W]),
                                  waits=[("sV", uev[par])], inc="sSt")
            lnxt_rd[qt % 2] = S.cnt["sT"]
        ph.add("gpsimd", lambda e: e.engine_nop(), waits=[("sSt", S.cnt["sSt"])])
        S.emit(ph)
        if third == 0:
            for e in reversed(extra):
                e.__exit__(None, None, None)


def phase_f2(nc, S, ext, scr, eye_bf, trash, eps_col):
    NCH = 11
    with (
        nc.sbuf_tensor("f2_wd", [128, 8 * NCH * 512], BF16) as wdt,
        nc.sbuf_tensor("f2_h", [128, HID], BF16) as hbuf,
        nc.sbuf_tensor("f2_ht", [128, 8 * NCH * 128], BF16) as ht,
        nc.sbuf_tensor("f2_htn", [128, 8 * NCH * 128], BF16) as htn,
        nc.sbuf_tensor("f2_x2", [128, 4096], F32) as x2b,
        nc.sbuf_tensor("f2_ffn", [128, 4096], F32) as ffn,
        nc.sbuf_tensor("f2_st", [128, 8], F32) as stats,
        nc.psum_tensor("f2_d0", [128, 512], F32) as d0,
        nc.psum_tensor("f2_d1", [128, 512], F32) as d1,
        nc.psum_tensor("f2_d2", [128, 512], F32) as d2,
        nc.psum_tensor("f2_d3", [128, 512], F32) as d3,
        nc.psum_tensor("f2_tp0", [128, 512], BF16) as tp0,
        nc.psum_tensor("f2_tp1", [128, 512], BF16) as tp1,
    ):
        dps = [d0, d1, d2, d3]
        tps = [tp0, tp1]
        ph = Ph(S)
        ldw = ph.add("sync", lambda e: e.dma_start(
            wdt.rearrange("p (c o) -> p c o", c=8 * NCH),
            scr["wdtS"].rearrange("(c p) o -> p c o", p=128)), inc="sLd")
        mz1 = ph.add("vector", lambda e: e.memset(ht[:, :], 0.0), inc="sV")
        tp_ev = {0: 0, 1: 0}
        dev = {0: 0, 1: 0, 2: 0, 3: 0}
        out_st = 0
        x2_rd = 0
        ffn_rd_v = 0
        ffn_rd_s = 0
        for qt in range(NQT):
            ldh = ph.add("sync", lambda e, qt=qt: e.dma_start(
                hbuf[:, :], scr["hS"][qt * 128:(qt + 1) * 128, :]),
                waits=[("sT", getattr(ph, "_f2tr", 0))], inc="sLd")
            ev_last = 0
            nfill = 0
            for i in range(8):
                for pf in range(3):
                    ndc = 4 if pf < 2 else 3
                    par = nfill % 2
                    nfill += 1
                    tcnt = None
                    for q in range(ndc):
                        dc = pf * 4 + q
                        w = WD_W[dc]
                        tcnt = ph.add("tensor", lambda e, par=par, q=q, i=i, dc=dc, w=w:
                                      e.transpose(tps[par][0:w, q * 128:q * 128 + 128],
                                                  hbuf[:, i * H8 + dc * 128:i * H8 + dc * 128 + w],
                                                  eye_bf[:, :]),
                                      waits=([("sLd", ldh), ("sV", mz1), ("sS", tp_ev[par]),
                                              ("sT", getattr(ph, "_f2dmm", 0))] if q == 0 else []),
                                      inc="sT" if q == ndc - 1 else None)
                    evq = None
                    for q in range(ndc):
                        dc = pf * 4 + q
                        w = WD_W[dc]
                        evq = ph.add("scalar", lambda e, par=par, i=i, dc=dc, q=q, w=w:
                                     e.copy(ht[0:w, (i * NCH + dc) * 128:
                                               (i * NCH + dc + 1) * 128],
                                            tps[par][0:w, q * 128:(q + 1) * 128]),
                                     waits=[("sT", tcnt)] if q == 0 else [],
                                     inc="sS")
                    tp_ev[par] = evq
                    ev_last = evq
            setattr(ph, "_f2tr", S.cnt["sT"])
            neg = ph.add("scalar", lambda e: e.mul(htn[:, :], ht[:, :], -1.0),
                         waits=[("sS", ev_last)], inc="sS")
            ldx2 = ph.add("sync", lambda e, qt=qt: e.dma_start(
                x2b[:, :], scr["x2S"][qt * 128:(qt + 1) * 128, :]),
                waits=[("sV", x2_rd), ("sSt", out_st)], inc="sLd")
            for wave in range(2):
                for kk in range(4):
                    k = wave * 4 + kk
                    mm = None
                    first = True
                    for i in range(8):
                        j = int(JMAP[i, k])
                        pos = SMAP[i, k] > 0
                        hsrc = ht if pos else htn
                        for dc in range(NCH):
                            w = WD_W[dc]
                            mm = ph.add("tensor",
                                        lambda e, kk=kk, i=i, dc=dc, j=j, hsrc=hsrc,
                                        first=first:
                                        e.matmul(dps[kk][:, :],
                                                 hsrc[:, (i * NCH + dc) * 128:
                                                      (i * NCH + dc + 1) * 128],
                                                 wdt[:, (j * NCH + dc) * 512:
                                                     (j * NCH + dc + 1) * 512],
                                                 start=first, stop=(i == 7 and dc == NCH - 1)),
                                        waits=([("sS", neg), ("sLd", ldw),
                                                ("sS", dev[kk])] if first else []),
                                        inc="sT" if (i == 7 and dc == NCH - 1) else None)
                            first = False
                    dev[kk] = ph.add("scalar", lambda e, kk=kk, k=k:
                                     e.copy(ffn[:, k * 512:(k + 1) * 512], dps[kk][:, :]),
                                     waits=[("sT", mm), ("sV", ffn_rd_v),
                                            ("sS", ffn_rd_s if kk == 0 and wave == 0 else 0)],
                                     inc="sS")
            setattr(ph, "_f2dmm", S.cnt["sT"])
            vdone, s1 = emit_ln(ph, ffn[:, :], ffn[:, :], stats[:, 0:8], trash[:, :],
                                [("sS", dev[0]), ("sS", dev[1]), ("sS", dev[2]), ("sS", dev[3])],
                                eps_ap=eps_col[:, :])
            ffn_rd_s = s1
            vadd = ph.add("vector", lambda e: e.tensor_add(x2b[:, :], ffn[:, :], x2b[:, :]),
                          waits=[("sLd", ldx2), ("sSt", out_st), ("sV", vdone)], inc="sV")
            ffn_rd_v = vadd
            x2_rd = vadd
            out_st = ph.add("gpsimd", lambda e, qt=qt: e.dma_start(
                ext["out"][qt * 128:(qt + 1) * 128, :], x2b[:, :]),
                waits=[("sV", vadd)], inc="sSt")
        ph.add("gpsimd", lambda e: e.engine_nop(), waits=[("sSt", S.cnt["sSt"])])
        S.emit(ph)


_NC_CACHE = None


def kernel(L0_attn_x, L0_attn_Wqkv, L0_attn_bqkv, L0_attn_Wo, L0_attn_bo,
           L0_ffn_w_gate, L0_ffn_w_up, L0_ffn_w_down):
    global _NC_CACHE
    x = np.ascontiguousarray(np.asarray(L0_attn_x, dtype=np.float32))
    wqkv = np.ascontiguousarray(np.asarray(L0_attn_Wqkv, dtype=np.float32))
    bqkv = np.ascontiguousarray(np.asarray(L0_attn_bqkv, dtype=np.float32))
    wo = np.ascontiguousarray(np.asarray(L0_attn_Wo, dtype=np.float32))
    bo = np.ascontiguousarray(np.asarray(L0_attn_bo, dtype=np.float32))
    wg = np.ascontiguousarray(np.asarray(L0_ffn_w_gate, dtype=np.float32))
    wu = np.ascontiguousarray(np.asarray(L0_ffn_w_up, dtype=np.float32))
    wd = np.ascontiguousarray(np.asarray(L0_ffn_w_down, dtype=np.float32))
    eye = np.eye(128, dtype=np.float32)

    if _NC_CACHE is None:
        _NC_CACHE = build_nc()
    nc = _NC_CACHE

    in_maps = []
    for c in range(8):
        b = c // 4
        q0 = (c % 4) * TOK
        in_maps.append({
            "xseq": x[b],
            "xown": np.ascontiguousarray(x[b, q0:q0 + TOK]),
            "wqkv": wqkv, "bqkv": bqkv, "wo": wo, "bo": bo,
            "wg": wg, "wu": wu, "wd": wd, "eye": eye,
        })
    res = run_bass_kernel_spmd(nc, in_maps, core_ids=list(range(8)))
    out = np.empty((2, SEQ, C), dtype=np.float32)
    for c in range(8):
        b = c // 4
        q0 = (c % 4) * TOK
        out[b, q0:q0 + TOK] = res.results[c]["out"]
    return out


if __name__ == "__main__":
    import reference
    ins = reference.setup_inputs(0)
    got = kernel(**{k: np.asarray(v) for k, v in ins.items()})
    ref = np.asarray(reference.reference(**ins))
    num = float(np.mean((got - ref) ** 2))
    den = float(np.mean(ref ** 2))
    print("residual variance ratio:", num / den)


# version 08
# speedup vs baseline: 1.0690x; 1.0690x over previous
import numpy as np
from contextlib import ExitStack

import concourse.bass as bass
import concourse.mybir as mybir
from concourse.bass_utils import run_bass_kernel_spmd

F32 = mybir.dt.float32
BF16 = mybir.dt.bfloat16
AF = mybir.ActivationFunctionType
ALU = mybir.AluOpType
AX = mybir.AxisListType

SEQ = 4096
TOK = 1024
C = 4096
D = 1024
HID = 10928
H8 = 1366
FC = C // 128
DC = D // 128
KCH = SEQ // 128
NQT = TOK // 128
NG = SEQ // 512
OG = TOK // 512
EPS = 1e-5
QSCALE = 1.0 / 32.0
LFFN = 8 * H8 * 512
THIRDS = [(0, 512), (512, 1024), (1024, 1366)]
WD_W = [128] * 10 + [86]
ENGS = ("sync", "gpsimd", "tensor", "vector", "scalar")


def _oct_maps():
    TRIPLES = [(1, 2, 3), (1, 4, 5), (1, 7, 6), (2, 4, 6), (2, 5, 7), (3, 4, 7), (3, 6, 5)]
    S = np.zeros((8, 8, 8), np.float32)
    S[0, 0, 0] = 1.0
    for i in range(1, 8):
        S[0, i, i] = 1.0
        S[i, 0, i] = 1.0
        S[i, i, 0] = -1.0
    for a, b, c in TRIPLES:
        for x, y, z in ((a, b, c), (b, c, a), (c, a, b)):
            S[x, y, z] = 1.0
            S[y, x, z] = -1.0
    jmap = np.zeros((8, 8), np.int64)
    smap = np.zeros((8, 8), np.float64)
    for i in range(8):
        for k in range(8):
            js = np.nonzero(S[:, i, k])[0]
            assert len(js) == 1
            jmap[i, k] = js[0]
            smap[i, k] = S[js[0], i, k]
    return jmap, smap


JMAP, SMAP = _oct_maps()
DEBUG_SCR = False


class Ph:
    def __init__(self, S):
        self.S = S
        self.ops = {e: [] for e in ENGS}

    def add(self, eng, fn, waits=(), inc=None, n=None):
        if n is None:
            n = 16 if inc in ("sLd", "sSt") else 1
        ws = [(s, v) for (s, v) in waits if v > 0]
        self.ops[eng].append((fn, ws, inc, n))
        if inc is None:
            return None
        self.S.cnt[inc] += n
        return self.S.cnt[inc]


class Sched:
    def __init__(self, nc, stack):
        self.nc = nc
        self.cnt = {}
        self.sems = {}
        self.stack = stack
        for name in ("sLd", "sSt", "sT", "sV", "sS"):
            h = stack.enter_context(nc.semaphore(name))
            self.sems[name] = h
            self.cnt[name] = 0

    def emit(self, ph):
        sems = self.sems
        with self.nc.Block() as blk:
            for e in ENGS:
                ops = ph.ops[e]
                if not ops:
                    continue

                def runner(eng, ops=ops):
                    seen = {}
                    for fn, ws, inc, n in ops:
                        for s, v in ws:
                            if seen.get(s, 0) < v:
                                eng.wait_ge(sems[s], v)
                                seen[s] = v
                        ins = fn(eng)
                        if inc is not None:
                            ins.then_inc(sems[inc], n)
                getattr(blk, e)(runner)


def emit_ln(ph, x_ap, out_ap, st, trash_ap, wait_x, n_feat=4096, eps_ap=None):
    inv = 1.0 / n_feat
    v1 = ph.add("vector", lambda e, x=x_ap, st=st: e.reduce_sum(st[:, 0:1], x, AX.X),
                waits=wait_x, inc="sV")
    s1 = ph.add("scalar", lambda e, x=x_ap, st=st, tr=trash_ap:
                e.activation(tr, x, AF.Square, accum_out=st[:, 1:2]),
                waits=wait_x, inc="sS")
    v2 = ph.add("vector", lambda e, st=st, inv=inv:
                e.tensor_scalar_mul(st[:, 2:3], st[:, 0:1], inv),
                waits=[("sV", v1)], inc="sV")
    s2 = ph.add("scalar", lambda e, st=st: e.activation(st[:, 3:4], st[:, 2:3], AF.Square),
                waits=[("sV", v2)], inc="sS")
    v3 = ph.add("vector", lambda e, st=st, inv=inv:
                e.tensor_scalar(st[:, 4:5], st[:, 1:2], inv, st[:, 3:4], ALU.mult, ALU.subtract),
                waits=[("sS", s2)], inc="sV")
    s3 = ph.add("scalar", lambda e, st=st, eps_ap=eps_ap:
                e.activation(st[:, 5:6], st[:, 4:5], AF.Sqrt, bias=eps_ap, scale=1.0),
                waits=[("sV", v3)], inc="sS")
    v3r = ph.add("vector", lambda e, st=st: e.reciprocal(st[:, 6:7], st[:, 5:6]),
                 waits=[("sS", s3)], inc="sV")
    v4 = ph.add("vector", lambda e, x=x_ap, o=out_ap, st=st:
                e.tensor_scalar(o, x, st[:, 2:3], st[:, 6:7], ALU.subtract, ALU.mult),
                waits=[("sV", v3r)], inc="sV")
    return v4, s1


def build_nc():
    nc = bass.Bass("TRN2", target_bir_lowering=False)

    ext = {}
    ext["xseq"] = nc.dram_tensor("xseq", [SEQ, C], F32, kind="ExternalInput").ap()
    ext["xown"] = nc.dram_tensor("xown", [TOK, C], F32, kind="ExternalInput").ap()
    ext["wqkv"] = nc.dram_tensor("wqkv", [C, 3 * D], F32, kind="ExternalInput").ap()
    ext["bqkv"] = nc.dram_tensor("bqkv", [3 * D], F32, kind="ExternalInput").ap()
    ext["wo"] = nc.dram_tensor("wo", [D, C], F32, kind="ExternalInput").ap()
    ext["bo"] = nc.dram_tensor("bo", [C], F32, kind="ExternalInput").ap()
    ext["wg"] = nc.dram_tensor("wg", [8, H8, 512], F32, kind="ExternalInput").ap()
    ext["wu"] = nc.dram_tensor("wu", [8, H8, 512], F32, kind="ExternalInput").ap()
    ext["wd"] = nc.dram_tensor("wd", [8, 512, H8], F32, kind="ExternalInput").ap()
    ext["eye"] = nc.dram_tensor("eye", [128, 128], F32, kind="ExternalInput").ap()
    ext["out"] = nc.dram_tensor("out", [TOK, C], F32, kind="ExternalOutput").ap()

    kind = "ExternalOutput" if DEBUG_SCR else "Internal"
    scr = {}
    scr["ktS"] = nc.dram_tensor("ktS", [D, SEQ], BF16, kind=kind).ap()
    scr["vS"] = nc.dram_tensor("vS", [SEQ, D], BF16, kind=kind).ap()
    scr["lnxtS"] = nc.dram_tensor("lnxtS", [NG, 128, FC * 512], BF16, kind="Internal").ap()
    scr["x2S"] = nc.dram_tensor("x2S", [TOK, C], F32, kind=kind).ap()
    scr["lnx2tS"] = nc.dram_tensor("lnx2tS", [FC, 128, TOK], BF16, kind=kind).ap()
    scr["hS"] = nc.dram_tensor("hS", [8, 11, 128, TOK], BF16, kind=kind).ap()
    scr["wgtS"] = nc.dram_tensor("wgtS", [FC * 128, H8], BF16, kind=kind).ap()
    scr["wutS"] = nc.dram_tensor("wutS", [FC * 128, H8], BF16, kind=kind).ap()
    scr["wdtS"] = nc.dram_tensor("wdtS", [8 * 11 * 128, 512], BF16, kind=kind).ap()

    with ExitStack() as stack:
        S = Sched(nc, stack)
        eye_bf = stack.enter_context(nc.sbuf_tensor("eye_bf", [128, 128], BF16))
        eye_f32 = stack.enter_context(nc.sbuf_tensor("eye_f32", [128, 128], F32))
        ones_col = stack.enter_context(nc.sbuf_tensor("ones_col", [128, 1], F32))
        ones_row = stack.enter_context(nc.sbuf_tensor("ones_row", [1, 128], F32))
        bqkv_sb = stack.enter_context(nc.sbuf_tensor("bqkv_sb", [128, 24], F32))
        bq32 = stack.enter_context(nc.sbuf_tensor("bq32", [128, 8], F32))
        bv_bc = stack.enter_context(nc.sbuf_tensor("bv_bc", [128, 1024], F32))
        scal = stack.enter_context(nc.sbuf_tensor("scal", [128, 8], F32))
        trash = stack.enter_context(nc.sbuf_tensor("trash", [128, 4096], BF16))
        eps_col = stack.enter_context(nc.sbuf_tensor("eps_col", [128, 1], F32))

        phase_p0(nc, S, ext, eye_bf, eye_f32, ones_col, ones_row, bqkv_sb, bq32, bv_bc, eps_col)
        prep_out = phase_prep(nc, S, ext, scr, eye_bf, eye_f32, ones_row, scal)

        with (
            nc.sbuf_tensor("QT", [128, DC * 1024], BF16) as QT,
            nc.sbuf_tensor("oaT", [128, DC * 1024], BF16) as oaT,
        ):
            phase_a1k(nc, S, ext, scr, eye_bf, bqkv_sb, trash, eps_col)
            phase_a1v(nc, S, ext, scr, bv_bc)
            phase_a1q(nc, S, ext, scr, eye_bf, bqkv_sb, trash, QT, eps_col)
            phase_a2(nc, S, scr, eye_bf, trash, QT, oaT)
            a3out = phase_a3(nc, S, ext, scr, ones_row, trash, oaT, eps_col, eye_bf)

        f1out = phase_f1(nc, S, ext, scr, scal, a3out, prep_out)
        phase_f2(nc, S, ext, scr, trash, eps_col, f1out, prep_out, a3out)

    return nc


def phase_p0(nc, S, ext, eye_bf, eye_f32, ones_col, ones_row, bqkv_sb, bq32, bv_bc, eps_col):
    with (
        nc.sbuf_tensor("p0stg", [128, 128], F32) as stg,
        nc.sbuf_tensor("bvrow", [1, 1024], F32) as bvrow,
        nc.psum_tensor("p0ps", [128, 512], F32) as ps,
    ):
        ph = Ph(S)
        ld1 = ph.add("sync", lambda e: e.dma_start(stg[:, :], ext["eye"]), inc="sLd")
        ph.add("vector", lambda e: e.tensor_copy(eye_bf[:, :], stg[:, :]),
               waits=[("sLd", ld1)], inc="sV")
        ph.add("vector", lambda e: e.tensor_copy(eye_f32[:, :], stg[:, :]),
               waits=[("sLd", ld1)], inc="sV")
        ld2 = ph.add("sync", lambda e: e.dma_start(
            bqkv_sb[:, :], ext["bqkv"].rearrange("(i p) -> p i", p=128),
            allow_slow_non_contiguous=True), inc="sLd")
        ph.add("scalar", lambda e: e.mul(bq32[:, :], bqkv_sb[:, 0:8], QSCALE),
               waits=[("sLd", ld2)], inc="sS")
        ld3 = ph.add("sync", lambda e: e.dma_start(
            bvrow[:, :], ext["bqkv"][2048:3072].rearrange("(o n) -> o n", o=1)), inc="sLd")
        ph.add("vector", lambda e: e.memset(ones_col[:, :], 1.0), inc="sV")
        ph.add("vector", lambda e: e.memset(eps_col[:, :], EPS), inc="sV")
        vo = ph.add("vector", lambda e: e.memset(ones_row[:, :], 1.0), inc="sV")
        for h in range(2):
            t = ph.add("tensor", lambda e, h=h: e.matmul(
                ps[:, h * 512:(h + 1) * 512] if False else ps[:, 0:512],
                ones_row[:, :], bvrow[:, h * 512:(h + 1) * 512], start=True, stop=True),
                waits=[("sLd", ld3), ("sV", vo), ("sS", S.cnt["sS"])], inc="sT")
            ph.add("scalar", lambda e, h=h: e.copy(bv_bc[:, h * 512:(h + 1) * 512], ps[:, 0:512]),
                   waits=[("sT", t)], inc="sS")
        S.emit(ph)


def emit_ternary(ph, src_ap, rows, width, absb, sgnb, ternb, thr_ap, wait_ld,
                 wait_tern=()):
    sA = ph.add("scalar", lambda e, s=src_ap, r=rows, w=width:
                e.activation(absb[0:r, 0:w], s, AF.Abs), waits=wait_ld, inc="sS")
    sG = ph.add("scalar", lambda e, s=src_ap, r=rows, w=width:
                e.sign(sgnb[0:r, 0:w], s), waits=wait_ld, inc="sS")
    vM = ph.add("vector", lambda e, r=rows, w=width, thr=thr_ap:
                e.tensor_single_scalar(absb[0:r, 0:w], absb[0:r, 0:w], thr[0:r, :],
                                       ALU.is_gt),
                waits=[("sS", sA)], inc="sV")
    vT = ph.add("vector", lambda e, r=rows, w=width:
                e.tensor_mul(ternb[0:r, 0:w], absb[0:r, 0:w], sgnb[0:r, 0:w]),
                waits=[("sS", sG), ("sV", vM)] + list(wait_tern), inc="sV")
    return vT, sG


def phase_prep(nc, S, ext, scr, eye_bf, eye_f32, ones_row, scal):
    NPER = LFFN // 128
    chunks = [(i * 4096, min(NPER, (i + 1) * 4096)) for i in range((NPER + 4095) // 4096)]
    GU_BATCH = [[0, 128, 256, 384], [512, 640, 768, 896], [1024, 1152], [1238]]

    with (
        nc.sbuf_tensor("pp_scstg", [128, 2 * 4096], F32) as stg,
        nc.sbuf_tensor("pp_part", [128, 16], F32) as part,
        nc.sbuf_tensor("pp_acc", [128, 4], F32) as acc,
        nc.sbuf_tensor("pp_tot", [1, 4], F32) as tot,
        nc.sbuf_tensor("pp_win", [128, 2 * 4 * 512], F32) as win,
        nc.sbuf_tensor("pp_abs", [128, 4 * 512], F32) as absb,
        nc.sbuf_tensor("pp_sgn", [128, 4 * 512], F32) as sgnb,
        nc.sbuf_tensor("pp_tern", [128, 2 * 4 * 512], BF16) as tern,
        nc.sbuf_tensor("pp_out", [128, 2 * 512], BF16) as outb,
        nc.sbuf_tensor("pp_dwin", [128, 2 * 4 * H8], F32) as dwin,
        nc.sbuf_tensor("pp_dabs", [128, 4 * H8], F32) as dabsb,
        nc.sbuf_tensor("pp_dsgn", [128, 4 * H8], F32) as dsgnb,
        nc.sbuf_tensor("pp_dtern", [128, 2 * 4 * H8], BF16) as dtern,
        nc.sbuf_tensor("pp_dout", [128, 2 * 1408], BF16) as doutb,
        nc.psum_tensor("pp_ps", [128, 128], F32) as ps,
        nc.psum_tensor("pp_ps2", [128, 4], F32) as ps2,
        nc.psum_tensor("pp_tp0", [128, 512], BF16) as tp0,
        nc.psum_tensor("pp_tp1", [128, 512], BF16) as tp1,
    ):
        tps = [tp0, tp1]
        ph = Ph(S)
        st = {"lastred": {0: 0, 1: 0}, "psrd": 0}

        def add_scale_chunk(mi, wname, ci):
            c0, c1 = chunks[ci]
            b = ci % 2
            flat = ext[wname].rearrange("a b c -> (a b c)").rearrange("(p n) -> p n", p=128)
            ld = ph.add("sync", lambda e, b=b, c0=c0, c1=c1, flat=flat:
                        e.dma_start(stg[:, b * 4096:b * 4096 + (c1 - c0)], flat[:, c0:c1]),
                        waits=[("sV", st["lastred"][b])], inc="sLd")
            st["lastred"][b] = ph.add("vector", lambda e, b=b, w=c1 - c0, ci=ci:
                                      e.reduce_sum(part[:, ci:ci + 1],
                                                   stg[:, b * 4096:b * 4096 + w],
                                                   AX.X, apply_absolute_value=True),
                                      waits=[("sLd", ld)], inc="sV")

        def add_scale_final(mi, col_sc2, col_sc):
            vacc = ph.add("vector", lambda e, mi=mi, n=len(chunks):
                          e.reduce_sum(acc[:, mi:mi + 1], part[:, 0:n], AX.X),
                          waits=[("sV", st["lastred"][0]), ("sV", st["lastred"][1])], inc="sV")
            t1 = ph.add("tensor", lambda e, mi=mi:
                        e.transpose(ps[0:1, 0:128], acc[:, mi:mi + 1], eye_f32[:, :]),
                        waits=[("sV", vacc), ("sV", st["psrd"])], inc="sT")
            vt = ph.add("vector", lambda e, mi=mi:
                        e.reduce_sum(tot[0:1, mi:mi + 1], ps[0:1, 0:128], AX.X),
                        waits=[("sT", t1)], inc="sV")
            st["psrd"] = vt
            t2 = ph.add("tensor", lambda e, mi=mi:
                        e.matmul(ps2[:, mi:mi + 1], ones_row[:, :], tot[0:1, mi:mi + 1],
                                 start=True, stop=True),
                        waits=[("sV", vt)], inc="sT")
            sc = ph.add("scalar", lambda e, mi=mi, col=col_sc2:
                        e.mul(scal[:, col:col + 1], ps2[:, mi:mi + 1], 0.5 / LFFN),
                        waits=[("sT", t2)], inc="sS")
            if col_sc is not None:
                sc = ph.add("scalar", lambda e, mi=mi, col=col_sc:
                            e.mul(scal[:, col:col + 1], ps2[:, mi:mi + 1], 1.0 / LFFN),
                            waits=[("sT", t2)], inc="sS")
            return sc

        gu = {"stg_rd": {0: 0, 1: 0}, "prev_vT": 0, "tern_rd": {0: 0, 1: 0},
              "wst": {0: 0, 1: 0}, "ev_prev": {0: 0, 1: 0}, "bi": 0}

        def add_gu_batch(mi, wname, dst, thrcol, j, rows, scale_ready):
            bb = gu["bi"] % 2
            gu["bi"] += 1
            nt = len(rows)
            wid = nt * 512
            if nt == 4:
                ld = ph.add("sync", lambda e, bb=bb, w=wname, j=j, o0=rows[0]:
                            e.dma_start(win[:, bb * 2048:bb * 2048 + 2048]
                                        .rearrange("p (t d) -> p t d", t=4),
                                        ext[w][j, o0:o0 + 512, :]
                                        .rearrange("(t p) d -> p t d", p=128)),
                            waits=[("sS", gu["stg_rd"][bb])], inc="sLd")
            else:
                ld = None
                for ti, o0 in enumerate(rows):
                    ld = ph.add("sync", lambda e, bb=bb, w=wname, j=j, o0=o0, ti=ti:
                                e.dma_start(win[:, bb * 2048 + ti * 512:bb * 2048 + (ti + 1) * 512],
                                            ext[w][j, o0:o0 + 128, :]),
                                waits=[("sS", gu["stg_rd"][bb])] if ti == 0 else [], inc="sLd")
            vT, sG = emit_ternary(ph, win[:, bb * 2048:bb * 2048 + wid], 128, wid,
                                  absb, sgnb, tern[:, bb * 2048:bb * 2048 + wid],
                                  scal[:, thrcol:thrcol + 1],
                                  [("sLd", ld), ("sV", gu["prev_vT"]), ("sS", scale_ready)],
                                  [("sT", gu["tern_rd"][bb])])
            gu["prev_vT"] = vT
            gu["stg_rd"][bb] = sG
            tlast = None
            for ti, o0 in enumerate(rows):
                par = (gu["bi"] + ti) % 2
                tcnt = None
                for dc in range(4):
                    tcnt = ph.add("tensor", lambda e, par=par, bb=bb, ti=ti, dc=dc:
                                  e.transpose(tps[par][:, dc * 128:(dc + 1) * 128],
                                              tern[:, bb * 2048 + ti * 512 + dc * 128:
                                                   bb * 2048 + ti * 512 + (dc + 1) * 128],
                                              eye_bf[:, :]),
                                  waits=([("sV", vT), ("sS", gu["ev_prev"][par])]
                                         if dc == 0 else []),
                                  inc="sT" if dc == 3 else None)
                tlast = tcnt
                ob = ti % 2
                ev = ph.add("scalar", lambda e, par=par, ob=ob: e.copy(
                    outb[:, ob * 512:(ob + 1) * 512], tps[par][:, :]),
                    waits=[("sT", tcnt), ("sSt", gu["wst"][ob])], inc="sS")
                gu["ev_prev"][par] = ev
                gu["wst"][ob] = ph.add("gpsimd", lambda e, ob=ob, j=j, o0=o0, dst=dst:
                                       e.dma_start(
                                           dst.rearrange("(c p) o -> p c o", p=128)
                                           [:, j::8, o0:o0 + 128],
                                           outb[:, ob * 512:(ob + 1) * 512]
                                           .rearrange("p (c o) -> p c o", c=4)),
                                       waits=[("sS", ev)], inc="sSt")
            gu["tern_rd"][bb] = tlast

        dd = {"stg_rd": {0: 0, 1: 0}, "prev_vT": 0, "tern_rd": {0: 0, 1: 0},
              "dst": {0: 0, 1: 0}, "ev_prev": {0: 0, 1: 0, 2: 0}, "mzo": 0}

        def add_d_batch(j, scale_ready):
            bb = j % 2
            ld = ph.add("sync", lambda e, bb=bb, j=j:
                        e.dma_start(dwin[:, bb * 4 * H8:(bb + 1) * 4 * H8]
                                    .rearrange("p (t d) -> p t d", t=4),
                                    ext["wd"][j].rearrange("(t p) d -> p t d", p=128)),
                        waits=[("sS", dd["stg_rd"][bb])], inc="sLd")
            vT, sG = emit_ternary(ph, dwin[:, bb * 4 * H8:(bb + 1) * 4 * H8], 128, 4 * H8,
                                  dabsb, dsgnb, dtern[:, bb * 4 * H8:(bb + 1) * 4 * H8],
                                  scal[:, 2:3],
                                  [("sLd", ld), ("sV", dd["prev_vT"]),
                                   ("sS", scale_ready), ("sV", dd["mzo"])],
                                  [("sT", dd["tern_rd"][bb])])
            dd["prev_vT"] = vT
            dd["stg_rd"][bb] = sG
            tlast = None
            for oc in range(4):
                ob = oc % 2
                evs = []
                for pf in range(3):
                    ndc = 4 if pf < 2 else 3
                    par = pf % 2
                    tcnt = None
                    for q in range(ndc):
                        dc = pf * 4 + q
                        w = WD_W[dc]
                        tcnt = ph.add("tensor", lambda e, par=par, bb=bb, oc=oc, q=q, dc=dc, w=w:
                                      e.transpose(tps[par][0:w, q * 128:q * 128 + 128],
                                                  dtern[:, bb * 4 * H8 + oc * H8 + dc * 128:
                                                       bb * 4 * H8 + oc * H8 + dc * 128 + w],
                                                  eye_bf[:, :]),
                                      waits=([("sV", vT), ("sS", dd["ev_prev"][par])]
                                             if q == 0 else []),
                                      inc="sT" if q == ndc - 1 else None)
                    ev = None
                    for q in range(ndc):
                        dc = pf * 4 + q
                        w = WD_W[dc]
                        ev = ph.add("scalar", lambda e, ob=ob, dc=dc, q=q, w=w, pf=pf,
                                    par=par:
                                    e.copy(doutb[0:w, ob * 1408 + dc * 128:
                                                 ob * 1408 + (dc + 1) * 128],
                                           tps[par][0:w, q * 128:(q + 1) * 128]),
                                    waits=([("sT", tcnt), ("sV", dd["mzo"]),
                                            ("sSt", dd["dst"][ob])] if q == 0 else []),
                                    inc="sS")
                    dd["ev_prev"][par] = ev
                    evs.append(ev)
                    tlast = tcnt
                dd["dst"][ob] = ph.add("gpsimd", lambda e, ob=ob, j=j, oc=oc:
                                       e.dma_start(
                                           scr["wdtS"].rearrange("(c p) o -> p c o", p=128)
                                           [:, j * 11:(j + 1) * 11, oc * 128:(oc + 1) * 128],
                                           doutb[:, ob * 1408:(ob + 1) * 1408]
                                           .rearrange("p (c o) -> p c o", c=11)),
                                       waits=[("sS", evs[-1])], inc="sSt")
            dd["tern_rd"][bb] = tlast

        out = {}
        dd["mzo"] = ph.add("vector", lambda e: e.memset(doutb[:, :], 0.0), inc="sV")
        for ci in range(len(chunks)):
            add_scale_chunk(0, "wg", ci)
        sc_g = add_scale_final(0, 0, 3)
        gwork = [(j, rows) for j in range(8) for rows in GU_BATCH]
        for i, (j, rows) in enumerate(gwork):
            if i < len(chunks):
                add_scale_chunk(1, "wu", i)
            add_gu_batch(0, "wg", scr["wgtS"], 0, j, rows, sc_g)
        out["wg_done"] = S.cnt["sSt"]
        sc_u = add_scale_final(1, 1, 4)
        uwork = gwork
        for i, (j, rows) in enumerate(uwork):
            if i < len(chunks):
                add_scale_chunk(2, "wd", i)
            add_gu_batch(1, "wu", scr["wutS"], 1, j, rows, sc_u)
        out["wu_done"] = S.cnt["sSt"]
        sc_d = add_scale_final(2, 2, None)
        for j in range(8):
            add_d_batch(j, sc_d)
        out["wd_done"] = S.cnt["sSt"]
        ph.add("gpsimd", lambda e: e.engine_nop(), waits=[("sSt", S.cnt["sSt"])])
        S.emit(ph)
        return out


def phase_a1k(nc, S, ext, scr, eye_bf, bqkv_sb, trash, eps_col):
    with (
        nc.sbuf_tensor("wk", [128, FC * 1024], BF16) as wk,
        nc.sbuf_tensor("a1_wstg", [128, 2 * 1024], F32) as wstg,
        nc.sbuf_tensor("a1_x", [128, 2 * 4096], F32) as xb,
        nc.sbuf_tensor("a1_lnx", [128, 2 * 4096], BF16) as lnx,
        nc.sbuf_tensor("a1_lnxt", [128, FC * 512], BF16) as lnxt,
        nc.sbuf_tensor("a1_kt", [128, 2 * 512], BF16) as ktb,
        nc.sbuf_tensor("a1_st", [128, 16], F32) as stats,
        nc.psum_tensor("a1_kp0", [128, 512], F32) as kp0,
        nc.psum_tensor("a1_kp1", [128, 512], F32) as kp1,
        nc.psum_tensor("a1_tp0", [128, 512], BF16) as tp0,
        nc.psum_tensor("a1_tp1", [128, 512], BF16) as tp1,
    ):
        kps = [kp0, kp1]
        tps = [tp0, tp1]
        ph = Ph(S)
        cast = {0: 0, 1: 0}
        for fc in range(FC):
            b = fc % 2
            ld = ph.add("sync", lambda e, b=b, fc=fc: e.dma_start(
                wstg[:, b * 1024:(b + 1) * 1024],
                ext["wqkv"][fc * 128:(fc + 1) * 128, 1024:2048]),
                waits=[("sV", cast[b])], inc="sLd")
            cast[b] = ph.add("vector", lambda e, b=b, fc=fc: e.tensor_copy(
                wk[:, fc * 1024:(fc + 1) * 1024], wstg[:, b * 1024:(b + 1) * 1024]),
                waits=[("sLd", ld)], inc="sV")
        xread_v = {0: 0, 1: 0}
        xread_s = {0: 0, 1: 0}
        lnx_rd_t = {0: 0, 1: 0}
        tp_ev = {0: 0, 1: 0}
        kev = {0: 0, 1: 0}
        kst = {0: 0, 1: 0}
        lnxt_store = 0
        kmm_done = 0
        for g in range(NG):
            ev_last = 0
            for t in range(4):
                tile = g * 4 + t
                b = tile % 2
                ld = ph.add("sync", lambda e, b=b, tile=tile: e.dma_start(
                    xb[:, b * 4096:(b + 1) * 4096],
                    ext["xseq"][tile * 128:(tile + 1) * 128, :]),
                    waits=[("sV", xread_v[b]), ("sS", xread_s[b])], inc="sLd")
                st = stats[:, (tile % 2) * 8:(tile % 2) * 8 + 8]
                vdone, s1 = emit_ln(ph, xb[:, b * 4096:(b + 1) * 4096],
                                    lnx[:, b * 4096:(b + 1) * 4096], st, trash[:, :],
                                    [("sLd", ld), ("sT", lnx_rd_t[b])], eps_ap=eps_col[:, :])
                xread_v[b], xread_s[b] = vdone, s1
                for pf in range(8):
                    par = pf % 2
                    tcnt = None
                    for q in range(4):
                        fc = pf * 4 + q
                        tcnt = ph.add("tensor",
                                      lambda e, par=par, q=q, fc=fc, b=b:
                                      e.transpose(tps[par][:, q * 128:(q + 1) * 128],
                                                  lnx[:, b * 4096 + fc * 128:b * 4096 + (fc + 1) * 128],
                                                  eye_bf[:, :]),
                                      waits=([("sV", vdone), ("sS", tp_ev[par])] if q == 0 else []),
                                      inc="sT" if q == 3 else None)
                    ev = ph.add("scalar",
                                lambda e, par=par, pf=pf, t=t:
                                e.copy(lnxt.rearrange("p (f q) -> p f q", q=512)
                                       [:, pf * 4:(pf + 1) * 4, t * 128:(t + 1) * 128],
                                       tps[par][:, :].rearrange("p (f q) -> p f q", q=128)),
                                waits=[("sT", tcnt), ("sT", kmm_done), ("sSt", lnxt_store)],
                                inc="sS")
                    tp_ev[par] = ev
                    ev_last = ev
                lnx_rd_t[b] = S.cnt["sT"]
            for dc in range(DC):
                par = dc % 2
                mm = None
                for fc in range(FC):
                    mm = ph.add("tensor", lambda e, par=par, dc=dc, fc=fc:
                                e.matmul(kps[par][:, :],
                                         wk[:, fc * 1024 + dc * 128:fc * 1024 + (dc + 1) * 128],
                                         lnxt[:, fc * 512:(fc + 1) * 512],
                                         start=(fc == 0), stop=(fc == FC - 1)),
                                waits=([("sS", ev_last), ("sV", kev[par]),
                                        ("sV", cast[0]), ("sV", cast[1])] if fc == 0 else []),
                                inc="sT" if fc == FC - 1 else None)
                kev[par] = ph.add("vector", lambda e, par=par, dc=dc:
                                  e.tensor_scalar(ktb[:, par * 512:(par + 1) * 512],
                                                  kps[par][:, :], bqkv_sb[:, 8 + dc:9 + dc], None,
                                                  ALU.add),
                                  waits=[("sT", mm), ("sSt", kst[par])], inc="sV")
                kst[par] = ph.add("gpsimd", lambda e, par=par, dc=dc, g=g:
                                  e.dma_start(scr["ktS"][dc * 128:(dc + 1) * 128,
                                                         g * 512:(g + 1) * 512],
                                              ktb[:, par * 512:(par + 1) * 512]),
                                  waits=[("sV", kev[par])], inc="sSt")
            kmm_done = S.cnt["sT"]
            lnxt_store = ph.add("gpsimd", lambda e, g=g:
                                e.dma_start(scr["lnxtS"][g, :, :], lnxt[:, :]),
                                waits=[("sS", ev_last)], inc="sSt")
        ph.add("gpsimd", lambda e: e.engine_nop(), waits=[("sSt", S.cnt["sSt"])])
        S.emit(ph)


def phase_a1v(nc, S, ext, scr, bv_bc):
    with (
        nc.sbuf_tensor("wv", [128, FC * 1024], BF16) as wv,
        nc.sbuf_tensor("v_wstg", [128, 2 * 1024], F32) as wstg,
        nc.sbuf_tensor("v_lnxt", [128, 2 * FC * 512], BF16) as lnxt,
        nc.sbuf_tensor("v_vt", [128, 2 * 512], BF16) as vtb,
        nc.psum_tensor("v_p0", [128, 512], F32) as p0,
        nc.psum_tensor("v_p1", [128, 512], F32) as p1,
    ):
        vps = [p0, p1]
        ph = Ph(S)
        cast = {0: 0, 1: 0}
        for fc in range(FC):
            b = fc % 2
            ld = ph.add("sync", lambda e, b=b, fc=fc: e.dma_start(
                wstg[:, b * 1024:(b + 1) * 1024],
                ext["wqkv"][fc * 128:(fc + 1) * 128, 2048:3072]),
                waits=[("sV", cast[b])], inc="sLd")
            cast[b] = ph.add("vector", lambda e, b=b, fc=fc: e.tensor_copy(
                wv[:, fc * 1024:(fc + 1) * 1024], wstg[:, b * 1024:(b + 1) * 1024]),
                waits=[("sLd", ld)], inc="sV")
        gdone = {0: 0, 1: 0}
        vev = {0: 0, 1: 0}
        vst = {0: 0, 1: 0}
        for g in range(NG):
            gb = g % 2
            ld = ph.add("sync", lambda e, gb=gb, g=g: e.dma_start(
                lnxt[:, gb * FC * 512:(gb + 1) * FC * 512], scr["lnxtS"][g, :, :]),
                waits=[("sT", gdone[gb])], inc="sLd")
            for t in range(4):
                for half in range(2):
                    par = (t * 2 + half) % 2
                    mm = None
                    for fc in range(FC):
                        mm = ph.add("tensor", lambda e, par=par, gb=gb, t=t, half=half, fc=fc:
                                    e.matmul(vps[par][:, :],
                                             lnxt[:, gb * FC * 512 + fc * 512 + t * 128:
                                                  gb * FC * 512 + fc * 512 + (t + 1) * 128],
                                             wv[:, fc * 1024 + half * 512:fc * 1024 + (half + 1) * 512],
                                             start=(fc == 0), stop=(fc == FC - 1)),
                                    waits=([("sLd", ld), ("sV", vev[par]),
                                            ("sV", cast[0]), ("sV", cast[1])] if fc == 0 else []),
                                    inc="sT" if fc == FC - 1 else None)
                    vev[par] = ph.add("vector", lambda e, par=par, half=half:
                                      e.tensor_add(vtb[:, par * 512:(par + 1) * 512],
                                                   vps[par][:, :],
                                                   bv_bc[:, half * 512:(half + 1) * 512]),
                                      waits=[("sT", mm), ("sSt", vst[par])], inc="sV")
                    vst[par] = ph.add("gpsimd", lambda e, par=par, g=g, t=t, half=half:
                                      e.dma_start(scr["vS"][(g * 4 + t) * 128:(g * 4 + t + 1) * 128,
                                                            half * 512:(half + 1) * 512],
                                                  vtb[:, par * 512:(par + 1) * 512]),
                                      waits=[("sV", vev[par])], inc="sSt")
            gdone[gb] = S.cnt["sT"]
        ph.add("gpsimd", lambda e: e.engine_nop(), waits=[("sSt", S.cnt["sSt"])])
        S.emit(ph)


def phase_a1q(nc, S, ext, scr, eye_bf, bqkv_sb, trash, QT, eps_col):
    with (
        nc.sbuf_tensor("wq", [128, FC * 1024], BF16) as wq,
        nc.sbuf_tensor("q_wstg", [128, 2 * 1024], F32) as wstg,
        nc.sbuf_tensor("q_x", [128, 2 * 4096], F32) as xb,
        nc.sbuf_tensor("q_lnx", [128, 2 * 4096], BF16) as lnx,
        nc.sbuf_tensor("q_lnxt", [128, FC * 512], BF16) as lnxt,
        nc.sbuf_tensor("q_st", [128, 16], F32) as stats,
        nc.psum_tensor("q_p0", [128, 512], F32) as p0,
        nc.psum_tensor("q_p1", [128, 512], F32) as p1,
        nc.psum_tensor("q_tp0", [128, 512], BF16) as tp0,
        nc.psum_tensor("q_tp1", [128, 512], BF16) as tp1,
    ):
        qps = [p0, p1]
        tps = [tp0, tp1]
        ph = Ph(S)
        cast = {0: 0, 1: 0}
        for fc in range(FC):
            b = fc % 2
            ld = ph.add("sync", lambda e, b=b, fc=fc: e.dma_start(
                wstg[:, b * 1024:(b + 1) * 1024],
                ext["wqkv"][fc * 128:(fc + 1) * 128, 0:1024]),
                waits=[("sV", cast[b])], inc="sLd")
            cast[b] = ph.add("vector", lambda e, b=b, fc=fc: e.tensor_copy(
                wq[:, fc * 1024:(fc + 1) * 1024], wstg[:, b * 1024:(b + 1) * 1024]),
                waits=[("sLd", ld)], inc="sV")
        xread_v = {0: 0, 1: 0}
        xread_s = {0: 0, 1: 0}
        lnx_rd_t = {0: 0, 1: 0}
        tp_ev = {0: 0, 1: 0}
        qev = {0: 0, 1: 0}
        qmm_done = 0
        for g in range(OG):
            ev_last = 0
            for t in range(4):
                tile = g * 4 + t
                b = tile % 2
                ld = ph.add("sync", lambda e, b=b, tile=tile: e.dma_start(
                    xb[:, b * 4096:(b + 1) * 4096],
                    ext["xown"][tile * 128:(tile + 1) * 128, :]),
                    waits=[("sV", xread_v[b]), ("sS", xread_s[b])], inc="sLd")
                st = stats[:, (tile % 2) * 8:(tile % 2) * 8 + 8]
                vdone, s1 = emit_ln(ph, xb[:, b * 4096:(b + 1) * 4096],
                                    lnx[:, b * 4096:(b + 1) * 4096], st, trash[:, :],
                                    [("sLd", ld), ("sT", lnx_rd_t[b])], eps_ap=eps_col[:, :])
                xread_v[b], xread_s[b] = vdone, s1
                for pf in range(8):
                    par = pf % 2
                    tcnt = None
                    for q in range(4):
                        fc = pf * 4 + q
                        tcnt = ph.add("tensor",
                                      lambda e, par=par, q=q, fc=fc, b=b:
                                      e.transpose(tps[par][:, q * 128:(q + 1) * 128],
                                                  lnx[:, b * 4096 + fc * 128:b * 4096 + (fc + 1) * 128],
                                                  eye_bf[:, :]),
                                      waits=([("sV", vdone), ("sS", tp_ev[par])] if q == 0 else []),
                                      inc="sT" if q == 3 else None)
                    ev = ph.add("scalar",
                                lambda e, par=par, pf=pf, t=t:
                                e.copy(lnxt.rearrange("p (f q) -> p f q", q=512)
                                       [:, pf * 4:(pf + 1) * 4, t * 128:(t + 1) * 128],
                                       tps[par][:, :].rearrange("p (f q) -> p f q", q=128)),
                                waits=[("sT", tcnt), ("sT", qmm_done)], inc="sS")
                    tp_ev[par] = ev
                    ev_last = ev
                lnx_rd_t[b] = S.cnt["sT"]
            for dc in range(DC):
                par = dc % 2
                mm = None
                for fc in range(FC):
                    mm = ph.add("tensor", lambda e, par=par, dc=dc, fc=fc:
                                e.matmul(qps[par][:, :],
                                         wq[:, fc * 1024 + dc * 128:fc * 1024 + (dc + 1) * 128],
                                         lnxt[:, fc * 512:(fc + 1) * 512],
                                         start=(fc == 0), stop=(fc == FC - 1)),
                                waits=([("sS", ev_last), ("sV", qev[par]),
                                        ("sV", cast[0]), ("sV", cast[1])] if fc == 0 else []),
                                inc="sT" if fc == FC - 1 else None)
                qev[par] = ph.add("vector", lambda e, par=par, dc=dc, g=g:
                                  e.tensor_scalar(QT[:, dc * 1024 + g * 512:dc * 1024 + (g + 1) * 512],
                                                  qps[par][:, :], bqkv_sb[:, dc:dc + 1], QSCALE,
                                                  ALU.add, ALU.mult),
                                  waits=[("sT", mm)], inc="sV")
            qmm_done = S.cnt["sT"]
        S.emit(ph)


def phase_a2(nc, S, scr, eye_bf, trash, QT, oaT):
    with (
        nc.sbuf_tensor("KT", [128, DC * SEQ], BF16) as KT,
        nc.sbuf_tensor("Vb", [128, KCH * 1024], BF16) as Vb,
        nc.sbuf_tensor("scoresF", [128, SEQ], F32) as scoresF,
        nc.sbuf_tensor("Pbf", [128, SEQ], BF16) as Pbf,
        nc.sbuf_tensor("PT", [128, SEQ], BF16) as PT,
        nc.sbuf_tensor("oa", [128, 1024], BF16) as oa,
        nc.sbuf_tensor("a2_st", [128, 8], F32) as stats,
        nc.psum_tensor("a2_s0", [128, 512], F32) as s0,
        nc.psum_tensor("a2_s1", [128, 512], F32) as s1,
        nc.psum_tensor("a2_tp0", [128, 512], BF16) as tp0,
        nc.psum_tensor("a2_tp1", [128, 512], BF16) as tp1,
        nc.psum_tensor("a2_pv0", [128, 512], F32) as pv0,
        nc.psum_tensor("a2_pv1", [128, 512], F32) as pv1,
    ):
        sps = [s0, s1]
        tps = [tp0, tp1]
        pvs = [pv0, pv1]
        ph = Ph(S)
        ldk = ph.add("sync", lambda e: e.dma_start(
            KT.rearrange("p (d k) -> p d k", k=SEQ),
            scr["ktS"].rearrange("(d p) k -> p d k", p=128)), inc="sLd")
        ldv = ph.add("sync", lambda e: e.dma_start(
            Vb.rearrange("p (k d) -> p k d", d=1024),
            scr["vS"].rearrange("(k p) d -> p k d", p=128)), inc="sLd")
        sev = {0: 0, 1: 0}
        exp_prev = 0
        pt_tr_prev = 0
        pv_prev = 0
        oa_tr_prev = 0
        ptev = {0: 0, 1: 0}
        for qt in range(NQT):
            stp = stats[:, (qt % 2) * 4:(qt % 2) * 4 + 4]
            ev8 = 0
            for kg in range(8):
                par = kg % 2
                mm = None
                for dc in range(DC):
                    mm = ph.add("tensor", lambda e, par=par, qt=qt, kg=kg, dc=dc:
                                e.matmul(sps[par][:, :],
                                         QT[:, dc * 1024 + qt * 128:dc * 1024 + qt * 128 + 128],
                                         KT[:, dc * SEQ + kg * 512:dc * SEQ + (kg + 1) * 512],
                                         start=(dc == 0), stop=(dc == DC - 1)),
                                waits=([("sLd", ldk), ("sV", sev[par]),
                                        ("sS", exp_prev)] if dc == 0 else []),
                                inc="sT" if dc == DC - 1 else None)
                sev[par] = ph.add("vector", lambda e, par=par, kg=kg:
                                  e.tensor_copy(scoresF[:, kg * 512:(kg + 1) * 512], sps[par][:, :]),
                                  waits=[("sT", mm)], inc="sV")
                ev8 = sev[par]
            vmax = ph.add("vector", lambda e, stp=stp:
                          e.reduce_max(stp[:, 0:1], scoresF[:, :], AX.X, negate=True),
                          waits=[("sV", ev8)], inc="sV")
            exp_prev = ph.add("scalar", lambda e, stp=stp:
                              e.activation(Pbf[:, :], scoresF[:, :], AF.Exp,
                                           bias=stp[:, 0:1], scale=1.0,
                                           accum_out=stp[:, 1:2]),
                              waits=[("sV", vmax), ("sT", pt_tr_prev)], inc="sS")
            vrec = ph.add("vector", lambda e, stp=stp:
                          e.reciprocal(stp[:, 2:3], stp[:, 1:2]),
                          waits=[("sS", exp_prev)], inc="sV")
            ptev_last = 0
            for pf in range(8):
                par = pf % 2
                tcnt = None
                for q in range(4):
                    kc = pf * 4 + q
                    tcnt = ph.add("tensor", lambda e, par=par, q=q, kc=kc:
                                  e.transpose(tps[par][:, q * 128:(q + 1) * 128],
                                              Pbf[:, kc * 128:(kc + 1) * 128], eye_bf[:, :]),
                                  waits=([("sS", exp_prev), ("sS", ptev[par])] if q == 0 else []),
                                  inc="sT" if q == 3 else None)
                ptev[par] = ph.add("scalar", lambda e, par=par, pf=pf:
                                   e.copy(PT[:, pf * 512:(pf + 1) * 512], tps[par][:, :]),
                                   waits=[("sT", tcnt), ("sT", pv_prev)], inc="sS")
                ptev_last = ptev[par]
            oaev = 0
            for half in range(2):
                mm = None
                for kc in range(KCH):
                    mm = ph.add("tensor", lambda e, half=half, kc=kc:
                                e.matmul(pvs[half][:, :],
                                         PT[:, kc * 128:(kc + 1) * 128],
                                         Vb[:, kc * 1024 + half * 512:kc * 1024 + (half + 1) * 512],
                                         start=(kc == 0), stop=(kc == KCH - 1)),
                                waits=([("sLd", ldv), ("sS", ptev_last),
                                        ("sV", oa_tr_prev if False else 0),
                                        ("sT", oa_tr_prev)] if kc == 0 else []),
                                inc="sT" if kc == KCH - 1 else None)
                oaev = ph.add("vector", lambda e, half=half, stp=stp:
                              e.tensor_scalar_mul(oa[:, half * 512:(half + 1) * 512],
                                                  pvs[half][:, :], stp[:, 2:3]),
                              waits=[("sT", mm), ("sV", vrec)], inc="sV")
            pv_prev = S.cnt["sT"]
            for pf in range(2):
                par = pf % 2
                tcnt = None
                for q in range(4):
                    dvc = pf * 4 + q
                    tcnt = ph.add("tensor", lambda e, par=par, q=q, dvc=dvc:
                                  e.transpose(tps[par][:, q * 128:(q + 1) * 128],
                                              oa[:, dvc * 128:(dvc + 1) * 128], eye_bf[:, :]),
                                  waits=([("sV", oaev), ("sS", ptev[par])] if q == 0 else []),
                                  inc="sT" if q == 3 else None)
                ptev[par] = ph.add("scalar", lambda e, par=par, pf=pf, qt=qt:
                                   e.copy(oaT.rearrange("p (d q) -> p d q", q=1024)
                                          [:, pf * 4:(pf + 1) * 4, qt * 128:(qt + 1) * 128],
                                          tps[par][:, :].rearrange("p (d q) -> p d q", q=128)),
                                   waits=[("sT", tcnt)], inc="sS")
            oa_tr_prev = S.cnt["sT"]
            pt_tr_prev = oa_tr_prev
        S.emit(ph)


def phase_a3(nc, S, ext, scr, ones_row, trash, oaT, eps_col, eye_bf):
    out = {}
    with (
        nc.sbuf_tensor("wo_sb", [128, DC * 4096], BF16) as wo_sb,
        nc.sbuf_tensor("a3_wstg", [128, 2048], F32) as wstg,
        nc.sbuf_tensor("bo_row", [1, 4096], F32) as bo_row,
        nc.sbuf_tensor("a3_attn", [128, 2 * 4096], F32) as attn,
        nc.sbuf_tensor("a3_x", [128, 4096], F32) as xb,
        nc.sbuf_tensor("a3_lnx2", [128, 4096], BF16) as lnx2,
        nc.sbuf_tensor("a3_lt", [128, FC * 128], BF16) as ltb,
        nc.sbuf_tensor("a3_st", [128, 24], F32) as stats,
        nc.psum_tensor("a3_p0", [128, 512], F32) as p0,
        nc.psum_tensor("a3_p1", [128, 512], F32) as p1,
        nc.psum_tensor("a3_tp0", [128, 512], BF16) as tp0,
        nc.psum_tensor("a3_tp1", [128, 512], BF16) as tp1,
    ):
        pps = [p0, p1]
        tps = [tp0, tp1]
        ph = Ph(S)
        ldb = ph.add("sync", lambda e: e.dma_start(
            bo_row[:, :], ext["bo"].rearrange("(o n) -> o n", o=1)), inc="sLd")
        cast = 0
        for dvc in range(DC):
            for hh in range(2):
                ld = ph.add("sync", lambda e, dvc=dvc, hh=hh: e.dma_start(
                    wstg[:, :], ext["wo"][dvc * 128:(dvc + 1) * 128,
                                          hh * 2048:(hh + 1) * 2048]),
                    waits=[("sV", cast)], inc="sLd")
                cast = ph.add("vector", lambda e, dvc=dvc, hh=hh: e.tensor_copy(
                    wo_sb[:, dvc * 4096 + hh * 2048:dvc * 4096 + (hh + 1) * 2048],
                    wstg[:, :]),
                    waits=[("sLd", ld)], inc="sV")
        a2_sS = S.cnt["sS"]
        st = {"aev": {0: 0, 1: 0}, "attn_v": {0: 0, 1: 0}, "attn_s": {0: 0, 1: 0},
              "x2_st": 0, "xread_v": 0, "xread_s": 0, "ldx": {}, "tp_ev": {0: 0, 1: 0},
              "lt_st": 0, "tr_done": 0}

        def stage_a(qt):
            b = qt % 2
            for cg in range(8):
                par = cg % 2
                mm = ph.add("tensor", lambda e, par=par, cg=cg: e.matmul(
                    pps[par][:, :], ones_row[:, :], bo_row[:, cg * 512:(cg + 1) * 512],
                    start=True, stop=False),
                    waits=[("sLd", ldb), ("sV", st["aev"][par]), ("sV", cast),
                           ("sS", a2_sS)],
                    inc=None)
                for dvc in range(DC):
                    mm = ph.add("tensor", lambda e, par=par, qt=qt, cg=cg, dvc=dvc:
                                e.matmul(pps[par][:, :],
                                         oaT[:, dvc * 1024 + qt * 128:dvc * 1024 + qt * 128 + 128],
                                         wo_sb[:, dvc * 4096 + cg * 512:dvc * 4096 + (cg + 1) * 512],
                                         start=False, stop=(dvc == DC - 1)),
                                waits=[],
                                inc="sT" if dvc == DC - 1 else None)
                st["aev"][par] = ph.add("vector", lambda e, par=par, cg=cg, b=b:
                                        e.tensor_copy(attn[:, b * 4096 + cg * 512:
                                                           b * 4096 + (cg + 1) * 512],
                                                      pps[par][:, :]),
                                        waits=[("sT", mm), ("sV", st["attn_v"][b]),
                                               ("sS", st["attn_s"][b])], inc="sV")

        def stage_b(qt):
            b = qt % 2
            st["ldx"][qt] = ph.add("sync", lambda e, qt=qt: e.dma_start(
                xb[:, :], ext["xown"][qt * 128:(qt + 1) * 128, :]),
                waits=([("sV", st["xread_v"]), ("sS", st["xread_s"]),
                        ("sSt", st["x2_st"])] if qt > 0 else []),
                inc="sLd")
            abuf = attn[:, b * 4096:(b + 1) * 4096]
            stA = stats[:, b * 8:b * 8 + 8]
            vdone, s1 = emit_ln(ph, abuf, abuf, stA, trash[:, :],
                                [("sV", st["aev"][0]), ("sV", st["aev"][1])],
                                eps_ap=eps_col[:, :])
            vadd = ph.add("vector", lambda e, abuf=abuf: e.tensor_add(xb[:, :], abuf, xb[:, :]),
                          waits=[("sLd", st["ldx"][qt]), ("sSt", st["x2_st"]),
                                 ("sV", vdone)], inc="sV")
            st["attn_v"][b] = vadd
            st["attn_s"][b] = s1
            st["x2_st"] = ph.add("gpsimd", lambda e, qt=qt: e.dma_start(
                scr["x2S"][qt * 128:(qt + 1) * 128, :], xb[:, :]),
                waits=[("sV", vadd)], inc="sSt")
            v2done, s2 = emit_ln(ph, xb[:, :], lnx2[:, :], stats[:, 16:24], trash[:, :],
                                 [("sV", vadd), ("sT", st["tr_done"])],
                                 eps_ap=eps_col[:, :])
            st["xread_v"] = v2done
            st["xread_s"] = s2
            for pf in range(8):
                par = pf % 2
                tcnt = None
                for q in range(4):
                    fc = pf * 4 + q
                    tcnt = ph.add("tensor", lambda e, par=par, q=q, fc=fc:
                                  e.transpose(tps[par][:, q * 128:(q + 1) * 128],
                                              lnx2[:, fc * 128:(fc + 1) * 128], eye_bf[:, :]),
                                  waits=([("sV", v2done), ("sS", st["tp_ev"][par])]
                                         if q == 0 else []),
                                  inc="sT" if q == 3 else None)
                st["tp_ev"][par] = ph.add(
                    "scalar", lambda e, par=par, pf=pf:
                    e.copy(ltb[:, pf * 512:(pf + 1) * 512], tps[par][:, :]),
                    waits=[("sT", tcnt), ("sSt", st["lt_st"])], inc="sS")
            st["tr_done"] = S.cnt["sT"]
            st["lt_st"] = ph.add("gpsimd", lambda e, qt=qt: e.dma_start(
                scr["lnx2tS"][:, :, qt * 128:(qt + 1) * 128]
                .rearrange("c p t -> p c t"),
                ltb.rearrange("p (c t) -> p c t", t=128)),
                waits=[("sS", st["tp_ev"][0]), ("sS", st["tp_ev"][1])], inc="sSt")
            if qt == 3:
                out["lt_store_g0"] = S.cnt["sSt"]
            if qt == 7:
                out["lt_store_g1"] = S.cnt["sSt"]

        stage_a(0)
        for qt in range(NQT):
            if qt + 1 < NQT:
                stage_a(qt + 1)
            stage_b(qt)
        out["x2_all"] = st["x2_st"]
        ph.add("gpsimd", lambda e: e.engine_nop(), waits=[("sSt", S.cnt["sSt"])])
        S.emit(ph)
    return out


HCW = [128] * 10 + [86]


def phase_f1(nc, S, ext, scr, scal, a3out, prep_out):
    out = {}
    with (
        nc.sbuf_tensor("f1_lx", [128, 2 * FC * 512], BF16) as lx,
        nc.sbuf_tensor("f1_lxn", [128, 2 * FC * 512], BF16) as lxn,
        nc.sbuf_tensor("f1_wg", [128, 2 * FC * 128], BF16) as wgb,
        nc.sbuf_tensor("f1_wu", [128, 2 * FC * 128], BF16) as wub,
        nc.sbuf_tensor("f1_sg", [128, 2 * 512], BF16) as sgb,
        nc.sbuf_tensor("f1_h", [128, 2 * 512], BF16) as hb,
        nc.psum_tensor("f1_g0", [128, 512], F32) as g0,
        nc.psum_tensor("f1_g1", [128, 512], F32) as g1,
        nc.psum_tensor("f1_u0", [128, 512], F32) as u0,
        nc.psum_tensor("f1_u1", [128, 512], F32) as u1,
    ):
        gps = [g0, g1]
        ups = [u0, u1]
        ph = Ph(S)
        ldslab = {}
        negdone = {}
        for g in range(2):
            ldslab[g] = ph.add("sync", lambda e, g=g: e.dma_start(
                lx.rearrange("p (g c t) -> p g c t", g=2, t=512)[:, g],
                scr["lnx2tS"][:, :, g * 512:(g + 1) * 512].rearrange("c p t -> p c t")),
                waits=[("sSt", a3out["lt_store_g0" if g == 0 else "lt_store_g1"])],
                inc="sLd")
            negdone[g] = ph.add("scalar", lambda e, g=g: e.mul(
                lxn[:, g * FC * 512:(g + 1) * FC * 512],
                lx[:, g * FC * 512:(g + 1) * FC * 512], -1.0),
                waits=[("sLd", ldslab[g])], inc="sS")
        ldg = {0: 0, 1: 0}
        ldu = {0: 0, 1: 0}
        wfree = {0: 0, 1: 0}
        gev = {0: 0, 1: 0}
        uev = {0: 0, 1: 0}
        hst = {0: 0, 1: 0}
        for hcg in range(11):
            w = HCW[hcg]
            b = hcg % 2
            ldg[b] = ph.add("sync", lambda e, b=b, hcg=hcg, w=w: e.dma_start(
                wgb.rearrange("p (b c o) -> p b c o", b=2, o=128)[:, b, :, 0:w],
                scr["wgtS"][:, hcg * 128:hcg * 128 + w]
                .rearrange("(c p) o -> p c o", p=128)),
                waits=[("sT", wfree[b]), ("sSt", prep_out["wg_done"])], inc="sLd")
            ldu[b] = ph.add("sync", lambda e, b=b, hcg=hcg, w=w: e.dma_start(
                wub.rearrange("p (b c o) -> p b c o", b=2, o=128)[:, b, :, 0:w],
                scr["wutS"][:, hcg * 128:hcg * 128 + w]
                .rearrange("(c p) o -> p c o", p=128)),
                waits=[("sT", wfree[b]), ("sSt", prep_out["wu_done"])], inc="sLd")
            for g in range(2):
                for k in range(8):
                    par = k % 2
                    gm = um = None
                    for fc in range(FC):
                        i, dc = fc // 4, fc % 4
                        j = int(JMAP[i, k])
                        src = lx if SMAP[i, k] > 0 else lxn
                        gm = ph.add("tensor", lambda e, par=par, b=b, g=g, fc=fc,
                                    dc=dc, j=j, src=src, w=w:
                                    e.matmul(gps[par][0:w, :],
                                             wgb[:, (b * FC + dc * 8 + j) * 128:
                                                 (b * FC + dc * 8 + j) * 128 + w],
                                             src[:, (g * FC + fc) * 512:
                                                 (g * FC + fc) * 512 + 512],
                                             start=(fc == 0), stop=(fc == FC - 1)),
                                    waits=([("sLd", ldg[b]), ("sLd", ldslab[g]),
                                            ("sS", negdone[g]), ("sS", gev[par])]
                                           if fc == 0 else []),
                                    inc="sT" if fc == FC - 1 else None)
                    for fc in range(FC):
                        i, dc = fc // 4, fc % 4
                        j = int(JMAP[i, k])
                        src = lx if SMAP[i, k] > 0 else lxn
                        um = ph.add("tensor", lambda e, par=par, b=b, g=g, fc=fc,
                                    dc=dc, j=j, src=src, w=w:
                                    e.matmul(ups[par][0:w, :],
                                             wub[:, (b * FC + dc * 8 + j) * 128:
                                                 (b * FC + dc * 8 + j) * 128 + w],
                                             src[:, (g * FC + fc) * 512:
                                                 (g * FC + fc) * 512 + 512],
                                             start=(fc == 0), stop=(fc == FC - 1)),
                                    waits=([("sLd", ldu[b]), ("sV", uev[par])]
                                           if fc == 0 else []),
                                    inc="sT" if fc == FC - 1 else None)
                    gev[par] = ph.add("scalar", lambda e, par=par, w=w:
                                      e.activation(sgb[0:w, par * 512:par * 512 + 512],
                                                   gps[par][0:w, :],
                                                   AF.Silu, bias=0.0, scale=scal[0:w, 3:4]),
                                      waits=[("sT", gm), ("sV", uev[par])], inc="sS")
                    uev[par] = ph.add("vector", lambda e, par=par, w=w:
                                      e.scalar_tensor_tensor(hb[0:w, par * 512:par * 512 + 512],
                                                             ups[par][0:w, :], scal[0:w, 4:5],
                                                             sgb[0:w, par * 512:par * 512 + 512],
                                                             ALU.mult, ALU.mult),
                                      waits=[("sT", um), ("sS", gev[par]),
                                             ("sSt", hst[par])], inc="sV")
                    hst[par] = ph.add("gpsimd", lambda e, par=par, k=k, hcg=hcg,
                                      g=g, w=w:
                                      e.dma_start(scr["hS"][k, hcg, 0:w,
                                                            g * 512:(g + 1) * 512],
                                                  hb[0:w, par * 512:par * 512 + 512]),
                                      waits=[("sV", uev[par])], inc="sSt")
            wfree[b] = S.cnt["sT"]
        out["h_done_g0"] = S.cnt["sSt"]
        out["h_done_g1"] = S.cnt["sSt"]
        ph.add("gpsimd", lambda e: e.engine_nop(), waits=[("sSt", S.cnt["sSt"])])
        S.emit(ph)
    return out


def phase_f2(nc, S, ext, scr, trash, eps_col, f1out, prep_out, a3out):
    with (
        nc.sbuf_tensor("f2_wd", [128, 11 * 8 * 512], BF16) as wdt,
        nc.sbuf_tensor("f2_ht", [128, 2 * 88 * 128], BF16) as ht,
        nc.sbuf_tensor("f2_x2", [128, 4096], F32) as x2b,
        nc.sbuf_tensor("f2_ffn", [128, 4096], F32) as ffn,
        nc.sbuf_tensor("f2_st", [128, 8], F32) as stats,
        nc.psum_tensor("f2_pp0", [128, 512], F32) as pp0,
        nc.psum_tensor("f2_pn0", [128, 512], F32) as pn0,
        nc.psum_tensor("f2_pp1", [128, 512], F32) as pp1,
        nc.psum_tensor("f2_pn1", [128, 512], F32) as pn1,
        nc.psum_tensor("f2_pp2", [128, 512], F32) as pp2,
        nc.psum_tensor("f2_pn2", [128, 512], F32) as pn2,
        nc.psum_tensor("f2_pp3", [128, 512], F32) as pp3,
        nc.psum_tensor("f2_pn3", [128, 512], F32) as pn3,
    ):
        dpp = [pp0, pp1, pp2, pp3]
        dpn = [pn0, pn1, pn2, pn3]
        ph = Ph(S)
        ldw = {}
        for dcw in range(11):
            ldw[dcw] = ph.add("sync", lambda e, dcw=dcw: e.dma_start(
                wdt.rearrange("p (d j o) -> p d j o", d=11, j=8)[:, dcw],
                scr["wdtS"].rearrange("(j d p) o -> p d j o", p=128, d=11)[:, dcw]),
                waits=[("sSt", prep_out["wd_done"])] if dcw == 0 else [], inc="sLd")
        ldh = {0: 0, 1: 0}
        htfree = {0: 0, 1: 0}
        devP = {0: 0, 1: 0, 2: 0, 3: 0}
        devN = {0: 0, 1: 0, 2: 0, 3: 0}
        out_st = 0
        x2_rd = 0
        ffn_rd_v = 0
        ffn_rd_s = 0
        for qt in range(NQT):
            hbuf = qt % 2
            ldh[hbuf] = ph.add("sync", lambda e, hbuf=hbuf, qt=qt: e.dma_start(
                ht.rearrange("p (b i d t) -> p b i d t", b=2, i=8, t=128)[:, hbuf],
                scr["hS"][:, :, :, qt * 128:(qt + 1) * 128]
                .rearrange("i d p t -> p i d t")),
                waits=[("sT", htfree[hbuf]),
                       ("sSt", f1out["h_done_g0" if qt < 4 else "h_done_g1"])],
                inc="sLd")
            ldx2 = ph.add("sync", lambda e, qt=qt: e.dma_start(
                x2b[:, :], scr["x2S"][qt * 128:(qt + 1) * 128, :]),
                waits=[("sV", x2_rd), ("sSt", max(out_st, a3out["x2_all"]))],
                inc="sLd")
            for wave in range(2):
                for kk in range(4):
                    k2 = wave * 4 + kk
                    work = []
                    for dc in range(11):
                        for i in range(8):
                            j = int(JMAP[i, k2])
                            work.append((dc, i, j, SMAP[i, k2] > 0))
                    npos = sum(1 for *_, p in work if p)
                    assert 0 < npos < len(work)
                    seen_pos = seen_neg = 0
                    last_pos = last_neg = None
                    for dc, i, j, pos in work:
                        wdc = HCW[dc]
                        if pos:
                            seen_pos += 1
                            first = seen_pos == 1
                            stop = seen_pos == npos
                            tgt = dpp[kk]
                        else:
                            seen_neg += 1
                            first = seen_neg == 1
                            stop = seen_neg == len(work) - npos
                            tgt = dpn[kk]
                        waits = []
                        if first:
                            waits += [("sLd", ldh[hbuf]),
                                      ("sS", devP[kk]) if pos else ("sV", devN[kk])]
                        if i == 0 or first:
                            waits += [("sLd", ldw[dc])]
                        mm = ph.add("tensor", lambda e, tgt=tgt, hbuf=hbuf, i=i,
                                    dc=dc, j=j, wdc=wdc, first=first, stop=stop:
                                    e.matmul(tgt[:, :],
                                             ht[0:wdc, (hbuf * 88 + i * 11 + dc) * 128:
                                                (hbuf * 88 + i * 11 + dc) * 128 + 128],
                                             wdt[0:wdc, (dc * 8 + j) * 512:
                                                 (dc * 8 + j + 1) * 512],
                                             start=first, stop=stop),
                                    waits=waits, inc="sT" if stop else None)
                        if pos and seen_pos == npos:
                            last_pos = mm
                        if (not pos) and seen_neg == len(work) - npos:
                            last_neg = mm
                    devP[kk] = ph.add("scalar", lambda e, kk=kk, k2=k2:
                                      e.copy(ffn[:, k2 * 512:(k2 + 1) * 512],
                                             dpp[kk][:, :]),
                                      waits=[("sT", last_pos), ("sV", ffn_rd_v),
                                             ("sS", ffn_rd_s)], inc="sS")
                    devN[kk] = ph.add("vector", lambda e, kk=kk, k2=k2:
                                      e.tensor_tensor(ffn[:, k2 * 512:(k2 + 1) * 512],
                                                      ffn[:, k2 * 512:(k2 + 1) * 512],
                                                      dpn[kk][:, :], ALU.subtract),
                                      waits=[("sT", last_neg), ("sS", devP[kk]),
                                             ("sV", ffn_rd_v)], inc="sV")
            htfree[hbuf] = S.cnt["sT"]
            vdone, s1 = emit_ln(ph, ffn[:, :], ffn[:, :], stats[:, 0:8], trash[:, :],
                                [("sV", devN[0]), ("sV", devN[1]),
                                 ("sV", devN[2]), ("sV", devN[3])],
                                eps_ap=eps_col[:, :])
            ffn_rd_s = s1
            vadd = ph.add("vector", lambda e: e.tensor_add(x2b[:, :], ffn[:, :], x2b[:, :]),
                          waits=[("sLd", ldx2), ("sV", vdone)], inc="sV")
            ffn_rd_v = vadd
            x2_rd = vadd
            out_st = ph.add("gpsimd", lambda e, qt=qt: e.dma_start(
                ext["out"][qt * 128:(qt + 1) * 128, :], x2b[:, :]),
                waits=[("sV", vadd)], inc="sSt")
        ph.add("gpsimd", lambda e: e.engine_nop(), waits=[("sSt", S.cnt["sSt"])])
        S.emit(ph)


_NC_CACHE = None


def kernel(L0_attn_x, L0_attn_Wqkv, L0_attn_bqkv, L0_attn_Wo, L0_attn_bo,
           L0_ffn_w_gate, L0_ffn_w_up, L0_ffn_w_down):
    global _NC_CACHE
    x = np.ascontiguousarray(np.asarray(L0_attn_x, dtype=np.float32))
    wqkv = np.ascontiguousarray(np.asarray(L0_attn_Wqkv, dtype=np.float32))
    bqkv = np.ascontiguousarray(np.asarray(L0_attn_bqkv, dtype=np.float32))
    wo = np.ascontiguousarray(np.asarray(L0_attn_Wo, dtype=np.float32))
    bo = np.ascontiguousarray(np.asarray(L0_attn_bo, dtype=np.float32))
    wg = np.ascontiguousarray(np.asarray(L0_ffn_w_gate, dtype=np.float32))
    wu = np.ascontiguousarray(np.asarray(L0_ffn_w_up, dtype=np.float32))
    wd = np.ascontiguousarray(np.asarray(L0_ffn_w_down, dtype=np.float32))
    eye = np.eye(128, dtype=np.float32)

    if _NC_CACHE is None:
        _NC_CACHE = build_nc()
    nc = _NC_CACHE

    in_maps = []
    for c in range(8):
        b = c // 4
        q0 = (c % 4) * TOK
        in_maps.append({
            "xseq": x[b],
            "xown": np.ascontiguousarray(x[b, q0:q0 + TOK]),
            "wqkv": wqkv, "bqkv": bqkv, "wo": wo, "bo": bo,
            "wg": wg, "wu": wu, "wd": wd, "eye": eye,
        })
    res = run_bass_kernel_spmd(nc, in_maps, core_ids=list(range(8)))
    out = np.empty((2, SEQ, C), dtype=np.float32)
    for c in range(8):
        b = c // 4
        q0 = (c % 4) * TOK
        out[b, q0:q0 + TOK] = res.results[c]["out"]
    return out


if __name__ == "__main__":
    import reference
    ins = reference.setup_inputs(0)
    got = kernel(**{k: np.asarray(v) for k, v in ins.items()})
    ref = np.asarray(reference.reference(**ins))
    num = float(np.mean((got - ref) ** 2))
    den = float(np.mean(ref ** 2))
    print("residual variance ratio:", num / den)


# version 16 (final)
# speedup vs baseline: 1.0968x; 1.0260x over previous
import numpy as np
from contextlib import ExitStack

import concourse.bass as bass
import concourse.mybir as mybir
from concourse.bass_utils import run_bass_kernel_spmd

F32 = mybir.dt.float32
BF16 = mybir.dt.bfloat16
AF = mybir.ActivationFunctionType
ALU = mybir.AluOpType
AX = mybir.AxisListType

SEQ = 4096
TOK = 1024
C = 4096
D = 1024
HID = 10928
H8 = 1366
FC = C // 128
DC = D // 128
KCH = SEQ // 128
NQT = TOK // 128
NG = SEQ // 512
OG = TOK // 512
EPS = 1e-5
QSCALE = 1.0 / 32.0
LFFN = 8 * H8 * 512
WD_W = [128] * 10 + [86]
ENGS = ("sync", "gpsimd", "tensor", "vector", "scalar")
SEMS = ("sLd", "sSt", "sT", "sV", "sS", "pLd", "pSt", "pT", "pV", "pS",
        "qVW0", "qVW1", "qVL0", "qVL1", "qVS0", "qVS1",
        "qPW0", "qPW1", "qPS0", "qPS1", "qPL0", "qPL1")
DMASEMS = ("sLd", "sSt", "pLd", "pSt")


def _oct_maps():
    TRIPLES = [(1, 2, 3), (1, 4, 5), (1, 7, 6), (2, 4, 6), (2, 5, 7), (3, 4, 7), (3, 6, 5)]
    S = np.zeros((8, 8, 8), np.float32)
    S[0, 0, 0] = 1.0
    for i in range(1, 8):
        S[0, i, i] = 1.0
        S[i, 0, i] = 1.0
        S[i, i, 0] = -1.0
    for a, b, c in TRIPLES:
        for x, y, z in ((a, b, c), (b, c, a), (c, a, b)):
            S[x, y, z] = 1.0
            S[y, x, z] = -1.0
    jmap = np.zeros((8, 8), np.int64)
    smap = np.zeros((8, 8), np.float64)
    for i in range(8):
        for k in range(8):
            js = np.nonzero(S[:, i, k])[0]
            assert len(js) == 1
            jmap[i, k] = js[0]
            smap[i, k] = S[js[0], i, k]
    return jmap, smap


JMAP, SMAP = _oct_maps()


class Ph:
    def __init__(self, S):
        self.S = S
        self.ops = {e: [] for e in ENGS}

    def add(self, eng, fn, waits=(), inc=None, n=None):
        if n is None:
            n = 16 if (inc in DMASEMS or (inc and inc.startswith("q"))) else 1
        ws = [(s, v) for (s, v) in waits if v > 0]
        self.ops[eng].append((fn, ws, inc, n))
        if inc is None:
            return None
        self.S.cnt[inc] += n
        return self.S.cnt[inc]


class Sched:
    def __init__(self, nc, stack):
        self.nc = nc
        self.cnt = {}
        self.sems = {}
        self.stack = stack
        self.ph = None
        for name in SEMS:
            h = stack.enter_context(nc.semaphore(name))
            self.sems[name] = h
            self.cnt[name] = 0

    def fin(self, names=("sT", "sV", "sS", "sSt")):
        qs = [s for s in self.cnt if s.startswith("q") and self.cnt[s] > 0]
        return [(s, self.cnt[s]) for s in list(names) + qs]

    def emit(self):
        ph = self.ph
        sems = self.sems
        with self.nc.Block() as blk:
            for e in ENGS:
                ops = ph.ops[e]
                if not ops:
                    continue

                def runner(eng, ops=ops):
                    seen = {}
                    for fn, ws, inc, n in ops:
                        for s, v in ws:
                            if seen.get(s, 0) < v:
                                eng.wait_ge(sems[s], v)
                                seen[s] = v
                        ins = fn(eng)
                        if inc is not None:
                            ins.then_inc(sems[inc], n)
                getattr(blk, e)(runner)
        self.ph = None


def pull(gen, n):
    if gen is None:
        return False
    for _ in range(n):
        try:
            next(gen)
        except StopIteration:
            return False
    return True


def drain(gen):
    while pull(gen, 64):
        pass


def emit_ln(ph, x_ap, out_ap, st, trash_ap, wait_x, n_feat=4096, eps_ap=None):
    inv = 1.0 / n_feat
    v1 = ph.add("vector", lambda e, x=x_ap, st=st: e.reduce_sum(st[:, 0:1], x, AX.X),
                waits=wait_x, inc="sV")
    s1 = ph.add("scalar", lambda e, x=x_ap, st=st, tr=trash_ap:
                e.activation(tr, x, AF.Square, accum_out=st[:, 1:2]),
                waits=wait_x, inc="sS")
    v2 = ph.add("vector", lambda e, st=st, inv=inv:
                e.tensor_scalar_mul(st[:, 2:3], st[:, 0:1], inv),
                waits=[("sV", v1)], inc="sV")
    s2 = ph.add("scalar", lambda e, st=st: e.activation(st[:, 3:4], st[:, 2:3], AF.Square),
                waits=[("sV", v2)], inc="sS")
    v3 = ph.add("vector", lambda e, st=st, inv=inv:
                e.tensor_scalar(st[:, 4:5], st[:, 1:2], inv, st[:, 3:4], ALU.mult, ALU.subtract),
                waits=[("sS", s2)], inc="sV")
    s3 = ph.add("scalar", lambda e, st=st, eps_ap=eps_ap:
                e.activation(st[:, 5:6], st[:, 4:5], AF.Sqrt, bias=eps_ap, scale=1.0),
                waits=[("sV", v3)], inc="sS")
    v3r = ph.add("vector", lambda e, st=st: e.reciprocal(st[:, 6:7], st[:, 5:6]),
                 waits=[("sS", s3)], inc="sV")
    v4 = ph.add("vector", lambda e, x=x_ap, o=out_ap, st=st:
                e.tensor_scalar(o, x, st[:, 2:3], st[:, 6:7], ALU.subtract, ALU.mult),
                waits=[("sV", v3r)], inc="sV")
    return v4, s1


def build_nc():
    nc = bass.Bass("TRN2", target_bir_lowering=False)

    ext = {}
    ext["xseq"] = nc.dram_tensor("xseq", [SEQ, C], F32, kind="ExternalInput").ap()
    ext["xown"] = nc.dram_tensor("xown", [TOK, C], F32, kind="ExternalInput").ap()
    ext["wqkv"] = nc.dram_tensor("wqkv", [C, 3 * D], F32, kind="ExternalInput").ap()
    ext["bqkv"] = nc.dram_tensor("bqkv", [3 * D], F32, kind="ExternalInput").ap()
    ext["wo"] = nc.dram_tensor("wo", [D, C], F32, kind="ExternalInput").ap()
    ext["bo"] = nc.dram_tensor("bo", [C], F32, kind="ExternalInput").ap()
    ext["wg"] = nc.dram_tensor("wg", [8, H8, 512], F32, kind="ExternalInput").ap()
    ext["wu"] = nc.dram_tensor("wu", [8, H8, 512], F32, kind="ExternalInput").ap()
    ext["wd"] = nc.dram_tensor("wd", [8, 512, H8], F32, kind="ExternalInput").ap()
    ext["eye"] = nc.dram_tensor("eye", [128, 128], F32, kind="ExternalInput").ap()
    ext["out"] = nc.dram_tensor("out", [TOK, C], F32, kind="ExternalOutput").ap()

    scr = {}
    scr["ktS"] = nc.dram_tensor("ktS", [D, SEQ], BF16, kind="Internal").ap()
    scr["vS"] = nc.dram_tensor("vS", [SEQ, D], BF16, kind="Internal").ap()
    scr["lnxtS"] = nc.dram_tensor("lnxtS", [NG, 128, FC * 512], BF16, kind="Internal").ap()
    scr["x2S"] = nc.dram_tensor("x2S", [TOK, C], F32, kind="Internal").ap()
    scr["lnx2tS"] = nc.dram_tensor("lnx2tS", [FC, 128, TOK], BF16, kind="Internal").ap()
    scr["hS"] = nc.dram_tensor("hS", [8, 11, 128, TOK], BF16, kind="Internal").ap()
    scr["wgtS"] = nc.dram_tensor("wgtS", [FC * 128, H8], BF16, kind="Internal").ap()
    scr["wutS"] = nc.dram_tensor("wutS", [FC * 128, H8], BF16, kind="Internal").ap()
    scr["wdtS"] = nc.dram_tensor("wdtS", [8 * 11 * 128, 512], BF16, kind="Internal").ap()

    with ExitStack() as stack:
        S = Sched(nc, stack)
        eye_bf = stack.enter_context(nc.sbuf_tensor("eye_bf", [128, 128], BF16))
        eye_f32 = stack.enter_context(nc.sbuf_tensor("eye_f32", [128, 128], F32))
        ones_col = stack.enter_context(nc.sbuf_tensor("ones_col", [128, 1], F32))
        ones_row = stack.enter_context(nc.sbuf_tensor("ones_row", [1, 128], F32))
        bqkv_sb = stack.enter_context(nc.sbuf_tensor("bqkv_sb", [128, 24], F32))
        bq32 = stack.enter_context(nc.sbuf_tensor("bq32", [128, 8], F32))
        scal = stack.enter_context(nc.sbuf_tensor("scal", [128, 8], F32))
        trash = stack.enter_context(nc.sbuf_tensor("trash", [128, 4096], BF16))
        eps_col = stack.enter_context(nc.sbuf_tensor("eps_col", [128, 1], F32))
        ebv = ExitStack()
        ebv.__enter__()
        bv_bc = ebv.enter_context(nc.sbuf_tensor("bv_bc", [128, 1024], F32))

        p0fin = phase_p0(nc, S, ext, eye_bf, eye_f32, ones_col, ones_row,
                         bqkv_sb, bq32, bv_bc, eps_col)

        a1out = {}
        prep1_out = {}
        esp = ExitStack()
        esp.__enter__()
        pbufs = dict(
            stg=esp.enter_context(nc.sbuf_tensor("pp_stg", [128, 2 * 2048], F32)),
            part=esp.enter_context(nc.sbuf_tensor("pp_part", [128, 48], F32)),
            acc=esp.enter_context(nc.sbuf_tensor("pp_acc", [128, 4], F32)),
            tot=esp.enter_context(nc.sbuf_tensor("pp_tot", [1, 4], F32)),
            win=esp.enter_context(nc.sbuf_tensor("pp_win", [128, 2 * 1024], F32)),
            absb=esp.enter_context(nc.sbuf_tensor("pp_abs", [128, 1024], F32)),
            sgnb=esp.enter_context(nc.sbuf_tensor("pp_sgn", [128, 1024], F32)),
            tern=esp.enter_context(nc.sbuf_tensor("pp_tern", [128, 2 * 1024], BF16)),
            outb=esp.enter_context(nc.sbuf_tensor("pp_out", [128, 2 * 512], BF16)),
            ps=esp.enter_context(nc.psum_tensor("pp_ps", [128, 128], F32)),
            ps2=esp.enter_context(nc.psum_tensor("pp_ps2", [128, 4], F32)),
            tps=[esp.enter_context(nc.psum_tensor("pp_tp0", [128, 512], BF16)),
                 esp.enter_context(nc.psum_tensor("pp_tp1", [128, 512], BF16))],
        )
        prep1_g = gen_prep1(nc, S, ext, scr, pbufs, scal, eye_bf, eye_f32,
                            ones_row, prep1_out, p0fin)
        esa = ExitStack()
        esa.__enter__()
        wk = esa.enter_context(nc.sbuf_tensor("wk", [128, FC * 1024], BF16))
        wstg = esa.enter_context(nc.sbuf_tensor("a1_wstg", [128, 2 * 1024], F32))
        xb = esa.enter_context(nc.sbuf_tensor("a1_x", [128, 2 * 4096], F32))
        lnx = esa.enter_context(nc.sbuf_tensor("a1_lnx", [128, 2 * 4096], BF16))
        lnxt = esa.enter_context(nc.sbuf_tensor("a1_lnxt", [128, FC * 512], BF16))
        ktb = esa.enter_context(nc.sbuf_tensor("a1_kt", [128, 2 * 512], BF16))
        stats = esa.enter_context(nc.sbuf_tensor("a1_st", [128, 16], F32))
        kps = [esa.enter_context(nc.psum_tensor("a1_kp0", [128, 512], F32)),
               esa.enter_context(nc.psum_tensor("a1_kp1", [128, 512], F32))]
        tps = [esa.enter_context(nc.psum_tensor("a1_tp0", [128, 512], BF16)),
               esa.enter_context(nc.psum_tensor("a1_tp1", [128, 512], BF16))]
        S.ph = Ph(S)
        for _ in gen_a1k(nc, S, ext, scr, eye_bf, bqkv_sb, trash, eps_col,
                         wk, wstg, xb, lnx, lnxt, ktb, stats,
                         kps, tps, p0fin, a1out):
            pull(prep1_g, 2)
        S.emit()
        esa.close()
        esb = ExitStack()
        esb.__enter__()
        wv = esb.enter_context(nc.sbuf_tensor("wv", [128, FC * 1024], BF16))
        vwstg = esb.enter_context(nc.sbuf_tensor("v_wstg", [128, 2 * 1024], F32))
        vlnxt = esb.enter_context(nc.sbuf_tensor("v_lnxt", [128, 2 * FC * 512], BF16))
        vtb = esb.enter_context(nc.sbuf_tensor("v_vt", [128, 2 * 512], BF16))
        vps = [esb.enter_context(nc.psum_tensor("v_p0", [128, 512], F32)),
               esb.enter_context(nc.psum_tensor("v_p1", [128, 512], F32))]
        S.ph = Ph(S)
        for _ in gen_a1v(nc, S, ext, scr, bv_bc, wv, vwstg, vlnxt, vtb,
                         vps, a1out):
            pull(prep1_g, 3)
        drain(prep1_g)
        S.emit()
        esb.close()
        esp.close()
        ebv.close()
        prep1_fin = [("pT", S.cnt["pT"]), ("pV", S.cnt["pV"]),
                     ("pS", S.cnt["pS"]),
                     ("qPS0", S.cnt["qPS0"]), ("qPS1", S.cnt["qPS1"]),
                     ("qPW0", S.cnt["qPW0"]), ("qPW1", S.cnt["qPW1"]),
                     ("qPL0", S.cnt["qPL0"]), ("qPL1", S.cnt["qPL1"])]

        es1 = stack.enter_context(ExitStack())
        oaT = es1.enter_context(nc.sbuf_tensor("oaT", [128, DC * 1024], BF16))
        esq = ExitStack()
        esq.__enter__()
        QT = esq.enter_context(nc.sbuf_tensor("QT", [128, DC * 1024], BF16))
        b_fin = S.fin()
        esc = ExitStack()
        esc.__enter__()
        wq = esc.enter_context(nc.sbuf_tensor("wq", [128, FC * 1024], BF16))
        qwstg = esc.enter_context(nc.sbuf_tensor("q_wstg", [128, 2 * 512], F32))
        qlnxt = esc.enter_context(nc.sbuf_tensor("q_lnxt", [128, 2 * FC * 512], BF16))
        qps = [esc.enter_context(nc.psum_tensor("q_p0", [128, 512], F32)),
               esc.enter_context(nc.psum_tensor("q_p1", [128, 512], F32))]
        S.ph = Ph(S)
        phase_a1q(nc, S, ext, scr, bqkv_sb, QT, wq, qwstg, qlnxt,
                  qps, a1out, b_fin + prep1_fin)
        S.emit()
        esc.close()
        c_fin = S.fin()
        a2_guard = c_fin + prep1_fin
        phase_a2(nc, S, scr, eye_bf, trash, QT, oaT, a1out, a2_guard)
        d_fin = S.fin()
        esq.close()
        a3out = phase_a3(nc, S, ext, scr, ones_row, trash, oaT, eps_col,
                         eye_bf, d_fin)
        es1.close()
        e_fin = S.fin()
        es2 = stack.enter_context(ExitStack())
        wgb = es2.enter_context(nc.sbuf_tensor("f1_wg", [128, 2 * FC * 128], BF16))
        wub = es2.enter_context(nc.sbuf_tensor("f1_wu", [128, 2 * FC * 128], BF16))

        f1out = {}
        prep2_out = {}
        esd = ExitStack()
        esd.__enter__()
        dbufs = dict(
            stg=esd.enter_context(nc.sbuf_tensor("pd_stg", [128, 2 * 2048], F32)),
            part=esd.enter_context(nc.sbuf_tensor("pd_part", [128, 24], F32)),
            acc=esd.enter_context(nc.sbuf_tensor("pd_acc", [128, 4], F32)),
            tot=esd.enter_context(nc.sbuf_tensor("pd_tot", [1, 4], F32)),
            win=esd.enter_context(nc.sbuf_tensor("pd_win", [128, 2 * 2 * H8], F32)),
            absb=esd.enter_context(nc.sbuf_tensor("pd_abs", [128, 2 * H8], F32)),
            sgnb=esd.enter_context(nc.sbuf_tensor("pd_sgn", [128, 2 * H8], F32)),
            tern=esd.enter_context(nc.sbuf_tensor("pd_tern", [128, 2 * 2 * H8], BF16)),
            outb=esd.enter_context(nc.sbuf_tensor("pd_out", [128, 2 * 1408], BF16)),
            ps=esd.enter_context(nc.psum_tensor("pd_ps", [128, 128], F32)),
            ps2=esd.enter_context(nc.psum_tensor("pd_ps2", [128, 4], F32)),
            tps=[esd.enter_context(nc.psum_tensor("pd_tp0", [128, 512], BF16)),
                 esd.enter_context(nc.psum_tensor("pd_tp1", [128, 512], BF16))],
            eye_bf=eye_bf,
        )
        prep2_g = gen_prep2(nc, S, ext, scr, dbufs, scal, eye_f32, ones_row,
                            prep2_out, e_fin)
        esf = ExitStack()
        esf.__enter__()
        lx = esf.enter_context(nc.sbuf_tensor("f1_lx", [128, FC * 512], BF16))
        lxn = esf.enter_context(nc.sbuf_tensor("f1_lxn", [128, FC * 512], BF16))
        sgb = esf.enter_context(nc.sbuf_tensor("f1_sg", [128, 2 * 512], BF16))
        hb = esf.enter_context(nc.sbuf_tensor("f1_h", [128, 2 * 512], BF16))
        f1gps = [esf.enter_context(nc.psum_tensor("f1_g0", [128, 512], F32)),
                 esf.enter_context(nc.psum_tensor("f1_g1", [128, 512], F32))]
        f1ups = [esf.enter_context(nc.psum_tensor("f1_u0", [128, 512], F32)),
                 esf.enter_context(nc.psum_tensor("f1_u1", [128, 512], F32))]
        S.ph = Ph(S)
        for _ in gen_f1(nc, S, ext, scr, scal, a3out, prep1_out,
                        wgb, wub, lx, lxn, sgb, hb,
                        f1gps, f1ups, e_fin,
                        e_fin, f1out):
            pull(prep2_g, 1)
        drain(prep2_g)
        S.emit()
        esf.close()
        esd.close()
        es2.close()
        prep2_fin = [("pT", S.cnt["pT"]), ("pV", S.cnt["pV"]),
                     ("pS", S.cnt["pS"]),
                     ("qPS0", S.cnt["qPS0"]), ("qPS1", S.cnt["qPS1"]),
                     ("qPW0", S.cnt["qPW0"]), ("qPW1", S.cnt["qPW1"]),
                     ("qPL0", S.cnt["qPL0"]), ("qPL1", S.cnt["qPL1"])]
        f_fin = S.fin()

        phase_f2(nc, S, ext, scr, trash, eps_col, f1out, prep2_out, a3out,
                 f_fin + prep2_fin)

    return nc


def phase_p0(nc, S, ext, eye_bf, eye_f32, ones_col, ones_row, bqkv_sb, bq32, bv_bc, eps_col):
    with (
        nc.sbuf_tensor("p0stg", [128, 128], F32) as stg,
        nc.sbuf_tensor("bvrow", [1, 1024], F32) as bvrow,
        nc.psum_tensor("p0ps", [128, 512], F32) as ps,
    ):
        S.ph = ph = Ph(S)
        ld1 = ph.add("sync", lambda e: e.dma_start(stg[:, :], ext["eye"]), inc="sLd")
        ph.add("vector", lambda e: e.tensor_copy(eye_bf[:, :], stg[:, :]),
               waits=[("sLd", ld1)], inc="sV")
        ph.add("vector", lambda e: e.tensor_copy(eye_f32[:, :], stg[:, :]),
               waits=[("sLd", ld1)], inc="sV")
        ld2 = ph.add("sync", lambda e: e.dma_start(
            bqkv_sb[:, :], ext["bqkv"].rearrange("(i p) -> p i", p=128),
            allow_slow_non_contiguous=True), inc="sLd")
        ph.add("scalar", lambda e: e.mul(bq32[:, :], bqkv_sb[:, 0:8], QSCALE),
               waits=[("sLd", ld2)], inc="sS")
        ld3 = ph.add("sync", lambda e: e.dma_start(
            bvrow[:, :], ext["bqkv"][2048:3072].rearrange("(o n) -> o n", o=1)), inc="sLd")
        ph.add("vector", lambda e: e.memset(ones_col[:, :], 1.0), inc="sV")
        ph.add("vector", lambda e: e.memset(eps_col[:, :], EPS), inc="sV")
        vo = ph.add("vector", lambda e: e.memset(ones_row[:, :], 1.0), inc="sV")
        for h in range(2):
            t = ph.add("tensor", lambda e, h=h: e.matmul(
                ps[:, 0:512],
                ones_row[:, :], bvrow[:, h * 512:(h + 1) * 512], start=True, stop=True),
                waits=[("sLd", ld3), ("sV", vo), ("sS", S.cnt["sS"])], inc="sT")
            ph.add("scalar", lambda e, h=h: e.copy(bv_bc[:, h * 512:(h + 1) * 512], ps[:, 0:512]),
                   waits=[("sT", t)], inc="sS")
        S.emit()
        return [("sT", S.cnt["sT"]), ("sS", S.cnt["sS"]), ("sV", S.cnt["sV"])]


def emit_ternary(ph, src_ap, rows, width, absb, sgnb, ternb, thr_ap, wait_ld,
                 wait_tern=()):
    sA = ph.add("scalar", lambda e, s=src_ap, r=rows, w=width:
                e.activation(absb[0:r, 0:w], s, AF.Abs), waits=wait_ld, inc="pS")
    sG = ph.add("scalar", lambda e, s=src_ap, r=rows, w=width:
                e.sign(sgnb[0:r, 0:w], s), waits=wait_ld, inc="pS")
    vM = ph.add("vector", lambda e, r=rows, w=width, thr=thr_ap:
                e.tensor_single_scalar(absb[0:r, 0:w], absb[0:r, 0:w], thr[0:r, :],
                                       ALU.is_gt),
                waits=[("pS", sA)], inc="pV")
    vT = ph.add("vector", lambda e, r=rows, w=width:
                e.tensor_mul(ternb[0:r, 0:w], absb[0:r, 0:w], sgnb[0:r, 0:w]),
                waits=[("pS", sG), ("pV", vM)] + list(wait_tern), inc="pV")
    return vT, sG


def _scale_units(nc, S, ext, B, wname, part_c0, st):
    CHW = 2048
    NPER = LFFN // 128
    chunks = [(i * CHW, min(NPER, (i + 1) * CHW)) for i in range((NPER + CHW - 1) // CHW)]
    flat = ext[wname].rearrange("a b c -> (a b c)").rearrange("(p n) -> p n", p=128)
    for ci, (c0, c1) in enumerate(chunks):
        b = ci % 2
        ph = S.ph
        ld = ph.add("sync", lambda e, b=b, c0=c0, c1=c1, flat=flat:
                    e.dma_start(B["stg"][:, b * 2048:b * 2048 + (c1 - c0)], flat[:, c0:c1]),
                    waits=[("pV", st["lastred"][b])] +
                          (list(st.get("guard") or []) if ci == 0 else []),
                    inc="qPL%d" % b)
        st["lastred"][b] = ph.add("vector", lambda e, b=b, w=c1 - c0, col=part_c0 + ci:
                                  e.reduce_sum(B["part"][:, col:col + 1],
                                               B["stg"][:, b * 2048:b * 2048 + w],
                                               AX.X, apply_absolute_value=True),
                                  waits=[("qPL%d" % b, ld)], inc="pV")
        yield
    st["nch"] = len(chunks)


def _scale_final(nc, S, B, scal, eye_f32, ones_row, mi, part_c0, nch, st,
                 col_thr, col_sc):
    ph = S.ph
    vacc = ph.add("vector", lambda e, mi=mi, c0=part_c0, n=nch:
                  e.reduce_sum(B["acc"][:, mi:mi + 1], B["part"][:, c0:c0 + n], AX.X),
                  waits=[("pV", st["lastred"][0]), ("pV", st["lastred"][1])], inc="pV")
    t1 = ph.add("tensor", lambda e, mi=mi:
                e.transpose(B["ps"][0:1, 0:128], B["acc"][:, mi:mi + 1], eye_f32[:, :]),
                waits=[("pV", vacc), ("pV", st["psrd"])], inc="pT")
    vt = ph.add("vector", lambda e, mi=mi:
                e.reduce_sum(B["tot"][0:1, mi:mi + 1], B["ps"][0:1, 0:128], AX.X),
                waits=[("pT", t1)], inc="pV")
    st["psrd"] = vt
    t2 = ph.add("tensor", lambda e, mi=mi:
                e.matmul(B["ps2"][:, mi:mi + 1], ones_row[:, :], B["tot"][0:1, mi:mi + 1],
                         start=True, stop=True),
                waits=[("pV", vt)], inc="pT")
    sc = ph.add("scalar", lambda e, mi=mi, col=col_thr:
                e.mul(scal[:, col:col + 1], B["ps2"][:, mi:mi + 1], 0.5 / LFFN),
                waits=[("pT", t2)], inc="pS")
    if col_sc is not None:
        sc = ph.add("scalar", lambda e, mi=mi, col=col_sc:
                    e.mul(scal[:, col:col + 1], B["ps2"][:, mi:mi + 1], 1.0 / LFFN),
                    waits=[("pT", t2)], inc="pS")
    return sc


GU_BATCHES = [[0, 128], [256, 384], [512, 640], [768, 896], [1024, 1152], [1238]]


def gen_prep1(nc, S, ext, scr, B, scal, eye_bf, eye_f32, ones_row, out, guard):
    st = {"lastred": {0: 0, 1: 0}, "psrd": 0, "nch": 0, "guard": guard}
    yield from _scale_units(nc, S, ext, B, "wg", 0, st)
    sc_g = _scale_final(nc, S, B, scal, eye_f32, ones_row, 0, 0, st["nch"], st, 0, 3)
    yield
    stu = {"lastred": dict(st["lastred"]), "psrd": st["psrd"], "nch": 0, "guard": []}
    yield from _scale_units(nc, S, ext, B, "wu", 24, stu)
    sc_u = _scale_final(nc, S, B, scal, eye_f32, ones_row, 1, 24, stu["nch"], stu, 1, 4)
    yield

    gu = {"stg_rd": {0: 0, 1: 0}, "prev_vT": 0, "tern_rd": {0: 0, 1: 0},
          "wst": {0: 0, 1: 0}, "ev_prev": {0: 0, 1: 0}, "bi": 0, "ti": 0}
    tps = B["tps"]

    def gu_batch(wname, dst, thrcol, j, rows, scale_ready):
        ph = S.ph
        bb = gu["bi"] % 2
        gu["bi"] += 1
        nt = len(rows)
        wid = nt * 512
        if nt == 2:
            ld = ph.add("sync", lambda e, bb=bb, w=wname, j=j, o0=rows[0]:
                        e.dma_start(B["win"][:, bb * 1024:bb * 1024 + 1024]
                                    .rearrange("p (t d) -> p t d", t=2),
                                    ext[w][j, o0:o0 + 256, :]
                                    .rearrange("(t p) d -> p t d", p=128)),
                        waits=[("pS", gu["stg_rd"][bb])], inc="qPW%d" % bb)
        else:
            ld = ph.add("sync", lambda e, bb=bb, w=wname, j=j, o0=rows[0]:
                        e.dma_start(B["win"][:, bb * 1024:bb * 1024 + 512],
                                    ext[w][j, o0:o0 + 128, :]),
                        waits=[("pS", gu["stg_rd"][bb])], inc="qPW%d" % bb)
        vT, sG = emit_ternary(ph, B["win"][:, bb * 1024:bb * 1024 + wid], 128, wid,
                              B["absb"], B["sgnb"], B["tern"][:, bb * 1024:bb * 1024 + wid],
                              scal[:, thrcol:thrcol + 1],
                              [("qPW%d" % bb, ld), ("pV", gu["prev_vT"]),
                               ("pS", scale_ready)],
                              [("pT", gu["tern_rd"][bb])])
        gu["prev_vT"] = vT
        gu["stg_rd"][bb] = sG
        tlast = None
        for ti, o0 in enumerate(rows):
            gu["ti"] += 1
            par = gu["ti"] % 2
            tcnt = None
            for dc in range(4):
                tcnt = ph.add("tensor", lambda e, par=par, bb=bb, ti=ti, dc=dc:
                              e.transpose(tps[par][:, dc * 128:(dc + 1) * 128],
                                          B["tern"][:, bb * 1024 + ti * 512 + dc * 128:
                                                    bb * 1024 + ti * 512 + (dc + 1) * 128],
                                          eye_bf[:, :]),
                              waits=([("pV", vT), ("pS", gu["ev_prev"][par])]
                                     if dc == 0 else []),
                              inc="pT" if dc == 3 else None)
            tlast = tcnt
            ob = gu["ti"] % 2
            ev = ph.add("scalar", lambda e, par=par, ob=ob: e.copy(
                B["outb"][:, ob * 512:(ob + 1) * 512], tps[par][:, :]),
                waits=[("pT", tcnt), ("qPS%d" % ob, gu["wst"][ob])], inc="pS")
            gu["ev_prev"][par] = ev
            gu["wst"][ob] = ph.add("gpsimd", lambda e, ob=ob, j=j, o0=o0, dst=dst:
                                   e.dma_start(
                                       dst.rearrange("(c p) o -> p c o", p=128)
                                       [:, j::8, o0:o0 + 128],
                                       B["outb"][:, ob * 512:(ob + 1) * 512]
                                       .rearrange("p (c o) -> p c o", c=4)),
                                   waits=[("pS", ev)], inc="qPS%d" % ob)
        gu["tern_rd"][bb] = tlast

    for j in range(8):
        for rows in GU_BATCHES:
            gu_batch("wg", scr["wgtS"], 0, j, rows, sc_g)
            yield
    out["wg_done"] = [("qPS0", S.cnt["qPS0"]), ("qPS1", S.cnt["qPS1"])]
    for j in range(8):
        for rows in GU_BATCHES:
            gu_batch("wu", scr["wutS"], 1, j, rows, sc_u)
            yield
    out["wu_done"] = [("qPS0", S.cnt["qPS0"]), ("qPS1", S.cnt["qPS1"])]


def gen_prep2(nc, S, ext, scr, B, scal, eye_f32, ones_row, out, guard):
    eye_bf = B["eye_bf"]
    st = {"lastred": {0: 0, 1: 0}, "psrd": 0, "nch": 0, "guard": guard}
    yield from _scale_units(nc, S, ext, B, "wd", 0, st)
    sc_d = _scale_final(nc, S, B, scal, eye_f32, ones_row, 2, 0, st["nch"], st, 2, None)
    yield

    tps = B["tps"]
    dd = {"stg_rd": {0: 0, 1: 0}, "prev_vT": 0, "tern_rd": {0: 0, 1: 0},
          "dst": {0: 0, 1: 0}, "ev_prev": {0: 0, 1: 0}, "mzo": 0}
    dd["mzo"] = S.ph.add("vector", lambda e: e.memset(B["outb"][:, :], 0.0), inc="pV")

    def d_batch(j, half, scale_ready):
        ph = S.ph
        bb = half % 2
        oc0 = half * 2
        ld = ph.add("sync", lambda e, bb=bb, j=j, oc0=oc0:
                    e.dma_start(B["win"][:, bb * 2 * H8:(bb + 1) * 2 * H8]
                                .rearrange("p (t d) -> p t d", t=2),
                                ext["wd"][j, oc0 * 128:(oc0 + 2) * 128, :]
                                .rearrange("(t p) d -> p t d", p=128)),
                    waits=[("pS", dd["stg_rd"][bb])], inc="qPW%d" % bb)
        vT, sG = emit_ternary(ph, B["win"][:, bb * 2 * H8:(bb + 1) * 2 * H8], 128, 2 * H8,
                              B["absb"], B["sgnb"],
                              B["tern"][:, bb * 2 * H8:(bb + 1) * 2 * H8],
                              scal[:, 2:3],
                              [("qPW%d" % bb, ld), ("pV", dd["prev_vT"]),
                               ("pS", scale_ready), ("pV", dd["mzo"])],
                              [("pT", dd["tern_rd"][bb])])
        dd["prev_vT"] = vT
        dd["stg_rd"][bb] = sG
        tlast = None
        for toc in range(2):
            oc = oc0 + toc
            ob = oc % 2
            evs = []
            for pf in range(3):
                ndc = 4 if pf < 2 else 3
                par = pf % 2
                tcnt = None
                for q in range(ndc):
                    dc = pf * 4 + q
                    w = WD_W[dc]
                    tcnt = ph.add("tensor", lambda e, par=par, bb=bb, toc=toc, q=q, dc=dc, w=w:
                                  e.transpose(tps[par][0:w, q * 128:q * 128 + 128],
                                              B["tern"][:, bb * 2 * H8 + toc * H8 + dc * 128:
                                                        bb * 2 * H8 + toc * H8 + dc * 128 + w],
                                              eye_bf[:, :]),
                                  waits=([("pV", vT), ("pS", dd["ev_prev"][par])]
                                         if q == 0 else []),
                                  inc="pT" if q == ndc - 1 else None)
                ev = None
                for q in range(ndc):
                    dc = pf * 4 + q
                    w = WD_W[dc]
                    ev = ph.add("scalar", lambda e, ob=ob, dc=dc, q=q, w=w, par=par:
                                e.copy(B["outb"][0:w, ob * 1408 + dc * 128:
                                                 ob * 1408 + (dc + 1) * 128],
                                       tps[par][0:w, q * 128:(q + 1) * 128]),
                                waits=([("pT", tcnt), ("pV", dd["mzo"]),
                                        ("qPS%d" % ob, dd["dst"][ob])] if q == 0 else []),
                                inc="pS")
                dd["ev_prev"][par] = ev
                evs.append(ev)
                tlast = tcnt
            dd["dst"][ob] = ph.add("gpsimd", lambda e, ob=ob, j=j, oc=oc:
                                   e.dma_start(
                                       scr["wdtS"].rearrange("(c p) o -> p c o", p=128)
                                       [:, j * 11:(j + 1) * 11, oc * 128:(oc + 1) * 128],
                                       B["outb"][:, ob * 1408:(ob + 1) * 1408]
                                       .rearrange("p (c o) -> p c o", c=11)),
                                   waits=[("pS", evs[-1])], inc="qPS%d" % ob)
        dd["tern_rd"][bb] = tlast

    for j in range(8):
        for half in range(2):
            d_batch(j, half, sc_d)
            yield
    out["wd_done"] = [("qPS0", S.cnt["qPS0"]), ("qPS1", S.cnt["qPS1"])]


def gen_a1k(nc, S, ext, scr, eye_bf, bqkv_sb, trash, eps_col,
            wk, wstg, xb, lnx, lnxt, ktb, stats, kps, tps, p0fin, out):
    ph = S.ph
    cast = {0: 0, 1: 0}
    for fc in range(FC):
        b = fc % 2
        ld = ph.add("sync", lambda e, b=b, fc=fc: e.dma_start(
            wstg[:, b * 1024:(b + 1) * 1024],
            ext["wqkv"][fc * 128:(fc + 1) * 128, 1024:2048]),
            waits=[("sV", cast[b])] + (p0fin if fc == 0 else []), inc="sLd")
        cast[b] = ph.add("vector", lambda e, b=b, fc=fc: e.tensor_copy(
            wk[:, fc * 1024:(fc + 1) * 1024], wstg[:, b * 1024:(b + 1) * 1024]),
            waits=[("sLd", ld)], inc="sV")
    xread_v = {0: 0, 1: 0}
    xread_s = {0: 0, 1: 0}
    lnx_rd_t = {0: 0, 1: 0}
    tp_ev = {0: 0, 1: 0}
    kev = {0: 0, 1: 0}
    kst = {0: 0, 1: 0}
    lnxt_store = 0
    kmm_done = 0
    out["lnxt_store_g"] = {}
    for g in range(NG):
        ev_last = 0
        for t in range(4):
            tile = g * 4 + t
            b = tile % 2
            ld = ph.add("sync", lambda e, b=b, tile=tile: e.dma_start(
                xb[:, b * 4096:(b + 1) * 4096],
                ext["xseq"][tile * 128:(tile + 1) * 128, :]),
                waits=[("sV", xread_v[b]), ("sS", xread_s[b])], inc="sLd")
            st = stats[:, (tile % 2) * 8:(tile % 2) * 8 + 8]
            vdone, s1 = emit_ln(ph, xb[:, b * 4096:(b + 1) * 4096],
                                lnx[:, b * 4096:(b + 1) * 4096], st, trash[:, :],
                                [("sLd", ld), ("sT", lnx_rd_t[b])], eps_ap=eps_col[:, :])
            xread_v[b], xread_s[b] = vdone, s1
            for pf in range(8):
                par = pf % 2
                tcnt = None
                for q in range(4):
                    fc = pf * 4 + q
                    tcnt = ph.add("tensor",
                                  lambda e, par=par, q=q, fc=fc, b=b:
                                  e.transpose(tps[par][:, q * 128:(q + 1) * 128],
                                              lnx[:, b * 4096 + fc * 128:b * 4096 + (fc + 1) * 128],
                                              eye_bf[:, :]),
                                  waits=([("sV", vdone), ("sS", tp_ev[par])] if q == 0 else []),
                                  inc="sT" if q == 3 else None)
                ev = ph.add("scalar",
                            lambda e, par=par, pf=pf, t=t:
                            e.copy(lnxt.rearrange("p (f q) -> p f q", q=512)
                                   [:, pf * 4:(pf + 1) * 4, t * 128:(t + 1) * 128],
                                   tps[par][:, :].rearrange("p (f q) -> p f q", q=128)),
                            waits=[("sT", tcnt), ("sT", kmm_done), ("sSt", lnxt_store)],
                            inc="sS")
                tp_ev[par] = ev
                ev_last = ev
            lnx_rd_t[b] = S.cnt["sT"]
            yield
        for dc in range(DC):
            par = dc % 2
            mm = None
            for fc in range(FC):
                mm = ph.add("tensor", lambda e, par=par, dc=dc, fc=fc:
                            e.matmul(kps[par][:, :],
                                     wk[:, fc * 1024 + dc * 128:fc * 1024 + (dc + 1) * 128],
                                     lnxt[:, fc * 512:(fc + 1) * 512],
                                     start=(fc == 0), stop=(fc == FC - 1)),
                            waits=([("sS", ev_last), ("sV", kev[par]),
                                    ("sV", cast[0]), ("sV", cast[1])] if fc == 0 else []),
                            inc="sT" if fc == FC - 1 else None)
            kev[par] = ph.add("vector", lambda e, par=par, dc=dc:
                              e.tensor_scalar(ktb[:, par * 512:(par + 1) * 512],
                                              kps[par][:, :], bqkv_sb[:, 8 + dc:9 + dc], None,
                                              ALU.add),
                              waits=[("sT", mm), ("sSt", kst[par])], inc="sV")
            kst[par] = ph.add("gpsimd", lambda e, par=par, dc=dc, g=g:
                              e.dma_start(scr["ktS"][dc * 128:(dc + 1) * 128,
                                                     g * 512:(g + 1) * 512],
                                          ktb[:, par * 512:(par + 1) * 512]),
                              waits=[("sV", kev[par])], inc="sSt")
        kmm_done = S.cnt["sT"]
        lnxt_store = ph.add("gpsimd", lambda e, g=g:
                            e.dma_start(scr["lnxtS"][g, :, :], lnxt[:, :]),
                            waits=[("sS", ev_last)], inc="sSt")
        out["lnxt_store_g"][g] = lnxt_store
        yield
    out["ktS_done"] = S.cnt["sSt"]
    ph.add("gpsimd", lambda e: e.engine_nop(), waits=[("sSt", S.cnt["sSt"])])


def gen_a1v(nc, S, ext, scr, bv_bc, wv, wstg, lnxt, vtb, vps, out):
    ph = S.ph
    cast = {0: 0, 1: 0}
    for fc in range(FC):
        b = fc % 2
        ld = ph.add("sync", lambda e, b=b, fc=fc: e.dma_start(
            wstg[:, b * 1024:(b + 1) * 1024],
            ext["wqkv"][fc * 128:(fc + 1) * 128, 2048:3072]),
            waits=[("sV", cast[b])], inc="qVW%d" % b)
        cast[b] = ph.add("vector", lambda e, b=b, fc=fc: e.tensor_copy(
            wv[:, fc * 1024:(fc + 1) * 1024], wstg[:, b * 1024:(b + 1) * 1024]),
            waits=[("qVW%d" % b, ld)], inc="sV")
        if fc % 8 == 7:
            yield
    gdone = {0: 0, 1: 0}
    vev = {0: 0, 1: 0}
    vst = {0: 0, 1: 0}
    for g in range(NG):
        gb = g % 2
        ld = ph.add("sync", lambda e, gb=gb, g=g: e.dma_start(
            lnxt[:, gb * FC * 512:(gb + 1) * FC * 512], scr["lnxtS"][g, :, :]),
            waits=[("sT", gdone[gb]), ("sSt", out["lnxt_store_g"][g])],
            inc="qVL%d" % gb)
        for t in range(4):
            for half in range(2):
                par = (t * 2 + half) % 2
                mm = None
                for fc in range(FC):
                    mm = ph.add("tensor", lambda e, par=par, gb=gb, t=t, half=half, fc=fc:
                                e.matmul(vps[par][:, :],
                                         lnxt[:, gb * FC * 512 + fc * 512 + t * 128:
                                              gb * FC * 512 + fc * 512 + (t + 1) * 128],
                                         wv[:, fc * 1024 + half * 512:fc * 1024 + (half + 1) * 512],
                                         start=(fc == 0), stop=(fc == FC - 1)),
                                waits=([("qVL%d" % gb, ld), ("sV", vev[par]),
                                        ("sV", cast[0]), ("sV", cast[1])] if fc == 0 else []),
                                inc="sT" if fc == FC - 1 else None)
                vev[par] = ph.add("vector", lambda e, par=par, half=half:
                                  e.tensor_add(vtb[:, par * 512:(par + 1) * 512],
                                               vps[par][:, :],
                                               bv_bc[:, half * 512:(half + 1) * 512]),
                                  waits=[("sT", mm), ("qVS%d" % par, vst[par])], inc="sV")
                vst[par] = ph.add("gpsimd", lambda e, par=par, g=g, t=t, half=half:
                                  e.dma_start(scr["vS"][(g * 4 + t) * 128:(g * 4 + t + 1) * 128,
                                                        half * 512:(half + 1) * 512],
                                              vtb[:, par * 512:(par + 1) * 512]),
                                  waits=[("sV", vev[par])], inc="qVS%d" % par)
            yield
        gdone[gb] = S.cnt["sT"]
    out["vS_done"] = [("qVS0", S.cnt["qVS0"]), ("qVS1", S.cnt["qVS1"])]
    ph.add("gpsimd", lambda e: e.engine_nop(),
           waits=[("sSt", S.cnt["sSt"])] + out["vS_done"])


def phase_a1q(nc, S, ext, scr, bqkv_sb, QT, wq, wstg, lnxt, qps, a1out, guard):
    ph = S.ph
    cast = {0: 0, 1: 0}
    for fc in range(FC):
        for hh in range(2):
            b = (2 * fc + hh) % 2
            ld = ph.add("sync", lambda e, b=b, fc=fc, hh=hh: e.dma_start(
                wstg[:, b * 512:(b + 1) * 512],
                ext["wqkv"][fc * 128:(fc + 1) * 128, hh * 512:(hh + 1) * 512]),
                waits=[("sV", cast[b])] + (guard if fc == 0 and hh == 0 else []),
                inc="sLd")
            cast[b] = ph.add("vector", lambda e, b=b, fc=fc, hh=hh: e.tensor_copy(
                wq[:, fc * 1024 + hh * 512:fc * 1024 + (hh + 1) * 512],
                wstg[:, b * 512:(b + 1) * 512]),
                waits=[("sLd", ld)], inc="sV")
    lds = {}
    for g in range(OG):
        lds[g] = ph.add("sync", lambda e, g=g: e.dma_start(
            lnxt[:, g * FC * 512:(g + 1) * FC * 512], scr["lnxtS"][g, :, :]),
            waits=[("sSt", a1out["lnxt_store_g"][g])], inc="sLd")
    qev = {0: 0, 1: 0}
    for g in range(OG):
        for dc in range(DC):
            par = dc % 2
            mm = None
            for fc in range(FC):
                mm = ph.add("tensor", lambda e, par=par, g=g, dc=dc, fc=fc:
                            e.matmul(qps[par][:, :],
                                     wq[:, fc * 1024 + dc * 128:fc * 1024 + (dc + 1) * 128],
                                     lnxt[:, g * FC * 512 + fc * 512:
                                          g * FC * 512 + (fc + 1) * 512],
                                     start=(fc == 0), stop=(fc == FC - 1)),
                            waits=([("sLd", lds[g]), ("sV", qev[par]),
                                    ("sV", cast[0]), ("sV", cast[1])] if fc == 0 else []),
                            inc="sT" if fc == FC - 1 else None)
            qev[par] = ph.add("vector", lambda e, par=par, dc=dc, g=g:
                              e.tensor_scalar(QT[:, dc * 1024 + g * 512:dc * 1024 + (g + 1) * 512],
                                              qps[par][:, :], bqkv_sb[:, dc:dc + 1], QSCALE,
                                              ALU.add, ALU.mult),
                              waits=[("sT", mm)], inc="sV")


def phase_a2(nc, S, scr, eye_bf, trash, QT, oaT, a1out, guard):
    with (
        nc.sbuf_tensor("KT", [128, DC * SEQ], BF16) as KT,
        nc.sbuf_tensor("Vb", [128, KCH * 1024], BF16) as Vb,
        nc.sbuf_tensor("scoresF", [128, SEQ], F32) as scoresF,
        nc.sbuf_tensor("Pbf", [128, SEQ], BF16) as Pbf,
        nc.sbuf_tensor("PT", [128, SEQ], BF16) as PT,
        nc.sbuf_tensor("oa", [128, 1024], BF16) as oa,
        nc.sbuf_tensor("a2_st", [128, 8], F32) as stats,
        nc.psum_tensor("a2_s0", [128, 512], F32) as s0,
        nc.psum_tensor("a2_s1", [128, 512], F32) as s1,
        nc.psum_tensor("a2_tp0", [128, 512], BF16) as tp0,
        nc.psum_tensor("a2_tp1", [128, 512], BF16) as tp1,
        nc.psum_tensor("a2_pv0", [128, 512], F32) as pv0,
        nc.psum_tensor("a2_pv1", [128, 512], F32) as pv1,
    ):
        sps = [s0, s1]
        tps = [tp0, tp1]
        pvs = [pv0, pv1]
        S.ph = ph = Ph(S)
        ldk = ph.add("sync", lambda e: e.dma_start(
            KT.rearrange("p (d k) -> p d k", k=SEQ),
            scr["ktS"].rearrange("(d p) k -> p d k", p=128)),
            waits=[("sSt", a1out["ktS_done"])] + guard, inc="sLd")
        ldv = ph.add("sync", lambda e: e.dma_start(
            Vb.rearrange("p (k d) -> p k d", d=1024),
            scr["vS"].rearrange("(k p) d -> p k d", p=128)),
            waits=list(a1out["vS_done"]), inc="sLd")
        sev = {0: 0, 1: 0}
        exp_prev = 0
        pt_tr_prev = 0
        pv_prev = 0
        oa_tr_prev = 0
        ptev = {0: 0, 1: 0}
        for qt in range(NQT):
            stp = stats[:, (qt % 2) * 4:(qt % 2) * 4 + 4]
            ev8 = 0
            for kg in range(8):
                par = kg % 2
                mm = None
                for dc in range(DC):
                    mm = ph.add("tensor", lambda e, par=par, qt=qt, kg=kg, dc=dc:
                                e.matmul(sps[par][:, :],
                                         QT[:, dc * 1024 + qt * 128:dc * 1024 + qt * 128 + 128],
                                         KT[:, dc * SEQ + kg * 512:dc * SEQ + (kg + 1) * 512],
                                         start=(dc == 0), stop=(dc == DC - 1)),
                                waits=([("sLd", ldk), ("sV", sev[par]),
                                        ("sS", exp_prev)] if dc == 0 else []),
                                inc="sT" if dc == DC - 1 else None)
                sev[par] = ph.add("vector", lambda e, par=par, kg=kg:
                                  e.tensor_copy(scoresF[:, kg * 512:(kg + 1) * 512], sps[par][:, :]),
                                  waits=[("sT", mm)], inc="sV")
                ev8 = sev[par]
            vmax = ph.add("vector", lambda e, stp=stp:
                          e.reduce_max(stp[:, 0:1], scoresF[:, :], AX.X, negate=True),
                          waits=[("sV", ev8)], inc="sV")
            exp_prev = ph.add("scalar", lambda e, stp=stp:
                              e.activation(Pbf[:, :], scoresF[:, :], AF.Exp,
                                           bias=stp[:, 0:1], scale=1.0,
                                           accum_out=stp[:, 1:2]),
                              waits=[("sV", vmax), ("sT", pt_tr_prev)], inc="sS")
            vrec = ph.add("vector", lambda e, stp=stp:
                          e.reciprocal(stp[:, 2:3], stp[:, 1:2]),
                          waits=[("sS", exp_prev)], inc="sV")
            ptev_last = 0
            for pf in range(8):
                par = pf % 2
                tcnt = None
                for q in range(4):
                    kc = pf * 4 + q
                    tcnt = ph.add("tensor", lambda e, par=par, q=q, kc=kc:
                                  e.transpose(tps[par][:, q * 128:(q + 1) * 128],
                                              Pbf[:, kc * 128:(kc + 1) * 128], eye_bf[:, :]),
                                  waits=([("sS", exp_prev), ("sS", ptev[par])] if q == 0 else []),
                                  inc="sT" if q == 3 else None)
                ptev[par] = ph.add("scalar", lambda e, par=par, pf=pf:
                                   e.copy(PT[:, pf * 512:(pf + 1) * 512], tps[par][:, :]),
                                   waits=[("sT", tcnt), ("sT", pv_prev)], inc="sS")
                ptev_last = ptev[par]
            oaev = 0
            for half in range(2):
                mm = None
                for kc in range(KCH):
                    mm = ph.add("tensor", lambda e, half=half, kc=kc:
                                e.matmul(pvs[half][:, :],
                                         PT[:, kc * 128:(kc + 1) * 128],
                                         Vb[:, kc * 1024 + half * 512:kc * 1024 + (half + 1) * 512],
                                         start=(kc == 0), stop=(kc == KCH - 1)),
                                waits=([("sLd", ldv), ("sS", ptev_last),
                                        ("sT", oa_tr_prev)] if kc == 0 else []),
                                inc="sT" if kc == KCH - 1 else None)
                oaev = ph.add("vector", lambda e, half=half, stp=stp:
                              e.tensor_scalar_mul(oa[:, half * 512:(half + 1) * 512],
                                                  pvs[half][:, :], stp[:, 2:3]),
                              waits=[("sT", mm), ("sV", vrec)], inc="sV")
            pv_prev = S.cnt["sT"]
            for pf in range(2):
                par = pf % 2
                tcnt = None
                for q in range(4):
                    dvc = pf * 4 + q
                    tcnt = ph.add("tensor", lambda e, par=par, q=q, dvc=dvc:
                                  e.transpose(tps[par][:, q * 128:(q + 1) * 128],
                                              oa[:, dvc * 128:(dvc + 1) * 128], eye_bf[:, :]),
                                  waits=([("sV", oaev), ("sS", ptev[par])] if q == 0 else []),
                                  inc="sT" if q == 3 else None)
                ptev[par] = ph.add("scalar", lambda e, par=par, pf=pf, qt=qt:
                                   e.copy(oaT.rearrange("p (d q) -> p d q", q=1024)
                                          [:, pf * 4:(pf + 1) * 4, qt * 128:(qt + 1) * 128],
                                          tps[par][:, :].rearrange("p (d q) -> p d q", q=128)),
                                   waits=[("sT", tcnt)], inc="sS")
            oa_tr_prev = S.cnt["sT"]
            pt_tr_prev = oa_tr_prev
        S.emit()


def phase_a3(nc, S, ext, scr, ones_row, trash, oaT, eps_col, eye_bf, guard):
    out = {}
    with (
        nc.sbuf_tensor("wo_sb", [128, DC * 4096], BF16) as wo_sb,
        nc.sbuf_tensor("a3_wstg", [128, 1024], F32) as wstg,
        nc.sbuf_tensor("bo_row", [1, 4096], F32) as bo_row,
        nc.sbuf_tensor("a3_attn", [128, 2 * 4096], F32) as attn,
        nc.sbuf_tensor("a3_x", [128, 4096], F32) as xb,
        nc.sbuf_tensor("a3_lnx2", [128, 4096], BF16) as lnx2,
        nc.sbuf_tensor("a3_lt", [128, FC * 128], BF16) as ltb,
        nc.sbuf_tensor("a3_st", [128, 24], F32) as stats,
        nc.psum_tensor("a3_p0", [128, 512], F32) as p0,
        nc.psum_tensor("a3_p1", [128, 512], F32) as p1,
        nc.psum_tensor("a3_tp0", [128, 512], BF16) as tp0,
        nc.psum_tensor("a3_tp1", [128, 512], BF16) as tp1,
    ):
        pps = [p0, p1]
        tps = [tp0, tp1]
        S.ph = ph = Ph(S)
        ldb = ph.add("sync", lambda e: e.dma_start(
            bo_row[:, :], ext["bo"].rearrange("(o n) -> o n", o=1)),
            waits=guard, inc="sLd")
        cast = 0
        for dvc in range(DC):
            for hh in range(4):
                ld = ph.add("sync", lambda e, dvc=dvc, hh=hh: e.dma_start(
                    wstg[:, :], ext["wo"][dvc * 128:(dvc + 1) * 128,
                                          hh * 1024:(hh + 1) * 1024]),
                    waits=[("sV", cast)], inc="sLd")
                cast = ph.add("vector", lambda e, dvc=dvc, hh=hh: e.tensor_copy(
                    wo_sb[:, dvc * 4096 + hh * 1024:dvc * 4096 + (hh + 1) * 1024],
                    wstg[:, :]),
                    waits=[("sLd", ld)], inc="sV")
        a2_sS = S.cnt["sS"]
        st = {"aev": {0: 0, 1: 0}, "attn_v": {0: 0, 1: 0}, "attn_s": {0: 0, 1: 0},
              "x2_st": 0, "xread_v": 0, "xread_s": 0, "ldx": {}, "tp_ev": {0: 0, 1: 0},
              "lt_st": 0, "tr_done": 0}

        def stage_a(qt):
            b = qt % 2
            for cg in range(8):
                par = cg % 2
                mm = ph.add("tensor", lambda e, par=par, cg=cg: e.matmul(
                    pps[par][:, :], ones_row[:, :], bo_row[:, cg * 512:(cg + 1) * 512],
                    start=True, stop=False),
                    waits=[("sLd", ldb), ("sV", st["aev"][par]), ("sV", cast),
                           ("sS", a2_sS)],
                    inc=None)
                for dvc in range(DC):
                    mm = ph.add("tensor", lambda e, par=par, qt=qt, cg=cg, dvc=dvc:
                                e.matmul(pps[par][:, :],
                                         oaT[:, dvc * 1024 + qt * 128:dvc * 1024 + qt * 128 + 128],
                                         wo_sb[:, dvc * 4096 + cg * 512:dvc * 4096 + (cg + 1) * 512],
                                         start=False, stop=(dvc == DC - 1)),
                                waits=[],
                                inc="sT" if dvc == DC - 1 else None)
                st["aev"][par] = ph.add("vector", lambda e, par=par, cg=cg, b=b:
                                        e.tensor_copy(attn[:, b * 4096 + cg * 512:
                                                           b * 4096 + (cg + 1) * 512],
                                                      pps[par][:, :]),
                                        waits=[("sT", mm), ("sV", st["attn_v"][b]),
                                               ("sS", st["attn_s"][b])], inc="sV")

        def stage_b(qt):
            b = qt % 2
            st["ldx"][qt] = ph.add("sync", lambda e, qt=qt: e.dma_start(
                xb[:, :], ext["xown"][qt * 128:(qt + 1) * 128, :]),
                waits=([("sV", st["xread_v"]), ("sS", st["xread_s"]),
                        ("sSt", st["x2_st"])] if qt > 0 else []),
                inc="sLd")
            abuf = attn[:, b * 4096:(b + 1) * 4096]
            stA = stats[:, b * 8:b * 8 + 8]
            vdone, s1 = emit_ln(ph, abuf, abuf, stA, trash[:, :],
                                [("sV", st["aev"][0]), ("sV", st["aev"][1])],
                                eps_ap=eps_col[:, :])
            vadd = ph.add("vector", lambda e, abuf=abuf: e.tensor_add(xb[:, :], abuf, xb[:, :]),
                          waits=[("sLd", st["ldx"][qt]), ("sSt", st["x2_st"]),
                                 ("sV", vdone)], inc="sV")
            st["attn_v"][b] = vadd
            st["attn_s"][b] = s1
            st["x2_st"] = ph.add("gpsimd", lambda e, qt=qt: e.dma_start(
                scr["x2S"][qt * 128:(qt + 1) * 128, :], xb[:, :]),
                waits=[("sV", vadd)], inc="sSt")
            v2done, s2 = emit_ln(ph, xb[:, :], lnx2[:, :], stats[:, 16:24], trash[:, :],
                                 [("sV", vadd), ("sT", st["tr_done"])],
                                 eps_ap=eps_col[:, :])
            st["xread_v"] = v2done
            st["xread_s"] = s2
            for pf in range(8):
                par = pf % 2
                tcnt = None
                for q in range(4):
                    fc = pf * 4 + q
                    tcnt = ph.add("tensor", lambda e, par=par, q=q, fc=fc:
                                  e.transpose(tps[par][:, q * 128:(q + 1) * 128],
                                              lnx2[:, fc * 128:(fc + 1) * 128], eye_bf[:, :]),
                                  waits=([("sV", v2done), ("sS", st["tp_ev"][par])]
                                         if q == 0 else []),
                                  inc="sT" if q == 3 else None)
                st["tp_ev"][par] = ph.add(
                    "scalar", lambda e, par=par, pf=pf:
                    e.copy(ltb[:, pf * 512:(pf + 1) * 512], tps[par][:, :]),
                    waits=[("sT", tcnt), ("sSt", st["lt_st"])], inc="sS")
            st["tr_done"] = S.cnt["sT"]
            st["lt_st"] = ph.add("gpsimd", lambda e, qt=qt: e.dma_start(
                scr["lnx2tS"][:, :, qt * 128:(qt + 1) * 128]
                .rearrange("c p t -> p c t"),
                ltb.rearrange("p (c t) -> p c t", t=128)),
                waits=[("sS", st["tp_ev"][0]), ("sS", st["tp_ev"][1])], inc="sSt")
            if qt == 3:
                out["lt_store_g0"] = S.cnt["sSt"]
            if qt == 7:
                out["lt_store_g1"] = S.cnt["sSt"]

        stage_a(0)
        for qt in range(NQT):
            if qt + 1 < NQT:
                stage_a(qt + 1)
            stage_b(qt)
        out["x2_all"] = st["x2_st"]
        ph.add("gpsimd", lambda e: e.engine_nop(), waits=[("sSt", S.cnt["sSt"])])
        S.emit()
    return out


HCW = [128] * 10 + [86]


def gen_f1(nc, S, ext, scr, scal, a3out, prep1_out, wgb, wub, lx, lxn, sgb, hb,
           gps, ups, e_fin, w_guard, out):
    ph = S.ph
    ldg = {0: 0, 1: 0}
    ldu = {0: 0, 1: 0}
    wfree = {0: 0, 1: 0}
    gev = {0: 0, 1: 0}
    uev = {0: 0, 1: 0}
    hst = {0: 0, 1: 0}
    slab_free = 0
    negdone = 0
    for g in range(2):
        ldslab = ph.add("sync", lambda e, g=g: e.dma_start(
            lx.rearrange("p (c t) -> p c t", t=512),
            scr["lnx2tS"][:, :, g * 512:(g + 1) * 512].rearrange("c p t -> p c t")),
            waits=[("sSt", a3out["lt_store_g0" if g == 0 else "lt_store_g1"]),
                   ("sT", slab_free), ("sS", negdone)] + (e_fin if g == 0 else []),
            inc="sLd")
        negdone = ph.add("scalar", lambda e: e.mul(lxn[:, :], lx[:, :], -1.0),
                         waits=[("sLd", ldslab), ("sT", slab_free)], inc="sS")
        for hcg in range(11):
            w = HCW[hcg]
            b = (g * 11 + hcg) % 2
            ldg[b] = ph.add("sync", lambda e, b=b, hcg=hcg, w=w: e.dma_start(
                wgb.rearrange("p (b c o) -> p b c o", b=2, o=128)[:, b, :, 0:w],
                scr["wgtS"][:, hcg * 128:hcg * 128 + w]
                .rearrange("(c p) o -> p c o", p=128)),
                waits=[("sT", wfree[b])] + prep1_out["wg_done"] +
                      (w_guard if g == 0 and hcg == 0 else []),
                inc="sLd")
            ldu[b] = ph.add("sync", lambda e, b=b, hcg=hcg, w=w: e.dma_start(
                wub.rearrange("p (b c o) -> p b c o", b=2, o=128)[:, b, :, 0:w],
                scr["wutS"][:, hcg * 128:hcg * 128 + w]
                .rearrange("(c p) o -> p c o", p=128)),
                waits=[("sT", wfree[b])] + prep1_out["wu_done"] +
                      (w_guard if g == 0 and hcg == 0 else []),
                inc="sLd")
            for k in range(8):
                par = k % 2
                gm = um = None
                for fc in range(FC):
                    i, dc = fc // 4, fc % 4
                    j = int(JMAP[i, k])
                    src = lx if SMAP[i, k] > 0 else lxn
                    gm = ph.add("tensor", lambda e, par=par, b=b, fc=fc,
                                dc=dc, j=j, src=src, w=w:
                                e.matmul(gps[par][0:w, :],
                                         wgb[:, (b * FC + dc * 8 + j) * 128:
                                             (b * FC + dc * 8 + j) * 128 + w],
                                         src[:, fc * 512:fc * 512 + 512],
                                         start=(fc == 0), stop=(fc == FC - 1)),
                                waits=([("sLd", ldg[b]), ("sLd", ldslab),
                                        ("sS", negdone), ("sS", gev[par])]
                                       if fc == 0 else []),
                                inc="sT" if fc == FC - 1 else None)
                for fc in range(FC):
                    i, dc = fc // 4, fc % 4
                    j = int(JMAP[i, k])
                    src = lx if SMAP[i, k] > 0 else lxn
                    um = ph.add("tensor", lambda e, par=par, b=b, fc=fc,
                                dc=dc, j=j, src=src, w=w:
                                e.matmul(ups[par][0:w, :],
                                         wub[:, (b * FC + dc * 8 + j) * 128:
                                             (b * FC + dc * 8 + j) * 128 + w],
                                         src[:, fc * 512:fc * 512 + 512],
                                         start=(fc == 0), stop=(fc == FC - 1)),
                                waits=([("sLd", ldu[b]), ("sV", uev[par])]
                                       if fc == 0 else []),
                                inc="sT" if fc == FC - 1 else None)
                gev[par] = ph.add("scalar", lambda e, par=par, w=w:
                                  e.activation(sgb[0:w, par * 512:par * 512 + 512],
                                               gps[par][0:w, :],
                                               AF.Silu, bias=0.0, scale=scal[0:w, 3:4]),
                                  waits=[("sT", gm), ("sV", uev[par])], inc="sS")
                uev[par] = ph.add("vector", lambda e, par=par, w=w:
                                  e.scalar_tensor_tensor(hb[0:w, par * 512:par * 512 + 512],
                                                         ups[par][0:w, :], scal[0:w, 4:5],
                                                         sgb[0:w, par * 512:par * 512 + 512],
                                                         ALU.mult, ALU.mult),
                                  waits=[("sT", um), ("sS", gev[par]),
                                         ("sSt", hst[par])], inc="sV")
                hst[par] = ph.add("gpsimd", lambda e, par=par, k=k, hcg=hcg,
                                  g=g, w=w:
                                  e.dma_start(scr["hS"][k, hcg, 0:w,
                                                        g * 512:(g + 1) * 512],
                                              hb[0:w, par * 512:par * 512 + 512]),
                                  waits=[("sV", uev[par])], inc="sSt")
                yield
            wfree[b] = S.cnt["sT"]
        slab_free = S.cnt["sT"]
        out["h_done_g%d" % g] = S.cnt["sSt"]
    ph.add("gpsimd", lambda e: e.engine_nop(), waits=[("sSt", S.cnt["sSt"])])


def phase_f2(nc, S, ext, scr, trash, eps_col, f1out, prep2_out, a3out, guard):
    with (
        nc.sbuf_tensor("f2_wd", [128, 11 * 8 * 512], BF16) as wdt,
        nc.sbuf_tensor("f2_ht", [128, 2 * 88 * 128], BF16) as ht,
        nc.sbuf_tensor("f2_x2", [128, 4096], F32) as x2b,
        nc.sbuf_tensor("f2_ffn", [128, 4096], F32) as ffn,
        nc.sbuf_tensor("f2_st", [128, 8], F32) as stats,
        nc.psum_tensor("f2_pp0", [128, 512], F32) as pp0,
        nc.psum_tensor("f2_pn0", [128, 512], F32) as pn0,
        nc.psum_tensor("f2_pp1", [128, 512], F32) as pp1,
        nc.psum_tensor("f2_pn1", [128, 512], F32) as pn1,
        nc.psum_tensor("f2_pp2", [128, 512], F32) as pp2,
        nc.psum_tensor("f2_pn2", [128, 512], F32) as pn2,
        nc.psum_tensor("f2_pp3", [128, 512], F32) as pp3,
        nc.psum_tensor("f2_pn3", [128, 512], F32) as pn3,
    ):
        dpp = [pp0, pp1, pp2, pp3]
        dpn = [pn0, pn1, pn2, pn3]
        S.ph = ph = Ph(S)
        ldw = {}
        for dcw in range(11):
            ldw[dcw] = ph.add("sync", lambda e, dcw=dcw: e.dma_start(
                wdt.rearrange("p (d j o) -> p d j o", d=11, j=8)[:, dcw],
                scr["wdtS"].rearrange("(j d p) o -> p d j o", p=128, d=11)[:, dcw]),
                waits=(prep2_out["wd_done"] + guard) if dcw == 0 else [],
                inc="sLd")
        ldh = {0: 0, 1: 0}
        htfree = {0: 0, 1: 0}
        devP = {0: 0, 1: 0, 2: 0, 3: 0}
        devN = {0: 0, 1: 0, 2: 0, 3: 0}
        out_st = 0
        x2_rd = 0
        ffn_rd_v = 0
        ffn_rd_s = 0
        first_mm = [True]
        for qt in range(NQT):
            hbuf = qt % 2
            ldh[hbuf] = ph.add("sync", lambda e, hbuf=hbuf, qt=qt: e.dma_start(
                ht.rearrange("p (b i d t) -> p b i d t", b=2, i=8, t=128)[:, hbuf],
                scr["hS"][:, :, :, qt * 128:(qt + 1) * 128]
                .rearrange("i d p t -> p i d t")),
                waits=[("sT", htfree[hbuf]),
                       ("sSt", f1out["h_done_g0" if qt < 4 else "h_done_g1"])],
                inc="sLd")
            ldx2 = ph.add("sync", lambda e, qt=qt: e.dma_start(
                x2b[:, :], scr["x2S"][qt * 128:(qt + 1) * 128, :]),
                waits=[("sV", x2_rd), ("sSt", max(out_st, a3out["x2_all"]))],
                inc="sLd")
            for wave in range(2):
                for kk in range(4):
                    k2 = wave * 4 + kk
                    work = []
                    for dc in range(11):
                        for i in range(8):
                            j = int(JMAP[i, k2])
                            work.append((dc, i, j, SMAP[i, k2] > 0))
                    npos = sum(1 for *_, p in work if p)
                    assert 0 < npos < len(work)
                    seen_pos = seen_neg = 0
                    last_pos = last_neg = None
                    for dc, i, j, pos in work:
                        wdc = HCW[dc]
                        if pos:
                            seen_pos += 1
                            first = seen_pos == 1
                            stop = seen_pos == npos
                            tgt = dpp[kk]
                        else:
                            seen_neg += 1
                            first = seen_neg == 1
                            stop = seen_neg == len(work) - npos
                            tgt = dpn[kk]
                        waits = []
                        if first_mm[0]:
                            waits += guard
                            first_mm[0] = False
                        if first:
                            waits += [("sLd", ldh[hbuf]),
                                      ("sS", devP[kk]) if pos else ("sV", devN[kk])]
                        if i == 0 or first:
                            waits += [("sLd", ldw[dc])]
                        mm = ph.add("tensor", lambda e, tgt=tgt, hbuf=hbuf, i=i,
                                    dc=dc, j=j, wdc=wdc, first=first, stop=stop:
                                    e.matmul(tgt[:, :],
                                             ht[0:wdc, (hbuf * 88 + i * 11 + dc) * 128:
                                                (hbuf * 88 + i * 11 + dc) * 128 + 128],
                                             wdt[0:wdc, (dc * 8 + j) * 512:
                                                 (dc * 8 + j + 1) * 512],
                                             start=first, stop=stop),
                                    waits=waits, inc="sT" if stop else None)
                        if pos and seen_pos == npos:
                            last_pos = mm
                        if (not pos) and seen_neg == len(work) - npos:
                            last_neg = mm
                    devP[kk] = ph.add("scalar", lambda e, kk=kk, k2=k2:
                                      e.copy(ffn[:, k2 * 512:(k2 + 1) * 512],
                                             dpp[kk][:, :]),
                                      waits=[("sT", last_pos), ("sV", ffn_rd_v),
                                             ("sS", ffn_rd_s)], inc="sS")
                    devN[kk] = ph.add("vector", lambda e, kk=kk, k2=k2:
                                      e.tensor_tensor(ffn[:, k2 * 512:(k2 + 1) * 512],
                                                      ffn[:, k2 * 512:(k2 + 1) * 512],
                                                      dpn[kk][:, :], ALU.subtract),
                                      waits=[("sT", last_neg), ("sS", devP[kk]),
                                             ("sV", ffn_rd_v)], inc="sV")
            htfree[hbuf] = S.cnt["sT"]
            vdone, s1 = emit_ln(ph, ffn[:, :], ffn[:, :], stats[:, 0:8], trash[:, :],
                                [("sV", devN[0]), ("sV", devN[1]),
                                 ("sV", devN[2]), ("sV", devN[3])],
                                eps_ap=eps_col[:, :])
            ffn_rd_s = s1
            vadd = ph.add("vector", lambda e: e.tensor_add(x2b[:, :], ffn[:, :], x2b[:, :]),
                          waits=[("sLd", ldx2), ("sV", vdone)], inc="sV")
            ffn_rd_v = vadd
            x2_rd = vadd
            out_st = ph.add("gpsimd", lambda e, qt=qt: e.dma_start(
                ext["out"][qt * 128:(qt + 1) * 128, :], x2b[:, :]),
                waits=[("sV", vadd)], inc="sSt")
        ph.add("gpsimd", lambda e: e.engine_nop(), waits=[("sSt", S.cnt["sSt"])])
        S.emit()


_NC_CACHE = {}


def kernel(L0_attn_x, L0_attn_Wqkv, L0_attn_bqkv, L0_attn_Wo, L0_attn_bo,
           L0_ffn_w_gate, L0_ffn_w_up, L0_ffn_w_down):
    x = np.ascontiguousarray(np.asarray(L0_attn_x, dtype=np.float32))
    wqkv = np.ascontiguousarray(np.asarray(L0_attn_Wqkv, dtype=np.float32))
    bqkv = np.ascontiguousarray(np.asarray(L0_attn_bqkv, dtype=np.float32))
    wo = np.ascontiguousarray(np.asarray(L0_attn_Wo, dtype=np.float32))
    bo = np.ascontiguousarray(np.asarray(L0_attn_bo, dtype=np.float32))
    wg = np.ascontiguousarray(np.asarray(L0_ffn_w_gate, dtype=np.float32))
    wu = np.ascontiguousarray(np.asarray(L0_ffn_w_up, dtype=np.float32))
    wd = np.ascontiguousarray(np.asarray(L0_ffn_w_down, dtype=np.float32))
    eye = np.eye(128, dtype=np.float32)

    if "nc" not in _NC_CACHE:
        _NC_CACHE["nc"] = build_nc()
    nc = _NC_CACHE["nc"]

    in_maps = []
    for c in range(8):
        b = c // 4
        q0 = (c % 4) * TOK
        in_maps.append({
            "xseq": np.ascontiguousarray(np.roll(x[b], -q0, axis=0)),
            "xown": np.ascontiguousarray(x[b, q0:q0 + TOK]),
            "wqkv": wqkv, "bqkv": bqkv, "wo": wo, "bo": bo,
            "wg": wg, "wu": wu, "wd": wd, "eye": eye,
        })
    res = run_bass_kernel_spmd(nc, in_maps, core_ids=list(range(8)))
    out = np.empty((2, SEQ, C), dtype=np.float32)
    for c in range(8):
        b = c // 4
        q0 = (c % 4) * TOK
        out[b, q0:q0 + TOK] = res.results[c]["out"]
    return out


if __name__ == "__main__":
    import reference
    ins = reference.setup_inputs(0)
    got = kernel(**{k: np.asarray(v) for k, v in ins.items()})
    ref = np.asarray(reference.reference(**ins))
    num = float(np.mean((got - ref) ** 2))
    den = float(np.mean(ref ** 2))
    print("residual variance ratio:", num / den)
